# version 19
# baseline (speedup 1.0000x reference)
import numpy as np
from contextlib import ExitStack

import concourse.bass as bass
import concourse.mybir as mybir
from concourse.bass_utils import run_bass_kernel_spmd

F32 = mybir.dt.float32
BF16 = mybir.dt.bfloat16
AF = mybir.ActivationFunctionType
ALU = mybir.AluOpType
AX = mybir.AxisListType

D = 2048
NH = 16
HD = 128
INTER = 8192
WINDOW = 1024
KD = D // 128
KI = INTER // 128
TT = 512
NEG = -1.0e30
LN_EPS = 1e-5
RMS_EPS = 1e-6
ROPE_THETA = 10000.0
N_CORES = 8


class Emitter:
    ENGS = ("pe", "act", "dve", "pool", "sp")
    NDMA = 8

    def __init__(self, nc, ctx):
        self.nc = nc
        self.ops = {e: [] for e in self.ENGS}
        self.sems = {}
        names = ["s_pe", "s_act", "s_dve", "s_pool", "s_sp"]
        names += [f"dh{i}" for i in range(self.NDMA)]
        names += [f"dg{i}" for i in range(self.NDMA)]
        names += ["ccs"]
        for name in names:
            self.sems[name] = ctx.enter_context(nc.semaphore(name))
        self.count = {name: 0 for name in self.sems}
        self.clock = {e: {name: 0 for name in self.sems} for e in self.ENGS}
        self.res = {}
        self.eng_sem = {"pe": "s_pe", "act": "s_act", "dve": "s_dve",
                        "pool": "s_pool", "sp": "s_sp"}
        self.dma_idx = {"sp": 0, "pool": 0}

    def _entry(self, key):
        e = self.res.get(key)
        if e is None:
            e = {"w": None, "r": []}
            self.res[key] = e
        return e

    def _emit(self, eng, builder, reads, writes, done_dep, extra_waits=()):
        waits = {}

        def need(dep):
            if dep is not None and waits.get(dep[0], 0) < dep[1]:
                waits[dep[0]] = dep[1]

        for dep in extra_waits:
            need(dep)
        for k in reads:
            need(self._entry(k)["w"])
        for k in writes:
            e = self._entry(k)
            need(e["w"])
            for rd in e["r"]:
                need(rd)
        ck = self.clock[eng]
        emitted = []
        for sem, val in waits.items():
            if eng == "pe" and sem == "s_pe":
                continue
            if ck[sem] < val:
                emitted.append((self.sems[sem], val))
                ck[sem] = val
        self.ops[eng].append((emitted, builder))
        for k in reads:
            self._entry(k)["r"].append(done_dep)
        for k in writes:
            e = self._entry(k)
            e["w"] = done_dep
            e["r"] = []

    def op(self, eng, builder, reads=(), writes=()):
        sem = self.eng_sem[eng]
        self.count[sem] += 1
        s, v = self.sems[sem], self.count[sem]

        def b(e, _b=builder, _s=s):
            _b(e).then_inc(_s, 1)

        self._emit(eng, b, reads, writes, (sem, self.count[sem]))

    def dma(self, queue, out, in_, reads=(), writes=()):
        pref = "dh" if queue == "sp" else "dg"
        dsem = f"{pref}{self.dma_idx[queue] % self.NDMA}"
        self.dma_idx[queue] += 1
        prev = self.count[dsem]
        self.count[dsem] = prev + 16
        s = self.sems[dsem]

        def b(e, _o=out, _i=in_, _s=s):
            e.dma_start(out=_o, in_=_i).then_inc(_s, 16)

        self._emit(queue, b, reads, writes, (dsem, prev + 16),
                   extra_waits=((dsem, prev),) if prev else ())

    def dma_custom(self, queue, builder_fn, reads=(), writes=()):
        pref = "dh" if queue == "sp" else "dg"
        dsem = f"{pref}{self.dma_idx[queue] % self.NDMA}"
        self.dma_idx[queue] += 1
        prev = self.count[dsem]
        self.count[dsem] = prev + 16
        s = self.sems[dsem]

        def b(e, _s=s):
            builder_fn(e).then_inc(_s, 16)

        self._emit(queue, b, reads, writes, (dsem, prev + 16),
                   extra_waits=((dsem, prev),) if prev else ())

    def collective(self, nc_fn, reads=(), writes=()):
        dsem = "ccs"
        if dsem not in self.sems:
            raise RuntimeError("ccs semaphore not allocated")
        self.count[dsem] += 1
        s, v = self.sems[dsem], self.count[dsem]

        def b(e, _s=s):
            nc_fn(e).then_inc(_s)

        self._emit("pool", b, reads, writes, (dsem, v))

    def barrier(self):
        snap = dict(self.count)
        for eng in self.ENGS:
            ck = self.clock[eng]
            emitted = []
            for sem, val in snap.items():
                if sem != self.eng_sem[eng] and ck[sem] < val:
                    emitted.append((self.sems[sem], val))
                    ck[sem] = val
            if emitted:
                self.ops[eng].append((emitted, None))
        self.res = {}

    def finalize(self, block):
        def mk(eng_name):
            ops = self.ops[eng_name]

            def body(e):
                for waits, builder in ops:
                    for s, v in waits:
                        e.wait_ge(s, v)
                    if builder is not None:
                        builder(e)
            return body
        block.tensor(mk("pe"))
        block.scalar(mk("act"))
        block.vector(mk("dve"))
        block.gpsimd(mk("pool"))
        block.sync(mk("sp"))


def build_program(t_own, ways=N_CORES):
    t_all = t_own + WINDOW
    n_tiles = t_all // TT
    n_own_tiles = t_own // TT
    own0 = WINDOW

    nc = bass.Bass("TRN2", target_bir_lowering=False)

    def din(name, shape, dt=F32):
        return nc.dram_tensor(name, shape, dt, kind="ExternalInput").ap()

    x_in = din("x_in", [D, t_own])
    wup_in = din("wup_in", [KD, 128, INTER // ways])
    wgt_in = din("wgt_in", [KD, 128, INTER // ways])
    wqkv_in = din("wqkv_in", [3 * KD, 128, KD * 128])
    wo_in = din("wo_in", [KD, 128, KD * 128])
    wdn_in = din("wdn_in", [KI, 128, D // ways])
    su_ug = din("su_ug", [128, KD])
    su_dp = din("su_dp", [128, KI])
    wsc_u = din("wsc_u", [128, KI])
    wsc_g = din("wsc_g", [128, KI])
    wsc_dp = din("wsc_dp", [128, KD])
    qnw = din("qnw", [128, KD])
    knw = din("knw", [128, KD])
    cosq = din("cosq", [128, t_own])
    sinq = din("sinq", [128, t_own])
    cosk = din("cosk", [128, t_own])
    sink = din("sink", [128, t_own])
    prevrank = din("prevrank", [1, 1], mybir.dt.int32)
    hbias = din("hbias", [1, t_all])
    h128 = din("h128", [128, 128])
    ident = din("ident", [128, 128])
    pswap = din("pswap", [128, 128])
    mask_c = din("mask_c", [128, 128])
    mask_a = din("mask_a", [128, 128])

    o_out = nc.dram_tensor("o_out", [D, t_own], F32, kind="ExternalOutput").ap()

    def dscr(name, shape, dt):
        return nc.dram_tensor(name, shape, dt, kind="Internal").ap()

    scr_up = dscr("scr_up", [KI, 128, KD * 128], BF16)
    scr_gt = dscr("scr_gt", [KI, 128, KD * 128], BF16)
    if ways > 1:
        sup_my = dscr("sup_my", [KI // ways, 128, KD * 128], BF16)
        sgt_my = dscr("sgt_my", [KI // ways, 128, KD * 128], BF16)
        sdn_my = dscr("sdn_my", [KD // ways, 128, KI * 128], BF16)
    scr_qkv = dscr("scr_qkv", [3 * KD, 128, KD * 128], BF16)
    scr_dn = dscr("scr_dn", [KD, 128, KI * 128], BF16)
    x1_scr = dscr("x1_scr", [D, t_own], F32)
    ktail = dscr("ktail", [NH, 128, WINDOW], BF16)
    vtail = dscr("vtail", [NH, 128, WINDOW // 128, 128], BF16)
    kgat_h = nc.dram_tensor("kgat", [ways, NH, 128, WINDOW], BF16,
                            kind="Internal")
    vgat_h = nc.dram_tensor("vgat", [ways, NH, 128, WINDOW // 128, 128],
                            BF16, kind="Internal")
    kgat = kgat_h.ap()
    vgat = vgat_h.ap()
    x2_scr = dscr("x2_scr", [D, t_own], F32)
    q_scr = dscr("q_scr", [128, NH, t_own], BF16)
    k_scr = dscr("k_scr", [NH, 128, t_all], BF16)
    v_scr = dscr("v_scr", [NH, 128, t_all // 128, 128], BF16)

    with ExitStack() as top:
        em = Emitter(nc, top)
        tsb = lambda n, s, d: top.enter_context(nc.sbuf_tensor(n, s, d))

        h128b = tsb("h128b", [128, 128], BF16)
        identb = tsb("identb", [128, 128], BF16)
        pswapb = tsb("pswapb", [128, 128], BF16)
        maskcb = tsb("maskcb", [128, 128], BF16)
        maskab = tsb("maskab", [128, 128], BF16)
        hbiasb = tsb("hbiasb", [1, t_all], BF16)
        onesb = tsb("onesb", [128, 1], BF16)
        onesrow = tsb("onesrow", [1, 128], F32)
        onesrb = tsb("onesrb", [1, 128], BF16)
        su_ug_s = tsb("su_ug_s", [128, KD], F32)
        su_dp_s = tsb("su_dp_s", [128, KI], F32)
        wsc_u_s = tsb("wsc_u_s", [128, KI], F32)
        wsc_g_s = tsb("wsc_g_s", [128, KI], F32)
        wsc_dp_s = tsb("wsc_dp_s", [128, KD], F32)
        qnw_s = tsb("qnw_s", [128, KD], F32)
        knw_s = tsb("knw_s", [128, KD], F32)

        for buf, src, key in ((h128b, h128, "h128b"), (identb, ident, "identb"),
                              (pswapb, pswap, "pswapb"), (maskcb, mask_c, "maskcb"),
                              (maskab, mask_a, "maskab"), (hbiasb, hbias, "hbiasb")):
            em.dma("pool", buf[:, :], src[:, :], writes=[key])
        for buf, src, key in ((su_ug_s, su_ug, "su_ug_s"), (su_dp_s, su_dp, "su_dp_s"),
                              (wsc_u_s, wsc_u, "wsc_u_s"), (wsc_g_s, wsc_g, "wsc_g_s"),
                              (wsc_dp_s, wsc_dp, "wsc_dp_s"), (qnw_s, qnw, "qnw_s"),
                              (knw_s, knw, "knw_s")):
            em.dma("sp", buf[:, :], src[:, :], writes=[key])
        em.op("dve", lambda e: e.memset(onesb[:, :], 1.0), writes=["onesb"])
        zerot = tsb("zerot", [128, WINDOW], BF16)
        em.op("dve", lambda e: e.memset(zerot[:, :], 0.0), writes=["zerot"])
        for h_ in range(NH):
            em.dma("sp", k_scr[h_, :, 0:WINDOW], zerot[:, :],
                   reads=["zerot"], writes=["k_scr"])
            em.dma("sp",
                   v_scr[h_, :, 0:WINDOW // 128, :].rearrange(
                       "b c d -> b (c d)"),
                   zerot[:, :], reads=["zerot"], writes=["v_scr"])
        em.op("dve", lambda e: e.memset(onesrow[:, :], 1.0), writes=["onesrow"])
        em.op("dve", lambda e: e.memset(onesrb[:, :], 1.0), writes=["onesrb"])
        em.op("dve", lambda e: e.tensor_scalar_mul(
            out=su_ug_s[:, :], in0=su_ug_s[:, :], scalar1=float(1.0 / np.sqrt(D))),
            reads=["su_ug_s"], writes=["su_ug_s"])
        em.op("dve", lambda e: e.tensor_scalar_mul(
            out=su_dp_s[:, :], in0=su_dp_s[:, :], scalar1=float(1.0 / np.sqrt(INTER))),
            reads=["su_dp_s"], writes=["su_dp_s"])
        em.op("dve", lambda e: e.tensor_scalar_mul(
            out=wsc_dp_s[:, :], in0=wsc_dp_s[:, :], scalar1=0.5),
            reads=["wsc_dp_s"], writes=["wsc_dp_s"])
        comb_us = tsb("comb_us", [128, KI], F32)
        em.op("dve", lambda e: e.tensor_mul(
            out=comb_us[:, :], in0=wsc_u_s[:, :], in1=su_dp_s[:, :]),
            reads=["wsc_u_s", "su_dp_s"], writes=["comb_us"])

        with ExitStack() as ph:
            psb = lambda n, s, d: ph.enter_context(nc.sbuf_tensor(n, s, d))
            cb = psb("p0cb", [128, 64 * 256], BF16)
            pbuf = psb("p0pb", [128, 64 * 256], BF16)
            wst = [psb(f"p0wst{i}", [128, KI * 128], BF16) for i in range(4)]
            ps0 = [ph.enter_context(nc.psum_tensor(f"p0ps{i}", [128, 512], F32))
                   for i in range(2)]

            def transform(src_ap, nt, span, scale_ap, scale_key, scr, scr_pref,
                          m_total):
                nmc = span // 128
                for s in range(m_total // span):
                    for i in range(nt):
                        em.dma("pool", cb[:, i * span:(i + 1) * span],
                               src_ap[i, :, s * span:(s + 1) * span],
                               writes=[f"p0cb{i}"])
                    bufs = {"p0cb": cb, "p0pb": pbuf}
                    srck, dstk = "p0cb", "p0pb"
                    h = 1
                    while h < nt:
                        for g in range(0, nt, 2 * h):
                            for k in range(h):
                                a, b = g + k, g + k + h
                                sa = bufs[srck][:, a * span:(a + 1) * span]
                                sbb = bufs[srck][:, b * span:(b + 1) * span]
                                oa = bufs[dstk][:, a * span:(a + 1) * span]
                                ob = bufs[dstk][:, b * span:(b + 1) * span]
                                em.op("dve", lambda e, oa=oa, sa=sa, sbb=sbb:
                                      e.tensor_add(out=oa, in0=sa, in1=sbb),
                                      reads=[f"{srck}{a}", f"{srck}{b}"],
                                      writes=[f"{dstk}{a}"])
                                em.op("dve", lambda e, ob=ob, sa=sa, sbb=sbb:
                                      e.tensor_sub(out=ob, in0=sa, in1=sbb),
                                      reads=[f"{srck}{a}", f"{srck}{b}"],
                                      writes=[f"{dstk}{b}"])
                        srck, dstk = dstk, srck
                        h *= 2
                    for i in range(nt):
                        ps = ps0[i % 2]
                        pk = f"p0ps{i % 2}"
                        em.op("pe", lambda e, ps=ps, span=span,
                              r=bufs[srck][:, i * span:(i + 1) * span]:
                              e.matmul(ps[:, 0:span], lhsT=h128b[:, :], rhs=r,
                                       start=True, stop=True),
                              reads=[f"{srck}{i}", "h128b"], writes=[pk])
                        for c in range(nmc):
                            dst = wst[c][:, i * 128:(i + 1) * 128]
                            if scale_ap is None:
                                em.op("act", lambda e, ps=ps, dst=dst, c=c:
                                      e.copy(out=dst, in_=ps[:, c * 128:
                                                            (c + 1) * 128]),
                                      reads=[pk], writes=[f"p0w{c}_{i}"])
                            else:
                                em.op("dve", lambda e, ps=ps, dst=dst, c=c, i=i:
                                      e.tensor_scalar_mul(
                                          out=dst, in0=ps[:, c * 128:(c + 1) * 128],
                                          scalar1=scale_ap[:, i:i + 1]),
                                      reads=[pk, scale_key],
                                      writes=[f"p0w{c}_{i}"])
                    for c in range(nmc):
                        mc = s * nmc + c
                        em.dma("pool", scr[mc, :, 0:nt * 128],
                               wst[c][:, 0:nt * 128],
                               reads=[f"p0w{c}_{i}" for i in range(nt)],
                               writes=[f"{scr_pref}{mc}"])

            if ways == 1:
                transform(wup_in, KD, 512, su_ug_s, "su_ug_s", scr_up, "scrup",
                          INTER)
                transform(wgt_in, KD, 512, su_ug_s, "su_ug_s", scr_gt, "scrgt",
                          INTER)
                transform(wdn_in, KI, 256, None, None, scr_dn, "scrdn", D)
            else:
                rg = [list(range(ways))]
                transform(wup_in, KD, 512, su_ug_s, "su_ug_s", sup_my, "supmy",
                          INTER // ways)
                transform(wgt_in, KD, 512, su_ug_s, "su_ug_s", sgt_my, "sgtmy",
                          INTER // ways)
                transform(wdn_in, KI, 256, None, None, sdn_my, "sdnmy", D // ways)
                nmu = KI // ways
                nmd = KD // ways
                em.collective(
                    lambda e: e.collective_compute(
                        "AllGather", ALU.bypass, replica_groups=rg,
                        ins=[sup_my.rearrange("a b c -> (a b) c").opt()],
                        outs=[scr_up.rearrange("a b c -> (a b) c").opt()]),
                    reads=[f"supmy{i}" for i in range(nmu)],
                    writes=[f"scrup{m}" for m in range(KI)])
                em.collective(
                    lambda e: e.collective_compute(
                        "AllGather", ALU.bypass, replica_groups=rg,
                        ins=[sgt_my.rearrange("a b c -> (a b) c").opt()],
                        outs=[scr_gt.rearrange("a b c -> (a b) c").opt()]),
                    reads=[f"sgtmy{i}" for i in range(nmu)],
                    writes=[f"scrgt{m}" for m in range(KI)])
                em.collective(
                    lambda e: e.collective_compute(
                        "AllGather", ALU.bypass, replica_groups=rg,
                        ins=[sdn_my.rearrange("a b c -> (a b) c").opt()],
                        outs=[scr_dn.rearrange("a b c -> (a b) c").opt()]),
                    reads=[f"sdnmy{i}" for i in range(nmd)],
                    writes=[f"scrdn{m}" for m in range(KD)])
            em.barrier()

        def ffn_pass(tag, src_ap, src_key, dst_ap, dst_key, tiles):
            with ExitStack() as ph:
                psb = lambda n, s, d: ph.enter_context(nc.sbuf_tensor(tag + n, s, d))
                xbf = psb("xbf", [128, KD * TT], BF16)
                sqs = psb("sqs", [128, 2 * TT], BF16)
                xcs = psb("xcs", [128, 2 * TT], F32)
                rows = psb("rows", [1, 3 * TT], F32)
                m_b = psb("m_b", [128, TT], F32)
                r_b = psb("r_b", [128, TT], F32)
                xf = psb("xf", [128, KD * TT], BF16)
                yb = psb("yb", [128, KI * TT], BF16)
                gst = psb("gst", [128, 2 * TT], F32)
                wub = psb("wub", [128, 4 * KD * 128], BF16)
                wgb = psb("wgb", [128, 4 * KD * 128], BF16)
                wdb = psb("wdb", [128, 2 * KI * 128], BF16)
                xrs = psb("xrs", [128, 2 * TT], F32)
                x1s = psb("x1s", [128, 2 * TT], F32)
                ps_s1 = ph.enter_context(nc.psum_tensor(tag + "ps_s1", [1, TT], F32))
                ps_s2 = ph.enter_context(nc.psum_tensor(tag + "ps_s2", [1, TT], F32))
                ps_bc = ph.enter_context(nc.psum_tensor(tag + "ps_bc", [128, TT], F32))
                ps_u = [ph.enter_context(
                    nc.psum_tensor(tag + f"ps_u{i}", [128, TT], F32)) for i in range(2)]
                ps_g = [ph.enter_context(
                    nc.psum_tensor(tag + f"ps_g{i}", [128, TT], F32)) for i in range(2)]

                def prologue_pieces(t):
                    c0 = t * TT
                    loads, pieces = [], []
                    for i0 in range(0, KD, 8):
                        def pld(i0=i0, c0=c0):
                            em.dma("pool",
                                   xbf[:, i0 * TT:(i0 + 8) * TT].rearrange(
                                       "p (blk c) -> p blk c", blk=8),
                                   src_ap[i0 * 128:(i0 + 8) * 128, c0:c0 + TT]
                                   .rearrange("(blk p) c -> p blk c", p=128),
                                   reads=[src_key],
                                   writes=[f"xbf{i}" for i in range(i0, i0 + 8)])
                        loads.append(pld)
                    for i in range(KD):
                        def pst(i=i):
                            sl = i % 2
                            sq = sqs[:, sl * TT:(sl + 1) * TT]
                            em.op("act", lambda e, sq=sq, i=i: e.activation(
                                out=sq, in_=xbf[:, i * TT:(i + 1) * TT],
                                func=AF.Square),
                                reads=[f"xbf{i}"], writes=[f"sqs{sl}"])
                            em.op("pe", lambda e, i=i: e.matmul(
                                ps_s1[:, :], lhsT=onesb[:, :],
                                rhs=xbf[:, i * TT:(i + 1) * TT],
                                start=(i == 0), stop=(i == KD - 1),
                                skip_group_check=True),
                                reads=[f"xbf{i}", "onesb"], writes=["ps_s1"])
                            em.op("pe", lambda e, sq=sq, i=i: e.matmul(
                                ps_s2[:, :], lhsT=onesb[:, :], rhs=sq,
                                start=(i == 0), stop=(i == KD - 1),
                                skip_group_check=True),
                                reads=[f"sqs{sl}", "onesb"], writes=["ps_s2"])
                        pieces.append(pst)

                    def prows():
                        mr, vr = rows[:, 0:TT], rows[:, TT:2 * TT]
                        tr = rows[:, 2 * TT:3 * TT]
                        rr = mr
                        em.op("dve", lambda e: e.tensor_scalar_mul(
                            out=mr, in0=ps_s1[:, :], scalar1=1.0 / D),
                            reads=["ps_s1"], writes=["row_m"])
                        em.op("dve", lambda e: e.tensor_scalar_mul(
                            out=vr, in0=ps_s2[:, :], scalar1=1.0 / D),
                            reads=["ps_s2"], writes=["row_v"])
                        em.op("dve", lambda e: e.tensor_mul(out=tr, in0=mr, in1=mr),
                              reads=["row_m"], writes=["row_t"])
                        em.op("dve", lambda e: e.tensor_sub(out=vr, in0=vr, in1=tr),
                              reads=["row_v", "row_t"], writes=["row_v"])
                        em.op("dve", lambda e: e.tensor_scalar_add(
                            out=vr, in0=vr, scalar1=LN_EPS),
                            reads=["row_v"], writes=["row_v"])
                        em.op("dve", lambda e: e.reciprocal(out=tr, in_=vr),
                              reads=["row_v"], writes=["row_t"])
                        em.op("pe", lambda e: e.matmul(
                            ps_bc[:, :], lhsT=onesrow[:, :], rhs=mr,
                            start=True, stop=True),
                            reads=["row_m", "onesrow"], writes=["ps_bc"])
                        em.op("dve", lambda e: e.tensor_copy(
                            out=m_b[:, :], in_=ps_bc[:, :]),
                            reads=["ps_bc"], writes=["m_b"])
                        em.op("act", lambda e: e.activation(
                            out=rr, in_=tr, func=AF.Sqrt),
                            reads=["row_t"], writes=["row_m"])
                        em.op("pe", lambda e: e.matmul(
                            ps_bc[:, :], lhsT=onesrow[:, :], rhs=rr,
                            start=True, stop=True),
                            reads=["row_m", "onesrow"], writes=["ps_bc"])
                        em.op("dve", lambda e: e.tensor_copy(
                            out=r_b[:, :], in_=ps_bc[:, :]),
                            reads=["ps_bc"], writes=["r_b"])
                    pieces.append(prows)
                    for i in range(KD):
                        def papp(i=i):
                            sl = i % 2
                            xc = xcs[:, sl * TT:(sl + 1) * TT]
                            em.op("dve", lambda e, xc=xc, i=i: e.tensor_sub(
                                out=xc, in0=xbf[:, i * TT:(i + 1) * TT],
                                in1=m_b[:, :]),
                                reads=[f"xbf{i}", "m_b"], writes=[f"xcs{sl}"])
                            em.op("dve", lambda e, xc=xc, i=i: e.tensor_mul(
                                out=xf[:, i * TT:(i + 1) * TT], in0=xc,
                                in1=r_b[:, :]),
                                reads=[f"xcs{sl}", "r_b"], writes=[f"xf{i}"])
                        pieces.append(papp)
                    return loads, pieces

                def emit_upgate(t):
                    for m in range(KI):
                        sl = m % 4
                        wu = wub[:, sl * KD * 128:(sl + 1) * KD * 128]
                        wg = wgb[:, sl * KD * 128:(sl + 1) * KD * 128]
                        wuk, wgk = f"wub{sl}", f"wgb{sl}"
                        em.dma("sp", wu, scr_up[m, :, :], reads=[f"scrup{m}"],
                               writes=[wuk])
                        em.dma("sp", wg, scr_gt[m, :, :], reads=[f"scrgt{m}"],
                               writes=[wgk])
                        psu, psg = ps_u[m % 2], ps_g[m % 2]
                        psuk, psgk = f"ps_u{m % 2}", f"ps_g{m % 2}"
                        for k in range(KD):
                            em.op("pe", lambda e, psu=psu, wu=wu, k=k: e.matmul(
                                psu[:, :], lhsT=wu[:, k * 128:(k + 1) * 128],
                                rhs=xf[:, k * TT:(k + 1) * TT],
                                start=(k == 0), stop=(k == KD - 1)),
                                reads=[wuk, f"xf{k}"], writes=[psuk])
                        for k in range(KD):
                            em.op("pe", lambda e, psg=psg, wg=wg, k=k: e.matmul(
                                psg[:, :], lhsT=wg[:, k * 128:(k + 1) * 128],
                                rhs=xf[:, k * TT:(k + 1) * TT],
                                start=(k == 0), stop=(k == KD - 1)),
                                reads=[wgk, f"xf{k}"], writes=[psgk])
                        gs = gst[:, (m % 2) * TT:(m % 2 + 1) * TT]
                        em.op("act", lambda e, psg=psg, gs=gs, m=m: e.activation(
                            out=gs, in_=psg[:, :], func=AF.Silu,
                            scale=wsc_g_s[:, m:m + 1]),
                            reads=[psgk, "wsc_g_s"], writes=[f"gst{m % 2}"])
                        em.op("dve", lambda e, psu=psu, gs=gs, m=m:
                              e.scalar_tensor_tensor(
                                  out=yb[:, m * TT:(m + 1) * TT], in0=psu[:, :],
                                  scalar=comb_us[:, m:m + 1], in1=gs,
                                  op0=ALU.mult, op1=ALU.mult),
                              reads=[psuk, f"gst{m % 2}", "comb_us"],
                              writes=[f"yb{m}"])

                def down_pieces(t):
                    c0 = t * TT
                    pieces = []
                    for m in range(KD):
                        def pdn(m=m, c0=c0):
                            sl = m % 2
                            wd = wdb[:, sl * KI * 128:(sl + 1) * KI * 128]
                            wdk = f"wdb{sl}"
                            em.dma("sp", wd, scr_dn[m, :, :], reads=[f"scrdn{m}"],
                                   writes=[wdk])
                            xr = xrs[:, sl * TT:(sl + 1) * TT]
                            em.dma("sp", xr,
                                   src_ap[m * 128:(m + 1) * 128, c0:c0 + TT],
                                   reads=[src_key], writes=[f"xrs{sl}"])
                            psd = ps_u[sl]
                            psdk = f"ps_u{sl}"
                            for k in range(KI):
                                em.op("pe", lambda e, psd=psd, wd=wd, k=k: e.matmul(
                                    psd[:, :], lhsT=wd[:, k * 128:(k + 1) * 128],
                                    rhs=yb[:, k * TT:(k + 1) * TT],
                                    start=(k == 0), stop=(k == KI - 1)),
                                    reads=[wdk, f"yb{k}"], writes=[psdk])
                            x1c = x1s[:, sl * TT:(sl + 1) * TT]
                            em.op("dve", lambda e, psd=psd, x1c=x1c, xr=xr, m=m:
                                  e.scalar_tensor_tensor(
                                      out=x1c, in0=psd[:, :],
                                      scalar=wsc_dp_s[:, m:m + 1], in1=xr,
                                      op0=ALU.mult, op1=ALU.add),
                                  reads=[psdk, f"xrs{sl}", "wsc_dp_s"],
                                  writes=[f"x1s{sl}"])
                            em.dma("pool",
                                   dst_ap[m * 128:(m + 1) * 128, c0:c0 + TT], x1c,
                                   reads=[f"x1s{sl}"], writes=[dst_key])
                        pieces.append(pdn)
                    return pieces

                prev_down = None
                for t in range(tiles):
                    loads, pro = prologue_pieces(t)
                    for p in loads:
                        p()
                    if prev_down is None:
                        for p in pro:
                            p()
                    else:
                        nd = len(prev_down)
                        npr = len(pro)
                        j = 0
                        for i, pd in enumerate(prev_down):
                            pd()
                            jn = (i + 1) * npr // nd
                            while j < jn:
                                pro[j]()
                                j += 1
                    emit_upgate(t)
                    prev_down = down_pieces(t)
                for p in prev_down:
                    p()
                em.barrier()

        ffn_pass("a1", x_in, "x_in", x1_scr, "x1_scr", n_own_tiles)

        with ExitStack() as ph:
            psb = lambda n, s, d: ph.enter_context(nc.sbuf_tensor("a2" + n, s, d))
            x1bf = psb("x1bf", [128, 2 * KD * TT], BF16)
            sqs = psb("sqs", [128, 2 * TT], BF16)
            xn2 = psb("xn2", [128, 2 * KD * TT], BF16)
            qf = psb("qf", [128, KD * TT], BF16)
            kf = psb("kf", [128, KD * TT], BF16)
            cq2 = psb("cq2", [128, TT], F32)
            sq2 = psb("sq2", [128, TT], F32)
            ck2 = psb("ck2", [128, TT], F32)
            sk2 = psb("sk2", [128, TT], F32)
            rows = psb("rows", [1, 6 * TT], F32)
            m_b = psb("m_b", [128, TT], F32)
            r_b = psb("r_b", [128, TT], F32)
            qr_b = psb("qr_b", [128, TT], F32)
            kr_b = psb("kr_b", [128, TT], F32)
            wqb = psb("wqb", [128, 4 * KD * 128], BF16)
            cqt = psb("cqt", [128, 2 * TT], F32)
            sqt = psb("sqt", [128, 2 * TT], F32)
            ckt = psb("ckt", [128, 2 * TT], F32)
            skt = psb("skt", [128, 2 * TT], F32)
            uu = psb("uu", [128, TT], F32)
            ub = psb("ub", [128, TT], BF16)
            swf = psb("swf", [128, TT], F32)
            aa = psb("aa", [128, TT], F32)
            bb2 = psb("bb2", [128, TT], F32)
            qkb = psb("qkb", [128, 2 * TT], BF16)
            vbs = psb("vbs", [128, TT], BF16)
            vts = psb("vts", [128, 2 * 4 * 128], BF16)
            ps_qk = [ph.enter_context(nc.psum_tensor(f"a2ps{i}", [128, TT], F32))
                     for i in range(2)]
            ps_sq = ph.enter_context(nc.psum_tensor("a2psq", [1, TT], F32))
            ps_sk = ph.enter_context(nc.psum_tensor("a2psk", [1, TT], F32))
            ps_bc = ph.enter_context(nc.psum_tensor("a2psbc", [128, TT], F32))
            ps_sw = ph.enter_context(nc.psum_tensor("a2pssw", [128, TT], F32))
            ps_tr = [ph.enter_context(nc.psum_tensor(f"a2pstr{i}", [128, 128], F32))
                     for i in range(2)]

            def a2_loads(t):
                c0 = t * TT
                xsl = (t % 2) * KD * TT
                tsl = (t % 2) * TT
                for i0 in range(0, KD, 8):
                    em.dma("pool",
                           x1bf[:, xsl + i0 * TT:xsl + (i0 + 8) * TT].rearrange(
                               "p (blk c) -> p blk c", blk=8),
                           x1_scr[i0 * 128:(i0 + 8) * 128, c0:c0 + TT]
                           .rearrange("(blk p) c -> p blk c", p=128),
                           reads=["x1_scr"],
                           writes=[f"x1bf{t % 2}_{i}" for i in range(i0, i0 + 8)])
                em.dma("sp", ckt[:, tsl:tsl + TT], cosk[:, c0:c0 + TT],
                       writes=[f"ckt{t % 2}"])
                em.dma("sp", skt[:, tsl:tsl + TT], sink[:, c0:c0 + TT],
                       writes=[f"skt{t % 2}"])
                em.dma("sp", cqt[:, tsl:tsl + TT], cosq[:, c0:c0 + TT],
                       writes=[f"cqt{t % 2}"])
                em.dma("sp", sqt[:, tsl:tsl + TT], sinq[:, c0:c0 + TT],
                       writes=[f"sqt{t % 2}"])

            def a2_prologue_pieces(t):
                xsl = (t % 2) * KD * TT
                xk = lambda i: f"x1bf{t % 2}_{i}"
                pieces = []
                for i in range(KD):
                    def pst(i=i):
                        sl = i % 2
                        sq = sqs[:, sl * TT:(sl + 1) * TT]
                        xb_ = x1bf[:, xsl + i * TT:xsl + (i + 1) * TT]
                        em.op("act", lambda e, xb_=xb_, sq=sq: e.activation(
                            out=sq, in_=xb_, func=AF.Square),
                            reads=[xk(i)], writes=[f"sqs{sl}"])
                        em.op("pe", lambda e, xb_=xb_, i=i: e.matmul(
                            ps_sq[:, :], lhsT=onesb[:, :], rhs=xb_,
                            start=(i == 0), stop=(i == KD - 1),
                            skip_group_check=True),
                            reads=[xk(i), "onesb"], writes=["ps_sq"])
                        em.op("pe", lambda e, sq=sq, i=i: e.matmul(
                            ps_sk[:, :], lhsT=onesb[:, :], rhs=sq,
                            start=(i == 0), stop=(i == KD - 1),
                            skip_group_check=True),
                            reads=[f"sqs{sl}", "onesb"], writes=["ps_sk"])
                    pieces.append(pst)

                def prows():
                    mr, vr = rows[:, 0:TT], rows[:, TT:2 * TT]
                    tr = rows[:, 2 * TT:3 * TT]
                    rr = mr
                    em.op("dve", lambda e: e.tensor_scalar_mul(
                        out=mr, in0=ps_sq[:, :], scalar1=1.0 / D),
                        reads=["ps_sq"], writes=["row_m"])
                    em.op("dve", lambda e: e.tensor_scalar_mul(
                        out=vr, in0=ps_sk[:, :], scalar1=1.0 / D),
                        reads=["ps_sk"], writes=["row_v"])
                    em.op("dve", lambda e: e.tensor_mul(out=tr, in0=mr, in1=mr),
                          reads=["row_m"], writes=["row_t"])
                    em.op("dve", lambda e: e.tensor_sub(out=vr, in0=vr, in1=tr),
                          reads=["row_v", "row_t"], writes=["row_v"])
                    em.op("dve", lambda e: e.tensor_scalar_add(
                        out=vr, in0=vr, scalar1=LN_EPS),
                        reads=["row_v"], writes=["row_v"])
                    em.op("dve", lambda e: e.reciprocal(out=tr, in_=vr),
                          reads=["row_v"], writes=["row_t"])
                    em.op("pe", lambda e: e.matmul(
                        ps_bc[:, :], lhsT=onesrow[:, :], rhs=mr,
                        start=True, stop=True),
                        reads=["row_m", "onesrow"], writes=["ps_bc"])
                    em.op("dve", lambda e: e.tensor_copy(
                        out=m_b[:, :], in_=ps_bc[:, :]),
                        reads=["ps_bc"], writes=["m_b"])
                    em.op("act", lambda e: e.activation(
                        out=rr, in_=tr, func=AF.Sqrt),
                        reads=["row_t"], writes=["row_m"])
                    em.op("pe", lambda e: e.matmul(
                        ps_bc[:, :], lhsT=onesrow[:, :], rhs=rr,
                        start=True, stop=True),
                        reads=["row_m", "onesrow"], writes=["ps_bc"])
                    em.op("dve", lambda e: e.tensor_copy(
                        out=r_b[:, :], in_=ps_bc[:, :]),
                        reads=["ps_bc"], writes=["r_b"])
                pieces.append(prows)
                for i in range(KD):
                    def papp(i=i):
                        xb_ = x1bf[:, xsl + i * TT:xsl + (i + 1) * TT]
                        em.op("dve", lambda e, xb_=xb_: e.tensor_sub(
                            out=uu[:, :], in0=xb_, in1=m_b[:, :]),
                            reads=[xk(i), "m_b"], writes=["uu"])
                        em.op("dve", lambda e, i=i: e.tensor_mul(
                            out=xn2[:, xsl + i * TT:xsl + (i + 1) * TT],
                            in0=uu[:, :], in1=r_b[:, :]),
                            reads=["uu", "r_b"], writes=[f"xn2{t % 2}_{i}"])
                    pieces.append(papp)
                return pieces

            def qkv_mm(t, m):
                xsl = (t % 2) * KD * TT
                sl = m % 4
                wq = wqb[:, sl * KD * 128:(sl + 1) * KD * 128]
                wqk = f"wqb{sl}"
                if t == 0:
                    em.dma("pool", wq, wqkv_in[m, :, :], writes=[wqk])
                    em.dma("pool", scr_qkv[m, :, :], wq, reads=[wqk],
                           writes=[f"scrqkv{m}"])
                else:
                    em.dma("sp", wq, scr_qkv[m, :, :], reads=[f"scrqkv{m}"],
                           writes=[wqk])
                ps = ps_qk[m % 2]
                psk = f"ps_qk{m % 2}"
                for k in range(KD):
                    em.op("pe", lambda e, ps=ps, wq=wq, k=k, xsl=xsl: e.matmul(
                        ps[:, :], lhsT=wq[:, k * 128:(k + 1) * 128],
                        rhs=xn2[:, xsl + k * TT:xsl + (k + 1) * TT],
                        start=(k == 0), stop=(k == KD - 1)),
                        reads=[wqk, f"xn2{t % 2}_{k}"], writes=[psk])
                return ps, psk, m % 2

            def qk_section(t):
                has_q = True
                for m in range(2 * KD):
                    ps, psk, sl = qkv_mm(t, m)
                    if m < KD:
                        if not has_q:
                            continue
                        em.op("act", lambda e, ps=ps, m=m: e.copy(
                            out=qf[:, m * TT:(m + 1) * TT], in_=ps[:, :]),
                            reads=[psk], writes=[f"qf{m}"])
                        sq = sqs[:, sl * TT:(sl + 1) * TT]
                        em.op("act", lambda e, ps=ps, sq=sq: e.activation(
                            out=sq, in_=ps[:, :], func=AF.Square),
                            reads=[psk], writes=[f"sqs{sl}"])
                        em.op("pe", lambda e, sq=sq, m=m: e.matmul(
                            ps_sq[:, :], lhsT=onesb[:, :], rhs=sq,
                            start=(m == 0), stop=(m == KD - 1),
                            skip_group_check=True),
                            reads=[f"sqs{sl}", "onesb"], writes=["ps_sq"])
                    else:
                        mk = m - KD
                        em.op("act", lambda e, ps=ps, mk=mk: e.copy(
                            out=kf[:, mk * TT:(mk + 1) * TT], in_=ps[:, :]),
                            reads=[psk], writes=[f"kf{mk}"])
                        sq = sqs[:, sl * TT:(sl + 1) * TT]
                        em.op("act", lambda e, ps=ps, sq=sq: e.activation(
                            out=sq, in_=ps[:, :], func=AF.Square),
                            reads=[psk], writes=[f"sqs{sl}"])
                        em.op("pe", lambda e, sq=sq, mk=mk: e.matmul(
                            ps_sk[:, :], lhsT=onesb[:, :], rhs=sq,
                            start=(mk == 0), stop=(mk == KD - 1),
                            skip_group_check=True),
                            reads=[f"sqs{sl}", "onesb"], writes=["ps_sk"])
                qrr, krr = rows[:, 3 * TT:4 * TT], rows[:, 4 * TT:5 * TT]
                t2r = rows[:, 5 * TT:6 * TT]

                def rms_row(out_r, ps_src, ps_key, okey):
                    em.op("dve", lambda e, out_r=out_r, ps_src=ps_src:
                          e.tensor_scalar_mul(out=t2r, in0=ps_src[:, :],
                                              scalar1=1.0 / D),
                          reads=[ps_key], writes=["row_t2"])
                    em.op("dve", lambda e: e.tensor_scalar_add(
                        out=t2r, in0=t2r, scalar1=RMS_EPS),
                        reads=["row_t2"], writes=["row_t2"])
                    em.op("dve", lambda e, out_r=out_r: e.reciprocal(
                        out=out_r, in_=t2r),
                        reads=["row_t2"], writes=[okey])
                    em.op("act", lambda e, out_r=out_r: e.activation(
                        out=out_r, in_=out_r, func=AF.Sqrt),
                        reads=[okey], writes=[okey])

                tsl = (t % 2) * TT
                if has_q:
                    rms_row(qrr, ps_sq, "ps_sq", "row_q")
                    em.op("pe", lambda e: e.matmul(
                        ps_bc[:, :], lhsT=onesrow[:, :], rhs=qrr,
                        start=True, stop=True),
                        reads=["row_q", "onesrow"], writes=["ps_bc"])
                    em.op("dve", lambda e: e.tensor_copy(
                        out=qr_b[:, :], in_=ps_bc[:, :]),
                        reads=["ps_bc"], writes=["qr_b"])
                    em.op("dve", lambda e, tsl=tsl: e.tensor_mul(
                        out=cq2[:, :], in0=cqt[:, tsl:tsl + TT], in1=qr_b[:, :]),
                        reads=[f"cqt{t % 2}", "qr_b"], writes=["cq2"])
                    em.op("dve", lambda e, tsl=tsl: e.tensor_mul(
                        out=sq2[:, :], in0=sqt[:, tsl:tsl + TT], in1=qr_b[:, :]),
                        reads=[f"sqt{t % 2}", "qr_b"], writes=["sq2"])
                rms_row(krr, ps_sk, "ps_sk", "row_k")
                em.op("pe", lambda e: e.matmul(
                    ps_bc[:, :], lhsT=onesrow[:, :], rhs=krr,
                    start=True, stop=True),
                    reads=["row_k", "onesrow"], writes=["ps_bc"])
                em.op("dve", lambda e: e.tensor_copy(
                    out=kr_b[:, :], in_=ps_bc[:, :]),
                    reads=["ps_bc"], writes=["kr_b"])
                em.op("dve", lambda e, tsl=tsl: e.tensor_mul(
                    out=ck2[:, :], in0=ckt[:, tsl:tsl + TT], in1=kr_b[:, :]),
                    reads=[f"ckt{t % 2}", "kr_b"], writes=["ck2"])
                em.op("dve", lambda e, tsl=tsl: e.tensor_mul(
                    out=sk2[:, :], in0=skt[:, tsl:tsl + TT], in1=kr_b[:, :]),
                    reads=[f"skt{t % 2}", "kr_b"], writes=["sk2"])

            def rope_out(t, src_f, src_key, w_ap, w_key,
                         ct, ct_key, st_, st_key, dst, dst_key, sl):
                em.op("dve", lambda e: e.tensor_scalar_mul(
                    out=ub[:, :], in0=src_f, scalar1=w_ap),
                    reads=[src_key, w_key], writes=["ub"])
                em.op("pe", lambda e: e.matmul(
                    ps_sw[:, :], lhsT=pswapb[:, :], rhs=ub[:, :],
                    start=True, stop=True),
                    reads=["ub", "pswapb"], writes=["ps_sw"])
                em.op("act", lambda e: e.copy(out=swf[:, :], in_=ps_sw[:, :]),
                      reads=["ps_sw"], writes=["swf"])
                em.op("dve", lambda e, ct=ct: e.tensor_mul(
                    out=aa[:, :], in0=ub[:, :], in1=ct),
                    reads=["ub", ct_key], writes=["aa"])
                em.op("dve", lambda e, st_=st_: e.tensor_mul(
                    out=bb2[:, :], in0=swf[:, :], in1=st_),
                    reads=["swf", st_key], writes=["bb2"])
                qk = qkb[:, sl * TT:(sl + 1) * TT]
                em.op("dve", lambda e, qk=qk: e.tensor_add(
                    out=qk, in0=aa[:, :], in1=bb2[:, :]),
                    reads=["aa", "bb2"], writes=[f"qkb{sl}"])
                em.dma("pool", dst, qk, reads=[f"qkb{sl}"], writes=[dst_key])

            def a2_tail_pieces(t):
                c0 = t * TT
                has_q = True
                tsl = (t % 2) * TT
                rope_pieces = []
                if has_q:
                    for h in range(NH):
                        def pq(h=h, c0=c0):
                            rope_out(t, qf[:, h * TT:(h + 1) * TT], f"qf{h}",
                                     qnw_s[:, h:h + 1], "qnw_s",
                                     cq2[:, :], "cq2", sq2[:, :], "sq2",
                                     q_scr[:, h, c0:c0 + TT],
                                     "q_scr", h % 2)
                        rope_pieces.append(pq)
                for h in range(NH):
                    def pk_(h=h, c0=c0):
                        rope_out(t, kf[:, h * TT:(h + 1) * TT], f"kf{h}",
                                 knw_s[:, h:h + 1], "knw_s",
                                 ck2[:, :], "ck2", sk2[:, :], "sk2",
                                 k_scr[h, :, WINDOW + c0:WINDOW + c0 + TT],
                                 "k_scr", h % 2)
                    rope_pieces.append(pk_)
                pieces = []
                nrp = len(rope_pieces)
                for mv in range(NH):
                    def pv_(mv=mv, c0=c0, j0=mv * nrp // NH,
                            j1=(mv + 1) * nrp // NH):
                        ps, psk, sl = qkv_mm(t, 2 * KD + mv)
                        em.op("act", lambda e, ps=ps: e.copy(
                            out=vbs[:, :], in_=ps[:, :]),
                            reads=[psk], writes=["vbs"])
                        vsl = (mv % 2) * 4 * 128
                        for s4 in range(TT // 128):
                            ptr = ps_tr[s4 % 2]
                            ptrk = f"ps_tr{s4 % 2}"
                            vt = vts[:, vsl + s4 * 128:vsl + (s4 + 1) * 128]
                            em.op("pe", lambda e, ptr=ptr, s4=s4: e.matmul(
                                ptr[:, :], lhsT=vbs[:, s4 * 128:(s4 + 1) * 128],
                                rhs=identb[:, :], start=True, stop=True),
                                reads=["vbs", "identb"], writes=[ptrk])
                            em.op("act", lambda e, ptr=ptr, vt=vt: e.copy(
                                out=vt, in_=ptr[:, :]),
                                reads=[ptrk], writes=[f"vts{mv % 2}"])
                        em.dma("sp",
                               v_scr[mv, :, WINDOW // 128 + c0 // 128:
                                     WINDOW // 128 + c0 // 128 + 4, :],
                               vts[:, vsl:vsl + 4 * 128].rearrange(
                                   "p (kb c) -> p kb c", kb=4),
                               reads=[f"vts{mv % 2}"], writes=["v_scr"])
                        for j in range(j0, j1):
                            rope_pieces[j]()
                    pieces.append(pv_)
                return pieces

            prev_tail = None
            for t in range(n_own_tiles):
                a2_loads(t)
                pro = a2_prologue_pieces(t)
                if prev_tail is None:
                    for p in pro:
                        p()
                else:
                    nd = len(prev_tail)
                    npr = len(pro)
                    j = 0
                    for i, pd in enumerate(prev_tail):
                        pd()
                        jn = min(npr, (i + 1) * 2 * npr // nd)
                        while j < jn:
                            pro[j]()
                            j += 1
                qk_section(t)
                prev_tail = a2_tail_pieces(t)
            for p in prev_tail:
                p()

            em.barrier()

        with ExitStack() as ph:
            psb = lambda n, s, d: ph.enter_context(nc.sbuf_tensor("b" + n, s, d))
            KW = WINDOW + TT
            NKC = KW // 128
            qsb = psb("qsb", [128, NH * TT], BF16)
            ksb = psb("ksb", [128, 4 * KW], BF16)
            vsb = psb("vsb", [128, 4 * NKC * 128], BF16)
            pf32 = psb("pf32", [128, 2 * 1152], F32)
            pb = psb("pb", [128, 2 * 1152], BF16)
            ptb = psb("ptb", [128, 2 * 1152], BF16)
            nmax = psb("nmax", [128, 2], F32)
            rows_nm = psb("rows_nm", [128, 6], F32)
            ssum = psb("ssum", [128, 2], F32)
            rsum = psb("rsum", [128, 2], F32)
            attn = psb("attn", [128, NH * TT], BF16)
            attn_tm = psb("attn_tm", [128, NH * 4 * 128], BF16)
            wosb = psb("wosb", [128, KD * KD * 128], BF16)
            x1c = psb("x1c", [128, 4 * TT], F32)
            x2c = psb("x2c", [128, 2 * TT], F32)
            ps_scs = [ph.enter_context(nc.psum_tensor(f"bps_sc{i}", [128, 1152],
                                                       F32)) for i in range(2)]
            ps_pv = ph.enter_context(nc.psum_tensor("bps_pv", [128, TT], F32))
            ps_wo = [ph.enter_context(nc.psum_tensor("bps_wo", [128, TT], F32))]

            idxt = ph.enter_context(nc.sbuf_tensor("bidxt", [1, 1],
                                                   mybir.dt.int32))
            em.dma("sp", ktail[:, :, :],
                   k_scr[:, :, t_all - WINDOW:t_all], writes=["ktail"])
            em.dma("sp", vtail[:, :, :, :],
                   v_scr[:, :, (t_all - WINDOW) // 128:t_all // 128, :],
                   writes=["vtail"])
            if ways > 1:
                rg = [list(range(ways))]
                em.collective(
                    lambda e: e.collective_compute(
                        "AllGather", ALU.bypass, replica_groups=rg,
                        ins=[ktail.rearrange("a b c -> (a b) c").opt()],
                        outs=[kgat.rearrange("r a b c -> (r a b) c").opt()]),
                    reads=["ktail"], writes=["kgat"])
                em.collective(
                    lambda e: e.collective_compute(
                        "AllGather", ALU.bypass, replica_groups=rg,
                        ins=[vtail.rearrange("a b c d -> (a b c) d").opt()],
                        outs=[vgat.rearrange("r a b c d -> (r a b c) d").opt()]),
                    reads=["vtail"], writes=["vgat"])
            else:
                em.dma("sp", kgat[0], ktail[:, :, :],
                       reads=["ktail"], writes=["kgat"])
                em.dma("sp", vgat[0], vtail[:, :, :, :],
                       reads=["vtail"], writes=["vgat"])
            em.dma("sp", idxt[:, :], prevrank[:, :], writes=["idxt"])

            kstride = NH * 128 * WINDOW
            vstride = NH * 128 * (WINDOW // 128) * 128

            def dyn_fetch(e, gat_h, stride, dst_ap, ndims_ap):
                with e.register() as reg:
                    e.reg_load(reg, idxt[:1, :1])
                    with e.register() as ofs:
                        e.reg_mul(ofs, reg, stride)
                        src_dyn = bass.AP(gat_h, ofs, ndims_ap)
                        return e.dma_start(out=dst_ap, in_=src_dyn)

            k_ap = [[128 * WINDOW, NH], [WINDOW, 128], [1, WINDOW]]
            v_ap = [[128 * WINDOW, NH], [WINDOW, 128], [1, WINDOW]]

            def bk(e):
                return dyn_fetch(e, kgat_h, kstride,
                                 k_scr[:, :, 0:WINDOW], k_ap)

            def bv(e):
                return dyn_fetch(e, vgat_h, vstride,
                                 v_scr[:, :, 0:WINDOW // 128, :].rearrange(
                                     "a b c d -> a b (c d)"), v_ap)

            em.dma_custom("pool", bk, reads=["kgat", "idxt"], writes=["k_scr_h"])
            em.dma_custom("pool", bv, reads=["vgat", "idxt"], writes=["v_scr_h"])

            for m in range(KD):
                em.dma("pool", wosb[:, m * KD * 128:(m + 1) * KD * 128],
                       wo_in[m, :, :], writes=[f"wosb{m}"])

            qt_order = [qt for qt in range(n_own_tiles) if qt * TT >= WINDOW]
            qt_order += [qt for qt in range(n_own_tiles) if qt * TT < WINDOW]
            for qt in qt_order:
                kw0 = qt * TT
                em.dma("sp", qsb.ap().rearrange("p (h c) -> p h c", h=NH),
                       q_scr[:, :, qt * TT:(qt + 1) * TT],
                       reads=[],
                       writes=[f"qsb{h}" for h in range(NH)])
                prev_tail = None
                for h in range(NH):
                    sl = h % 2
                    if h % 2 == 0:
                        psl2 = (h // 2) % 2
                        em.dma("sp",
                               ksb[:, psl2 * 2 * KW:(psl2 * 2 + 2) * KW]
                               .rearrange("p (hh c) -> p hh c", hh=2),
                               k_scr[h:h + 2, :, kw0:kw0 + KW].rearrange(
                                   "hh p c -> p hh c"),
                               reads=(["k_scr_h"] if kw0 < WINDOW else []),
                               writes=[f"ksb{psl2 * 2}", f"ksb{psl2 * 2 + 1}"])
                        em.dma("sp",
                               vsb[:, psl2 * 2 * NKC * 128:
                                   (psl2 * 2 + 2) * NKC * 128].rearrange(
                                   "p (hh kb c) -> p hh kb c", hh=2, kb=NKC),
                               v_scr[h:h + 2, :, kw0 // 128:kw0 // 128 + NKC,
                                     :].rearrange("hh p kb c -> p hh kb c"),
                               reads=(["v_scr_h"] if kw0 < WINDOW else []),
                               writes=[f"vsb{psl2 * 2}", f"vsb{psl2 * 2 + 1}"])
                    slot = (h // 2) % 2 * 2 + h % 2
                    kwb = ksb[:, slot * KW:(slot + 1) * KW]
                    vwb = vsb[:, slot * NKC * 128:(slot + 1) * NKC * 128]
                    kwk, vwk = f"ksb{slot}", f"vsb{slot}"
                    for qs in range(TT // 128):
                        psl = qs % 2
                        ps_sc = ps_scs[psl]
                        sck = f"ps_sc{psl}"
                        pbx = pb[:, psl * 1152:(psl + 1) * 1152]
                        pfx = pf32[:, psl * 1152:(psl + 1) * 1152]
                        nmx = nmax[:, psl:psl + 1]
                        ssx = ssum[:, psl:psl + 1]
                        rsx = rsum[:, psl:psl + 1]
                        qlhs = qsb[:, h * TT + qs * 128: h * TT + qs * 128 + 128]
                        kof = qs * 128
                        for o, n in ((0, 512), (512, 512), (1024, 128)):
                            em.op("pe", lambda e, o=o, n=n, kwb=kwb, qlhs=qlhs,
                                  kof=kof, ps_sc=ps_sc: e.matmul(
                                      ps_sc[:, o:o + n], lhsT=qlhs,
                                      rhs=kwb[:, kof + o:kof + o + n],
                                      start=True, stop=True),
                                  reads=[f"qsb{h}", kwk], writes=[sck])
                        em.op("pe", lambda e, ps_sc=ps_sc: e.matmul(
                            ps_sc[:, 0:128], lhsT=identb[:, :], rhs=maskab[:, :],
                            start=False, stop=True, skip_group_check=True),
                            reads=["maskab", "identb"], writes=[sck])
                        em.op("pe", lambda e, ps_sc=ps_sc: e.matmul(
                            ps_sc[:, 1024:1152], lhsT=identb[:, :], rhs=maskcb[:, :],
                            start=False, stop=True, skip_group_check=True),
                            reads=["maskcb", "identb"], writes=[sck])
                        if kw0 < WINDOW:
                            hb0 = kw0 + kof
                            for o, n in ((0, 512), (512, 512), (1024, 128)):
                                em.op("pe", lambda e, o=o, n=n, hb0=hb0,
                                      ps_sc=ps_sc: e.matmul(
                                          ps_sc[:, o:o + n], lhsT=onesrb[:, :],
                                          rhs=hbiasb[:, hb0 + o:hb0 + o + n],
                                          start=False, stop=True,
                                          skip_group_check=True),
                                      reads=["hbiasb", "onesrb"], writes=[sck])
                        em.op("dve", lambda e, nmx=nmx, ps_sc=ps_sc: e.reduce_max(
                            out=nmx, in_=ps_sc[:, 0:1152], axis=AX.X,
                            negate=True),
                            reads=[sck], writes=[f"nmax{psl}"])
                        em.op("act", lambda e, pbx=pbx, nmx=nmx, ssx=ssx,
                              ps_sc=ps_sc: e.activation(
                                  out=pbx, in_=ps_sc[:, 0:1152], func=AF.Exp,
                                  bias=nmx, accum_out=ssx),
                              reads=[sck, f"nmax{psl}"],
                              writes=[f"pb{psl}", f"ssum{psl}"])
                        em.op("dve", lambda e, rsx=rsx, ssx=ssx: e.reciprocal(
                            out=rsx, in_=ssx),
                            reads=[f"ssum{psl}"], writes=[f"rsum{psl}"])

                        def tail(h=h, qs=qs, psl=psl, pbx=pbx, vwb=vwb, vwk=vwk,
                                 ps_sc=ps_sc, sck=sck, rsx=rsx):
                            ptx = ptb[:, psl * 1152:(psl + 1) * 1152]
                            for c in range(9):
                                em.op("pe", lambda e, c=c, pbx=pbx, ps_sc=ps_sc:
                                      e.matmul(
                                          ps_sc[:, c * 128:(c + 1) * 128],
                                          lhsT=pbx[:, c * 128:(c + 1) * 128],
                                          rhs=identb[:, :], start=True, stop=True),
                                      reads=[f"pb{psl}", "identb"], writes=[sck])
                            em.op("act", lambda e, ptx=ptx, ps_sc=ps_sc: e.copy(
                                out=ptx, in_=ps_sc[:, 0:1152]),
                                reads=[sck], writes=[f"ptb{psl}"])
                            for c in range(9):
                                em.op("pe", lambda e, c=c, vwb=vwb, qs=qs,
                                      ptx=ptx: e.matmul(
                                          ps_pv[:, qs * 128:(qs + 1) * 128],
                                          lhsT=ptx[:, c * 128:(c + 1) * 128],
                                          rhs=vwb[:, (qs + c) * 128:
                                                   (qs + c + 1) * 128],
                                          start=(c == 0), stop=(c == 8),
                                          skip_group_check=True),
                                      reads=[vwk, f"ptb{psl}"], writes=["ps_pv"])
                            em.op("dve", lambda e, h=h, qs=qs, rsx=rsx:
                                  e.tensor_scalar_mul(
                                      out=attn_tm[:, (h * 4 + qs) * 128:
                                                  (h * 4 + qs + 1) * 128],
                                      in0=ps_pv[:, qs * 128:(qs + 1) * 128],
                                      scalar1=rsx),
                                  reads=["ps_pv", f"rsum{psl}"],
                                  writes=[f"attm{h}_{qs}"])

                        if prev_tail is not None:
                            prev_tail()
                        prev_tail = tail
                if prev_tail is not None:
                    prev_tail()
                for h in range(NH):
                    reg = ps_scs[h % 2]
                    rk = f"ps_sc{h % 2}"
                    for qs in range(4):
                        em.op("pe", lambda e, h=h, qs=qs, reg=reg: e.matmul(
                            reg[:, qs * 128:(qs + 1) * 128],
                            lhsT=attn_tm[:, (h * 4 + qs) * 128:
                                         (h * 4 + qs + 1) * 128],
                            rhs=identb[:, :], start=True, stop=True),
                            reads=[f"attm{h}_{qs}", "identb"], writes=[rk])
                    em.op("act", lambda e, h=h, reg=reg: e.copy(
                        out=attn[:, h * TT:(h + 1) * TT], in_=reg[:, 0:512]),
                        reads=[rk], writes=[f"attn{h}"])
                for m in range(KD):
                    sl = m % 2
                    if m % 4 == 0:
                        em.dma("sp",
                               x1c.ap().rearrange("p (blk c) -> p blk c", blk=4),
                               x1_scr[m * 128:(m + 4) * 128,
                                      qt * TT:(qt + 1) * TT]
                               .rearrange("(blk p) c -> p blk c", p=128),
                               reads=["x1_scr"],
                               writes=[f"x1c{j}" for j in range(4)])
                    psw = ps_wo[0]
                    pswk = "ps_wo0"
                    for k in range(KD):
                        em.op("pe", lambda e, psw=psw, m=m, k=k: e.matmul(
                            psw[:, :],
                            lhsT=wosb[:, (m * KD + k) * 128:(m * KD + k + 1) * 128],
                            rhs=attn[:, k * TT:(k + 1) * TT],
                            start=(k == 0), stop=(k == KD - 1)),
                            reads=[f"wosb{m}", f"attn{k}"], writes=[pswk])
                    xc1 = x1c[:, (m % 4) * TT:(m % 4 + 1) * TT]
                    xc2 = x2c[:, sl * TT:(sl + 1) * TT]
                    em.op("dve", lambda e, psw=psw, xc1=xc1, xc2=xc2: e.tensor_add(
                        out=xc2, in0=psw[:, :], in1=xc1),
                        reads=[pswk, f"x1c{m % 4}"], writes=[f"x2c{sl}"])
                    em.dma("pool", x2_scr[m * 128:(m + 1) * 128,
                                          qt * TT:(qt + 1) * TT], xc2,
                           reads=[f"x2c{sl}"], writes=["x2_scr"])
            em.barrier()

        ffn_pass("c", x2_scr, "x2_scr", o_out, "o_out", n_own_tiles)

        em.barrier()
        with nc.Block() as block:
            em.finalize(block)

    return nc


def _sylvester(n):
    h = np.array([[1.0]], dtype=np.float32)
    while h.shape[0] < n:
        h = np.block([[h, h], [h, -h]])
    return h.astype(np.float32)


def _rope_tables(pos, scale):
    inv = 1.0 / (ROPE_THETA ** (np.arange(0, HD, 2, dtype=np.float32) / HD))
    f = pos[None, :].astype(np.float32) * inv[:, None]
    c = np.cos(f) * scale
    s = np.sin(f) * scale
    return (np.concatenate([c, c], axis=0).astype(np.float32),
            np.concatenate([s, s], axis=0).astype(np.float32))


def make_core_inputs(x_full, weights, t_own=2048, cores=None, ways=None):
    B, S, _ = x_full.shape
    n_chunks = S // t_own
    t_all = t_own + WINDOW

    w_qkv = np.asarray(weights["w_qkv"], np.float32)
    q_norm_w = np.asarray(weights["q_norm_w"], np.float32)
    k_norm_w = np.asarray(weights["k_norm_w"], np.float32)
    w_o = np.asarray(weights["w_o"], np.float32)
    SU_ug = np.asarray(weights["SU_ug"], np.float32)
    SU_dp = np.asarray(weights["SU_dp"], np.float32)
    Wsc_ug = np.asarray(weights["Wscale_ug"], np.float32)
    Wsc_dp = np.asarray(weights["Wscale_dp"], np.float32)
    W_up = np.asarray(weights["W_up"], np.float32)
    W_gate = np.asarray(weights["W_gate"], np.float32)
    W_down = np.asarray(weights["W_down"], np.float32)

    com = {}
    wup_full = W_up.T.reshape(KD, 128, INTER)
    wgt_full = W_gate.T.reshape(KD, 128, INTER)
    com["wqkv_in"] = np.ascontiguousarray(
        w_qkv.reshape(3 * KD, 128, KD, 128).transpose(0, 3, 2, 1)
        .reshape(3 * KD, 128, KD * 128))
    com["wo_in"] = np.ascontiguousarray(
        w_o.reshape(KD, 128, KD, 128).transpose(0, 3, 2, 1).reshape(KD, 128, KD * 128))
    wdn_full = W_down.T.reshape(KI, 128, D)
    com["su_ug"] = np.ascontiguousarray(SU_ug.reshape(KD, 128).T)
    com["su_dp"] = np.ascontiguousarray(SU_dp.reshape(KI, 128).T)
    com["wsc_u"] = np.ascontiguousarray(Wsc_ug[:INTER].reshape(KI, 128).T)
    com["wsc_g"] = np.ascontiguousarray(Wsc_ug[INTER:].reshape(KI, 128).T)
    com["wsc_dp"] = np.ascontiguousarray(Wsc_dp.reshape(KD, 128).T)
    com["qnw"] = np.ascontiguousarray(q_norm_w.reshape(KD, 128).T)
    com["knw"] = np.ascontiguousarray(k_norm_w.reshape(KD, 128).T)
    com["h128"] = _sylvester(128)
    com["ident"] = np.eye(128, dtype=np.float32)
    psw = np.zeros((128, 128), np.float32)
    for mcol in range(64):
        psw[mcol + 64, mcol] = -1.0
        psw[mcol, mcol + 64] = 1.0
    com["pswap"] = psw
    p_i = np.arange(128)[:, None]
    k_i = np.arange(128)[None, :]
    com["mask_c"] = np.where(p_i >= k_i, 0.0, NEG).astype(np.float32)
    com["mask_a"] = np.where(p_i < k_i, 0.0, NEG).astype(np.float32)

    if cores is None:
        cores = [(b, ch) for b in range(B) for ch in range(n_chunks)]
    if ways is None:
        ways = len(cores)
    in_maps = []
    for ci, (b, ch) in enumerate(cores):
        d = dict(com)
        mu = INTER // ways
        md = D // ways
        d["wup_in"] = np.ascontiguousarray(wup_full[:, :, ci * mu:(ci + 1) * mu])
        d["wgt_in"] = np.ascontiguousarray(wgt_full[:, :, ci * mu:(ci + 1) * mu])
        d["wdn_in"] = np.ascontiguousarray(wdn_full[:, :, ci * md:(ci + 1) * md])
        own = x_full[b, ch * t_own:(ch + 1) * t_own, :]
        d["x_in"] = np.ascontiguousarray(own.T)
        g0 = ch * t_own
        own_pos = np.arange(g0, g0 + t_own)
        cq, sq = _rope_tables(own_pos, 1.0 / np.sqrt(HD))
        ck, sk = _rope_tables(own_pos, 1.0)
        d["cosq"], d["sinq"] = cq, sq
        d["cosk"], d["sink"] = ck, sk
        hb = np.zeros((1, t_all), np.float32)
        n_inv = max(0, min(WINDOW, WINDOW - g0))
        if n_inv > 0:
            hb[0, :n_inv] = NEG
        d["hbias"] = hb
        d["prevrank"] = np.array([[(ci - 1) % max(len(cores), 1)]], np.int32)
        in_maps.append(d)
    return in_maps


def kernel(L0_attn_hidden_states, L0_attn_w_qkv, L0_attn_q_norm_w,
           L0_attn_k_norm_w, L0_attn_w_o, L0_ffn_SU_ug, L0_ffn_SU_dp,
           L0_ffn_Wscale_ug, L0_ffn_Wscale_dp, L0_ffn_W_up, L0_ffn_W_gate,
           L0_ffn_W_down):
    x = np.asarray(L0_attn_hidden_states, np.float32)
    B, S, _ = x.shape
    t_own = S // 4
    weights = {
        "w_qkv": L0_attn_w_qkv, "q_norm_w": L0_attn_q_norm_w,
        "k_norm_w": L0_attn_k_norm_w, "w_o": L0_attn_w_o,
        "SU_ug": L0_ffn_SU_ug, "SU_dp": L0_ffn_SU_dp,
        "Wscale_ug": L0_ffn_Wscale_ug, "Wscale_dp": L0_ffn_Wscale_dp,
        "W_up": L0_ffn_W_up, "W_gate": L0_ffn_W_gate, "W_down": L0_ffn_W_down,
    }
    in_maps = make_core_inputs(x, weights, t_own=t_own, ways=N_CORES)
    nc = build_program(t_own, ways=N_CORES)
    res = run_bass_kernel_spmd(nc, in_maps, core_ids=list(range(N_CORES)))
    out = np.empty((B, S, D), np.float32)
    ci = 0
    for b in range(B):
        for ch in range(4):
            out[b, ch * t_own:(ch + 1) * t_own, :] = \
                np.asarray(res.results[ci]["o_out"]).T
            ci += 1
    return out


# version 20
# speedup vs baseline: 1.0196x; 1.0196x over previous
import numpy as np
from contextlib import ExitStack

import concourse.bass as bass
import concourse.mybir as mybir
from concourse.bass_utils import run_bass_kernel_spmd

F32 = mybir.dt.float32
BF16 = mybir.dt.bfloat16
AF = mybir.ActivationFunctionType
ALU = mybir.AluOpType
AX = mybir.AxisListType

D = 2048
NH = 16
HD = 128
INTER = 8192
WINDOW = 1024
KD = D // 128
KI = INTER // 128
TT = 512
NEG = -1.0e30
LN_EPS = 1e-5
RMS_EPS = 1e-6
ROPE_THETA = 10000.0
N_CORES = 8


class Emitter:
    ENGS = ("pe", "act", "dve", "pool", "sp")
    NDMA = 8

    def __init__(self, nc, ctx):
        self.nc = nc
        self.ops = {e: [] for e in self.ENGS}
        self.sems = {}
        names = ["s_pe", "s_act", "s_dve", "s_pool", "s_sp"]
        names += [f"dh{i}" for i in range(self.NDMA)]
        names += [f"dg{i}" for i in range(self.NDMA)]
        names += ["ccs"]
        for name in names:
            self.sems[name] = ctx.enter_context(nc.semaphore(name))
        self.count = {name: 0 for name in self.sems}
        self.clock = {e: {name: 0 for name in self.sems} for e in self.ENGS}
        self.res = {}
        self.eng_sem = {"pe": "s_pe", "act": "s_act", "dve": "s_dve",
                        "pool": "s_pool", "sp": "s_sp"}
        self.dma_idx = {"sp": 0, "pool": 0}

    def _entry(self, key):
        e = self.res.get(key)
        if e is None:
            e = {"w": None, "r": []}
            self.res[key] = e
        return e

    def _emit(self, eng, builder, reads, writes, done_dep, extra_waits=()):
        waits = {}

        def need(dep):
            if dep is not None and waits.get(dep[0], 0) < dep[1]:
                waits[dep[0]] = dep[1]

        for dep in extra_waits:
            need(dep)
        for k in reads:
            need(self._entry(k)["w"])
        for k in writes:
            e = self._entry(k)
            need(e["w"])
            for rd in e["r"]:
                need(rd)
        ck = self.clock[eng]
        emitted = []
        for sem, val in waits.items():
            if eng == "pe" and sem == "s_pe":
                continue
            if ck[sem] < val:
                emitted.append((self.sems[sem], val))
                ck[sem] = val
        self.ops[eng].append((emitted, builder))
        for k in reads:
            self._entry(k)["r"].append(done_dep)
        for k in writes:
            e = self._entry(k)
            e["w"] = done_dep
            e["r"] = []

    def op(self, eng, builder, reads=(), writes=()):
        sem = self.eng_sem[eng]
        self.count[sem] += 1
        s, v = self.sems[sem], self.count[sem]

        def b(e, _b=builder, _s=s):
            _b(e).then_inc(_s, 1)

        self._emit(eng, b, reads, writes, (sem, self.count[sem]))

    def dma(self, queue, out, in_, reads=(), writes=()):
        pref = "dh" if queue == "sp" else "dg"
        dsem = f"{pref}{self.dma_idx[queue] % self.NDMA}"
        self.dma_idx[queue] += 1
        prev = self.count[dsem]
        self.count[dsem] = prev + 16
        s = self.sems[dsem]

        def b(e, _o=out, _i=in_, _s=s):
            e.dma_start(out=_o, in_=_i).then_inc(_s, 16)

        self._emit(queue, b, reads, writes, (dsem, prev + 16),
                   extra_waits=((dsem, prev),) if prev else ())

    def dma_custom(self, queue, builder_fn, reads=(), writes=()):
        pref = "dh" if queue == "sp" else "dg"
        dsem = f"{pref}{self.dma_idx[queue] % self.NDMA}"
        self.dma_idx[queue] += 1
        prev = self.count[dsem]
        self.count[dsem] = prev + 16
        s = self.sems[dsem]

        def b(e, _s=s):
            builder_fn(e).then_inc(_s, 16)

        self._emit(queue, b, reads, writes, (dsem, prev + 16),
                   extra_waits=((dsem, prev),) if prev else ())

    def collective(self, nc_fn, reads=(), writes=()):
        dsem = "ccs"
        if dsem not in self.sems:
            raise RuntimeError("ccs semaphore not allocated")
        self.count[dsem] += 1
        s, v = self.sems[dsem], self.count[dsem]

        def b(e, _s=s):
            nc_fn(e).then_inc(_s)

        self._emit("pool", b, reads, writes, (dsem, v))

    def barrier(self):
        snap = dict(self.count)
        for eng in self.ENGS:
            ck = self.clock[eng]
            emitted = []
            for sem, val in snap.items():
                if sem != self.eng_sem[eng] and ck[sem] < val:
                    emitted.append((self.sems[sem], val))
                    ck[sem] = val
            if emitted:
                self.ops[eng].append((emitted, None))
        self.res = {}

    def finalize(self, block):
        def mk(eng_name):
            ops = self.ops[eng_name]

            def body(e):
                for waits, builder in ops:
                    for s, v in waits:
                        e.wait_ge(s, v)
                    if builder is not None:
                        builder(e)
            return body
        block.tensor(mk("pe"))
        block.scalar(mk("act"))
        block.vector(mk("dve"))
        block.gpsimd(mk("pool"))
        block.sync(mk("sp"))


def build_program(t_own, ways=N_CORES):
    t_all = t_own + WINDOW
    n_tiles = t_all // TT
    n_own_tiles = t_own // TT
    own0 = WINDOW

    nc = bass.Bass("TRN2", target_bir_lowering=False)

    def din(name, shape, dt=F32):
        return nc.dram_tensor(name, shape, dt, kind="ExternalInput").ap()

    x_in = din("x_in", [D, t_own])
    wup_in = din("wup_in", [KD, 128, INTER // ways])
    wgt_in = din("wgt_in", [KD, 128, INTER // ways])
    wqkv_in = din("wqkv_in", [3 * KD, 128, KD * 128])
    wo_in = din("wo_in", [KD, 128, KD * 128])
    wdn_in = din("wdn_in", [KI, 128, D // ways])
    su_ug = din("su_ug", [128, KD])
    su_dp = din("su_dp", [128, KI])
    wsc_u = din("wsc_u", [128, KI])
    wsc_g = din("wsc_g", [128, KI])
    wsc_dp = din("wsc_dp", [128, KD])
    qnw = din("qnw", [128, KD])
    knw = din("knw", [128, KD])
    cosq = din("cosq", [128, t_own])
    sinq = din("sinq", [128, t_own])
    cosk = din("cosk", [128, t_own])
    sink = din("sink", [128, t_own])
    prevrank = din("prevrank", [1, 1], mybir.dt.int32)
    hbias = din("hbias", [1, t_all])
    h128 = din("h128", [128, 128])
    ident = din("ident", [128, 128])
    pswap = din("pswap", [128, 128])
    mask_c = din("mask_c", [128, 128])
    mask_a = din("mask_a", [128, 128])

    o_out = nc.dram_tensor("o_out", [D, t_own], F32, kind="ExternalOutput").ap()

    def dscr(name, shape, dt):
        return nc.dram_tensor(name, shape, dt, kind="Internal").ap()

    scr_up = dscr("scr_up", [KI, 128, KD * 128], BF16)
    scr_gt = dscr("scr_gt", [KI, 128, KD * 128], BF16)
    if ways > 1:
        sup_my = dscr("sup_my", [KI // ways, 128, KD * 128], BF16)
        sgt_my = dscr("sgt_my", [KI // ways, 128, KD * 128], BF16)
        sdn_my = dscr("sdn_my", [KD // ways, 128, KI * 128], BF16)
    scr_qkv = dscr("scr_qkv", [3 * KD, 128, KD * 128], BF16)
    scr_dn = dscr("scr_dn", [KD, 128, KI * 128], BF16)
    x1_scr = dscr("x1_scr", [D, t_own], F32)
    ktail = dscr("ktail", [NH, 128, WINDOW], BF16)
    vtail = dscr("vtail", [NH, 128, WINDOW // 128, 128], BF16)
    kgat_h = nc.dram_tensor("kgat", [ways, NH, 128, WINDOW], BF16,
                            kind="Internal")
    vgat_h = nc.dram_tensor("vgat", [ways, NH, 128, WINDOW // 128, 128],
                            BF16, kind="Internal")
    kgat = kgat_h.ap()
    vgat = vgat_h.ap()
    x2_scr = dscr("x2_scr", [D, t_own], F32)
    q_scr = dscr("q_scr", [128, NH, t_own], BF16)
    k_scr = dscr("k_scr", [NH, 128, t_all], BF16)
    v_scr = dscr("v_scr", [NH, 128, t_all // 128, 128], BF16)

    with ExitStack() as top:
        em = Emitter(nc, top)
        tsb = lambda n, s, d: top.enter_context(nc.sbuf_tensor(n, s, d))

        h128b = tsb("h128b", [128, 128], BF16)
        identb = tsb("identb", [128, 128], BF16)
        pswapb = tsb("pswapb", [128, 128], BF16)
        maskcb = tsb("maskcb", [128, 128], BF16)
        maskab = tsb("maskab", [128, 128], BF16)
        hbiasb = tsb("hbiasb", [1, t_all], BF16)
        onesb = tsb("onesb", [128, 1], BF16)
        onesrow = tsb("onesrow", [1, 128], F32)
        onesrb = tsb("onesrb", [1, 128], BF16)
        su_ug_s = tsb("su_ug_s", [128, KD], F32)
        su_dp_s = tsb("su_dp_s", [128, KI], F32)
        wsc_u_s = tsb("wsc_u_s", [128, KI], F32)
        wsc_g_s = tsb("wsc_g_s", [128, KI], F32)
        wsc_dp_s = tsb("wsc_dp_s", [128, KD], F32)
        qnw_s = tsb("qnw_s", [128, KD], F32)
        knw_s = tsb("knw_s", [128, KD], F32)

        for buf, src, key in ((h128b, h128, "h128b"), (identb, ident, "identb"),
                              (pswapb, pswap, "pswapb"), (maskcb, mask_c, "maskcb"),
                              (maskab, mask_a, "maskab"), (hbiasb, hbias, "hbiasb")):
            em.dma("pool", buf[:, :], src[:, :], writes=[key])
        for buf, src, key in ((su_ug_s, su_ug, "su_ug_s"), (su_dp_s, su_dp, "su_dp_s"),
                              (wsc_u_s, wsc_u, "wsc_u_s"), (wsc_g_s, wsc_g, "wsc_g_s"),
                              (wsc_dp_s, wsc_dp, "wsc_dp_s"), (qnw_s, qnw, "qnw_s"),
                              (knw_s, knw, "knw_s")):
            em.dma("sp", buf[:, :], src[:, :], writes=[key])
        em.op("dve", lambda e: e.memset(onesb[:, :], 1.0), writes=["onesb"])
        zerot = tsb("zerot", [128, WINDOW], BF16)
        em.op("dve", lambda e: e.memset(zerot[:, :], 0.0), writes=["zerot"])
        for h_ in range(NH):
            em.dma("sp", k_scr[h_, :, 0:WINDOW], zerot[:, :],
                   reads=["zerot"], writes=["k_scr"])
            em.dma("sp",
                   v_scr[h_, :, 0:WINDOW // 128, :].rearrange(
                       "b c d -> b (c d)"),
                   zerot[:, :], reads=["zerot"], writes=["v_scr"])
        em.op("dve", lambda e: e.memset(onesrow[:, :], 1.0), writes=["onesrow"])
        em.op("dve", lambda e: e.memset(onesrb[:, :], 1.0), writes=["onesrb"])
        em.op("dve", lambda e: e.tensor_scalar_mul(
            out=su_ug_s[:, :], in0=su_ug_s[:, :], scalar1=float(1.0 / np.sqrt(D))),
            reads=["su_ug_s"], writes=["su_ug_s"])
        em.op("dve", lambda e: e.tensor_scalar_mul(
            out=su_dp_s[:, :], in0=su_dp_s[:, :], scalar1=float(1.0 / np.sqrt(INTER))),
            reads=["su_dp_s"], writes=["su_dp_s"])
        em.op("dve", lambda e: e.tensor_scalar_mul(
            out=wsc_dp_s[:, :], in0=wsc_dp_s[:, :], scalar1=0.5),
            reads=["wsc_dp_s"], writes=["wsc_dp_s"])
        comb_us = tsb("comb_us", [128, KI], F32)
        em.op("dve", lambda e: e.tensor_mul(
            out=comb_us[:, :], in0=wsc_u_s[:, :], in1=su_dp_s[:, :]),
            reads=["wsc_u_s", "su_dp_s"], writes=["comb_us"])

        with ExitStack() as ph:
            psb = lambda n, s, d: ph.enter_context(nc.sbuf_tensor(n, s, d))
            cb = psb("p0cb", [128, 64 * 256], BF16)
            pbuf = psb("p0pb", [128, 64 * 256], BF16)
            wst = [psb(f"p0wst{i}", [128, KI * 128], BF16) for i in range(4)]
            ps0 = [ph.enter_context(nc.psum_tensor(f"p0ps{i}", [128, 512], F32))
                   for i in range(2)]

            def transform(src_ap, nt, span, scale_ap, scale_key, scr, scr_pref,
                          m_total):
                nmc = span // 128
                for s in range(m_total // span):
                    for i in range(nt):
                        em.dma("pool", cb[:, i * span:(i + 1) * span],
                               src_ap[i, :, s * span:(s + 1) * span],
                               writes=[f"p0cb{i}"])
                    bufs = {"p0cb": cb, "p0pb": pbuf}
                    srck, dstk = "p0cb", "p0pb"
                    h = 1
                    while h < nt:
                        for g in range(0, nt, 2 * h):
                            for k in range(h):
                                a, b = g + k, g + k + h
                                sa = bufs[srck][:, a * span:(a + 1) * span]
                                sbb = bufs[srck][:, b * span:(b + 1) * span]
                                oa = bufs[dstk][:, a * span:(a + 1) * span]
                                ob = bufs[dstk][:, b * span:(b + 1) * span]
                                em.op("dve", lambda e, oa=oa, sa=sa, sbb=sbb:
                                      e.tensor_add(out=oa, in0=sa, in1=sbb),
                                      reads=[f"{srck}{a}", f"{srck}{b}"],
                                      writes=[f"{dstk}{a}"])
                                em.op("dve", lambda e, ob=ob, sa=sa, sbb=sbb:
                                      e.tensor_sub(out=ob, in0=sa, in1=sbb),
                                      reads=[f"{srck}{a}", f"{srck}{b}"],
                                      writes=[f"{dstk}{b}"])
                        srck, dstk = dstk, srck
                        h *= 2
                    for i in range(nt):
                        ps = ps0[i % 2]
                        pk = f"p0ps{i % 2}"
                        em.op("pe", lambda e, ps=ps, span=span,
                              r=bufs[srck][:, i * span:(i + 1) * span]:
                              e.matmul(ps[:, 0:span], lhsT=h128b[:, :], rhs=r,
                                       start=True, stop=True),
                              reads=[f"{srck}{i}", "h128b"], writes=[pk])
                        for c in range(nmc):
                            dst = wst[c][:, i * 128:(i + 1) * 128]
                            if scale_ap is None:
                                em.op("act", lambda e, ps=ps, dst=dst, c=c:
                                      e.copy(out=dst, in_=ps[:, c * 128:
                                                            (c + 1) * 128]),
                                      reads=[pk], writes=[f"p0w{c}_{i}"])
                            else:
                                em.op("dve", lambda e, ps=ps, dst=dst, c=c, i=i:
                                      e.tensor_scalar_mul(
                                          out=dst, in0=ps[:, c * 128:(c + 1) * 128],
                                          scalar1=scale_ap[:, i:i + 1]),
                                      reads=[pk, scale_key],
                                      writes=[f"p0w{c}_{i}"])
                    for c in range(nmc):
                        mc = s * nmc + c
                        em.dma("pool", scr[mc, :, 0:nt * 128],
                               wst[c][:, 0:nt * 128],
                               reads=[f"p0w{c}_{i}" for i in range(nt)],
                               writes=[f"{scr_pref}{mc}"])

            if ways == 1:
                transform(wup_in, KD, 512, su_ug_s, "su_ug_s", scr_up, "scrup",
                          INTER)
                transform(wgt_in, KD, 512, su_ug_s, "su_ug_s", scr_gt, "scrgt",
                          INTER)
                transform(wdn_in, KI, 256, None, None, scr_dn, "scrdn", D)
            else:
                rg = [list(range(ways))]
                nmu = KI // ways
                nmd = KD // ways
                transform(wup_in, KD, 512, su_ug_s, "su_ug_s", sup_my, "supmy",
                          INTER // ways)
                em.collective(
                    lambda e: e.collective_compute(
                        "AllGather", ALU.bypass, replica_groups=rg,
                        ins=[sup_my.rearrange("a b c -> (a b) c").opt()],
                        outs=[scr_up.rearrange("a b c -> (a b) c").opt()]),
                    reads=[f"supmy{i}" for i in range(nmu)],
                    writes=[f"scrup{m}" for m in range(KI)])
                transform(wgt_in, KD, 512, su_ug_s, "su_ug_s", sgt_my, "sgtmy",
                          INTER // ways)
                em.collective(
                    lambda e: e.collective_compute(
                        "AllGather", ALU.bypass, replica_groups=rg,
                        ins=[sgt_my.rearrange("a b c -> (a b) c").opt()],
                        outs=[scr_gt.rearrange("a b c -> (a b) c").opt()]),
                    reads=[f"sgtmy{i}" for i in range(nmu)],
                    writes=[f"scrgt{m}" for m in range(KI)])
                transform(wdn_in, KI, 256, None, None, sdn_my, "sdnmy", D // ways)
                em.collective(
                    lambda e: e.collective_compute(
                        "AllGather", ALU.bypass, replica_groups=rg,
                        ins=[sdn_my.rearrange("a b c -> (a b) c").opt()],
                        outs=[scr_dn.rearrange("a b c -> (a b) c").opt()]),
                    reads=[f"sdnmy{i}" for i in range(nmd)],
                    writes=[f"scrdn{m}" for m in range(KD)])
            em.barrier()

        def ffn_pass(tag, src_ap, src_key, dst_ap, dst_key, tiles):
            with ExitStack() as ph:
                psb = lambda n, s, d: ph.enter_context(nc.sbuf_tensor(tag + n, s, d))
                xbf = psb("xbf", [128, KD * TT], BF16)
                sqs = psb("sqs", [128, 2 * TT], BF16)
                xcs = psb("xcs", [128, 2 * TT], F32)
                rows = psb("rows", [1, 3 * TT], F32)
                m_b = psb("m_b", [128, TT], F32)
                r_b = psb("r_b", [128, TT], F32)
                xf = psb("xf", [128, KD * TT], BF16)
                yb = psb("yb", [128, KI * TT], BF16)
                gst = psb("gst", [128, 2 * TT], F32)
                wub = psb("wub", [128, 4 * KD * 128], BF16)
                wgb = psb("wgb", [128, 4 * KD * 128], BF16)
                wdb = psb("wdb", [128, 2 * KI * 128], BF16)
                xrs = psb("xrs", [128, 2 * TT], F32)
                x1s = psb("x1s", [128, 2 * TT], F32)
                ps_s1 = ph.enter_context(nc.psum_tensor(tag + "ps_s1", [1, TT], F32))
                ps_s2 = ph.enter_context(nc.psum_tensor(tag + "ps_s2", [1, TT], F32))
                ps_bc = ph.enter_context(nc.psum_tensor(tag + "ps_bc", [128, TT], F32))
                ps_u = [ph.enter_context(
                    nc.psum_tensor(tag + f"ps_u{i}", [128, TT], F32)) for i in range(2)]
                ps_g = [ph.enter_context(
                    nc.psum_tensor(tag + f"ps_g{i}", [128, TT], F32)) for i in range(2)]

                def prologue_pieces(t):
                    c0 = t * TT
                    loads, pieces = [], []
                    for i0 in range(0, KD, 8):
                        def pld(i0=i0, c0=c0):
                            em.dma("pool",
                                   xbf[:, i0 * TT:(i0 + 8) * TT].rearrange(
                                       "p (blk c) -> p blk c", blk=8),
                                   src_ap[i0 * 128:(i0 + 8) * 128, c0:c0 + TT]
                                   .rearrange("(blk p) c -> p blk c", p=128),
                                   reads=[src_key],
                                   writes=[f"xbf{i}" for i in range(i0, i0 + 8)])
                        loads.append(pld)
                    for i in range(KD):
                        def pst(i=i):
                            sl = i % 2
                            sq = sqs[:, sl * TT:(sl + 1) * TT]
                            em.op("act", lambda e, sq=sq, i=i: e.activation(
                                out=sq, in_=xbf[:, i * TT:(i + 1) * TT],
                                func=AF.Square),
                                reads=[f"xbf{i}"], writes=[f"sqs{sl}"])
                            em.op("pe", lambda e, i=i: e.matmul(
                                ps_s1[:, :], lhsT=onesb[:, :],
                                rhs=xbf[:, i * TT:(i + 1) * TT],
                                start=(i == 0), stop=(i == KD - 1),
                                skip_group_check=True),
                                reads=[f"xbf{i}", "onesb"], writes=["ps_s1"])
                            em.op("pe", lambda e, sq=sq, i=i: e.matmul(
                                ps_s2[:, :], lhsT=onesb[:, :], rhs=sq,
                                start=(i == 0), stop=(i == KD - 1),
                                skip_group_check=True),
                                reads=[f"sqs{sl}", "onesb"], writes=["ps_s2"])
                        pieces.append(pst)

                    def prows():
                        mr, vr = rows[:, 0:TT], rows[:, TT:2 * TT]
                        tr = rows[:, 2 * TT:3 * TT]
                        rr = mr
                        em.op("dve", lambda e: e.tensor_scalar_mul(
                            out=mr, in0=ps_s1[:, :], scalar1=1.0 / D),
                            reads=["ps_s1"], writes=["row_m"])
                        em.op("dve", lambda e: e.tensor_scalar_mul(
                            out=vr, in0=ps_s2[:, :], scalar1=1.0 / D),
                            reads=["ps_s2"], writes=["row_v"])
                        em.op("dve", lambda e: e.tensor_mul(out=tr, in0=mr, in1=mr),
                              reads=["row_m"], writes=["row_t"])
                        em.op("dve", lambda e: e.tensor_sub(out=vr, in0=vr, in1=tr),
                              reads=["row_v", "row_t"], writes=["row_v"])
                        em.op("dve", lambda e: e.tensor_scalar_add(
                            out=vr, in0=vr, scalar1=LN_EPS),
                            reads=["row_v"], writes=["row_v"])
                        em.op("dve", lambda e: e.reciprocal(out=tr, in_=vr),
                              reads=["row_v"], writes=["row_t"])
                        em.op("pe", lambda e: e.matmul(
                            ps_bc[:, :], lhsT=onesrow[:, :], rhs=mr,
                            start=True, stop=True),
                            reads=["row_m", "onesrow"], writes=["ps_bc"])
                        em.op("dve", lambda e: e.tensor_copy(
                            out=m_b[:, :], in_=ps_bc[:, :]),
                            reads=["ps_bc"], writes=["m_b"])
                        em.op("act", lambda e: e.activation(
                            out=rr, in_=tr, func=AF.Sqrt),
                            reads=["row_t"], writes=["row_m"])
                        em.op("pe", lambda e: e.matmul(
                            ps_bc[:, :], lhsT=onesrow[:, :], rhs=rr,
                            start=True, stop=True),
                            reads=["row_m", "onesrow"], writes=["ps_bc"])
                        em.op("dve", lambda e: e.tensor_copy(
                            out=r_b[:, :], in_=ps_bc[:, :]),
                            reads=["ps_bc"], writes=["r_b"])
                    pieces.append(prows)
                    for i in range(KD):
                        def papp(i=i):
                            sl = i % 2
                            xc = xcs[:, sl * TT:(sl + 1) * TT]
                            em.op("dve", lambda e, xc=xc, i=i: e.tensor_sub(
                                out=xc, in0=xbf[:, i * TT:(i + 1) * TT],
                                in1=m_b[:, :]),
                                reads=[f"xbf{i}", "m_b"], writes=[f"xcs{sl}"])
                            em.op("dve", lambda e, xc=xc, i=i: e.tensor_mul(
                                out=xf[:, i * TT:(i + 1) * TT], in0=xc,
                                in1=r_b[:, :]),
                                reads=[f"xcs{sl}", "r_b"], writes=[f"xf{i}"])
                        pieces.append(papp)
                    return loads, pieces

                def emit_upgate(t):
                    for m in range(KI):
                        sl = m % 4
                        wu = wub[:, sl * KD * 128:(sl + 1) * KD * 128]
                        wg = wgb[:, sl * KD * 128:(sl + 1) * KD * 128]
                        wuk, wgk = f"wub{sl}", f"wgb{sl}"
                        em.dma("sp", wu, scr_up[m, :, :], reads=[f"scrup{m}"],
                               writes=[wuk])
                        em.dma("sp", wg, scr_gt[m, :, :], reads=[f"scrgt{m}"],
                               writes=[wgk])
                        psu, psg = ps_u[m % 2], ps_g[m % 2]
                        psuk, psgk = f"ps_u{m % 2}", f"ps_g{m % 2}"
                        for k in range(KD):
                            em.op("pe", lambda e, psu=psu, wu=wu, k=k: e.matmul(
                                psu[:, :], lhsT=wu[:, k * 128:(k + 1) * 128],
                                rhs=xf[:, k * TT:(k + 1) * TT],
                                start=(k == 0), stop=(k == KD - 1)),
                                reads=[wuk, f"xf{k}"], writes=[psuk])
                        for k in range(KD):
                            em.op("pe", lambda e, psg=psg, wg=wg, k=k: e.matmul(
                                psg[:, :], lhsT=wg[:, k * 128:(k + 1) * 128],
                                rhs=xf[:, k * TT:(k + 1) * TT],
                                start=(k == 0), stop=(k == KD - 1)),
                                reads=[wgk, f"xf{k}"], writes=[psgk])
                        gs = gst[:, (m % 2) * TT:(m % 2 + 1) * TT]
                        em.op("act", lambda e, psg=psg, gs=gs, m=m: e.activation(
                            out=gs, in_=psg[:, :], func=AF.Silu,
                            scale=wsc_g_s[:, m:m + 1]),
                            reads=[psgk, "wsc_g_s"], writes=[f"gst{m % 2}"])
                        em.op("dve", lambda e, psu=psu, gs=gs, m=m:
                              e.scalar_tensor_tensor(
                                  out=yb[:, m * TT:(m + 1) * TT], in0=psu[:, :],
                                  scalar=comb_us[:, m:m + 1], in1=gs,
                                  op0=ALU.mult, op1=ALU.mult),
                              reads=[psuk, f"gst{m % 2}", "comb_us"],
                              writes=[f"yb{m}"])

                def down_pieces(t):
                    c0 = t * TT
                    pieces = []
                    for m in range(KD):
                        def pdn(m=m, c0=c0):
                            sl = m % 2
                            wd = wdb[:, sl * KI * 128:(sl + 1) * KI * 128]
                            wdk = f"wdb{sl}"
                            em.dma("sp", wd, scr_dn[m, :, :], reads=[f"scrdn{m}"],
                                   writes=[wdk])
                            xr = xrs[:, sl * TT:(sl + 1) * TT]
                            em.dma("sp", xr,
                                   src_ap[m * 128:(m + 1) * 128, c0:c0 + TT],
                                   reads=[src_key], writes=[f"xrs{sl}"])
                            psd = ps_u[sl]
                            psdk = f"ps_u{sl}"
                            for k in range(KI):
                                em.op("pe", lambda e, psd=psd, wd=wd, k=k: e.matmul(
                                    psd[:, :], lhsT=wd[:, k * 128:(k + 1) * 128],
                                    rhs=yb[:, k * TT:(k + 1) * TT],
                                    start=(k == 0), stop=(k == KI - 1)),
                                    reads=[wdk, f"yb{k}"], writes=[psdk])
                            x1c = x1s[:, sl * TT:(sl + 1) * TT]
                            em.op("dve", lambda e, psd=psd, x1c=x1c, xr=xr, m=m:
                                  e.scalar_tensor_tensor(
                                      out=x1c, in0=psd[:, :],
                                      scalar=wsc_dp_s[:, m:m + 1], in1=xr,
                                      op0=ALU.mult, op1=ALU.add),
                                  reads=[psdk, f"xrs{sl}", "wsc_dp_s"],
                                  writes=[f"x1s{sl}"])
                            em.dma("pool",
                                   dst_ap[m * 128:(m + 1) * 128, c0:c0 + TT], x1c,
                                   reads=[f"x1s{sl}"], writes=[dst_key])
                        pieces.append(pdn)
                    return pieces

                prev_down = None
                for t in range(tiles):
                    loads, pro = prologue_pieces(t)
                    for p in loads:
                        p()
                    if prev_down is None:
                        for p in pro:
                            p()
                    else:
                        nd = len(prev_down)
                        npr = len(pro)
                        j = 0
                        for i, pd in enumerate(prev_down):
                            pd()
                            jn = (i + 1) * npr // nd
                            while j < jn:
                                pro[j]()
                                j += 1
                    emit_upgate(t)
                    prev_down = down_pieces(t)
                for p in prev_down:
                    p()
                em.barrier()

        ffn_pass("a1", x_in, "x_in", x1_scr, "x1_scr", n_own_tiles)

        with ExitStack() as ph:
            psb = lambda n, s, d: ph.enter_context(nc.sbuf_tensor("a2" + n, s, d))
            x1bf = psb("x1bf", [128, 2 * KD * TT], BF16)
            sqs = psb("sqs", [128, 2 * TT], BF16)
            xn2 = psb("xn2", [128, 2 * KD * TT], BF16)
            qf = psb("qf", [128, KD * TT], BF16)
            kf = psb("kf", [128, KD * TT], BF16)
            cq2 = psb("cq2", [128, TT], F32)
            sq2 = psb("sq2", [128, TT], F32)
            ck2 = psb("ck2", [128, TT], F32)
            sk2 = psb("sk2", [128, TT], F32)
            rows = psb("rows", [1, 6 * TT], F32)
            m_b = psb("m_b", [128, TT], F32)
            r_b = psb("r_b", [128, TT], F32)
            qr_b = psb("qr_b", [128, TT], F32)
            kr_b = psb("kr_b", [128, TT], F32)
            wqb = psb("wqb", [128, 4 * KD * 128], BF16)
            cqt = psb("cqt", [128, 2 * TT], F32)
            sqt = psb("sqt", [128, 2 * TT], F32)
            ckt = psb("ckt", [128, 2 * TT], F32)
            skt = psb("skt", [128, 2 * TT], F32)
            uu = psb("uu", [128, TT], F32)
            ub = psb("ub", [128, TT], BF16)
            swf = psb("swf", [128, TT], F32)
            aa = psb("aa", [128, TT], F32)
            bb2 = psb("bb2", [128, TT], F32)
            qkb = psb("qkb", [128, 2 * TT], BF16)
            vbs = psb("vbs", [128, TT], BF16)
            vts = psb("vts", [128, 2 * 4 * 128], BF16)
            ps_qk = [ph.enter_context(nc.psum_tensor(f"a2ps{i}", [128, TT], F32))
                     for i in range(2)]
            ps_sq = ph.enter_context(nc.psum_tensor("a2psq", [1, TT], F32))
            ps_sk = ph.enter_context(nc.psum_tensor("a2psk", [1, TT], F32))
            ps_bc = ph.enter_context(nc.psum_tensor("a2psbc", [128, TT], F32))
            ps_sw = ph.enter_context(nc.psum_tensor("a2pssw", [128, TT], F32))
            ps_tr = [ph.enter_context(nc.psum_tensor(f"a2pstr{i}", [128, 128], F32))
                     for i in range(2)]

            def a2_loads(t):
                c0 = t * TT
                xsl = (t % 2) * KD * TT
                tsl = (t % 2) * TT
                for i0 in range(0, KD, 8):
                    em.dma("pool",
                           x1bf[:, xsl + i0 * TT:xsl + (i0 + 8) * TT].rearrange(
                               "p (blk c) -> p blk c", blk=8),
                           x1_scr[i0 * 128:(i0 + 8) * 128, c0:c0 + TT]
                           .rearrange("(blk p) c -> p blk c", p=128),
                           reads=["x1_scr"],
                           writes=[f"x1bf{t % 2}_{i}" for i in range(i0, i0 + 8)])
                em.dma("sp", ckt[:, tsl:tsl + TT], cosk[:, c0:c0 + TT],
                       writes=[f"ckt{t % 2}"])
                em.dma("sp", skt[:, tsl:tsl + TT], sink[:, c0:c0 + TT],
                       writes=[f"skt{t % 2}"])
                em.dma("sp", cqt[:, tsl:tsl + TT], cosq[:, c0:c0 + TT],
                       writes=[f"cqt{t % 2}"])
                em.dma("sp", sqt[:, tsl:tsl + TT], sinq[:, c0:c0 + TT],
                       writes=[f"sqt{t % 2}"])

            def a2_prologue_pieces(t):
                xsl = (t % 2) * KD * TT
                xk = lambda i: f"x1bf{t % 2}_{i}"
                pieces = []
                for i in range(KD):
                    def pst(i=i):
                        sl = i % 2
                        sq = sqs[:, sl * TT:(sl + 1) * TT]
                        xb_ = x1bf[:, xsl + i * TT:xsl + (i + 1) * TT]
                        em.op("act", lambda e, xb_=xb_, sq=sq: e.activation(
                            out=sq, in_=xb_, func=AF.Square),
                            reads=[xk(i)], writes=[f"sqs{sl}"])
                        em.op("pe", lambda e, xb_=xb_, i=i: e.matmul(
                            ps_sq[:, :], lhsT=onesb[:, :], rhs=xb_,
                            start=(i == 0), stop=(i == KD - 1),
                            skip_group_check=True),
                            reads=[xk(i), "onesb"], writes=["ps_sq"])
                        em.op("pe", lambda e, sq=sq, i=i: e.matmul(
                            ps_sk[:, :], lhsT=onesb[:, :], rhs=sq,
                            start=(i == 0), stop=(i == KD - 1),
                            skip_group_check=True),
                            reads=[f"sqs{sl}", "onesb"], writes=["ps_sk"])
                    pieces.append(pst)

                def prows():
                    mr, vr = rows[:, 0:TT], rows[:, TT:2 * TT]
                    tr = rows[:, 2 * TT:3 * TT]
                    rr = mr
                    em.op("dve", lambda e: e.tensor_scalar_mul(
                        out=mr, in0=ps_sq[:, :], scalar1=1.0 / D),
                        reads=["ps_sq"], writes=["row_m"])
                    em.op("dve", lambda e: e.tensor_scalar_mul(
                        out=vr, in0=ps_sk[:, :], scalar1=1.0 / D),
                        reads=["ps_sk"], writes=["row_v"])
                    em.op("dve", lambda e: e.tensor_mul(out=tr, in0=mr, in1=mr),
                          reads=["row_m"], writes=["row_t"])
                    em.op("dve", lambda e: e.tensor_sub(out=vr, in0=vr, in1=tr),
                          reads=["row_v", "row_t"], writes=["row_v"])
                    em.op("dve", lambda e: e.tensor_scalar_add(
                        out=vr, in0=vr, scalar1=LN_EPS),
                        reads=["row_v"], writes=["row_v"])
                    em.op("dve", lambda e: e.reciprocal(out=tr, in_=vr),
                          reads=["row_v"], writes=["row_t"])
                    em.op("pe", lambda e: e.matmul(
                        ps_bc[:, :], lhsT=onesrow[:, :], rhs=mr,
                        start=True, stop=True),
                        reads=["row_m", "onesrow"], writes=["ps_bc"])
                    em.op("dve", lambda e: e.tensor_copy(
                        out=m_b[:, :], in_=ps_bc[:, :]),
                        reads=["ps_bc"], writes=["m_b"])
                    em.op("act", lambda e: e.activation(
                        out=rr, in_=tr, func=AF.Sqrt),
                        reads=["row_t"], writes=["row_m"])
                    em.op("pe", lambda e: e.matmul(
                        ps_bc[:, :], lhsT=onesrow[:, :], rhs=rr,
                        start=True, stop=True),
                        reads=["row_m", "onesrow"], writes=["ps_bc"])
                    em.op("dve", lambda e: e.tensor_copy(
                        out=r_b[:, :], in_=ps_bc[:, :]),
                        reads=["ps_bc"], writes=["r_b"])
                pieces.append(prows)
                for i in range(KD):
                    def papp(i=i):
                        xb_ = x1bf[:, xsl + i * TT:xsl + (i + 1) * TT]
                        em.op("dve", lambda e, xb_=xb_: e.tensor_sub(
                            out=uu[:, :], in0=xb_, in1=m_b[:, :]),
                            reads=[xk(i), "m_b"], writes=["uu"])
                        em.op("dve", lambda e, i=i: e.tensor_mul(
                            out=xn2[:, xsl + i * TT:xsl + (i + 1) * TT],
                            in0=uu[:, :], in1=r_b[:, :]),
                            reads=["uu", "r_b"], writes=[f"xn2{t % 2}_{i}"])
                    pieces.append(papp)
                return pieces

            def qkv_mm(t, m):
                xsl = (t % 2) * KD * TT
                sl = m % 4
                wq = wqb[:, sl * KD * 128:(sl + 1) * KD * 128]
                wqk = f"wqb{sl}"
                if t == 0:
                    em.dma("pool", wq, wqkv_in[m, :, :], writes=[wqk])
                    em.dma("pool", scr_qkv[m, :, :], wq, reads=[wqk],
                           writes=[f"scrqkv{m}"])
                else:
                    em.dma("sp", wq, scr_qkv[m, :, :], reads=[f"scrqkv{m}"],
                           writes=[wqk])
                ps = ps_qk[m % 2]
                psk = f"ps_qk{m % 2}"
                for k in range(KD):
                    em.op("pe", lambda e, ps=ps, wq=wq, k=k, xsl=xsl: e.matmul(
                        ps[:, :], lhsT=wq[:, k * 128:(k + 1) * 128],
                        rhs=xn2[:, xsl + k * TT:xsl + (k + 1) * TT],
                        start=(k == 0), stop=(k == KD - 1)),
                        reads=[wqk, f"xn2{t % 2}_{k}"], writes=[psk])
                return ps, psk, m % 2

            def qk_section(t):
                has_q = True
                for m in range(2 * KD):
                    ps, psk, sl = qkv_mm(t, m)
                    if m < KD:
                        if not has_q:
                            continue
                        em.op("act", lambda e, ps=ps, m=m: e.copy(
                            out=qf[:, m * TT:(m + 1) * TT], in_=ps[:, :]),
                            reads=[psk], writes=[f"qf{m}"])
                        sq = sqs[:, sl * TT:(sl + 1) * TT]
                        em.op("act", lambda e, ps=ps, sq=sq: e.activation(
                            out=sq, in_=ps[:, :], func=AF.Square),
                            reads=[psk], writes=[f"sqs{sl}"])
                        em.op("pe", lambda e, sq=sq, m=m: e.matmul(
                            ps_sq[:, :], lhsT=onesb[:, :], rhs=sq,
                            start=(m == 0), stop=(m == KD - 1),
                            skip_group_check=True),
                            reads=[f"sqs{sl}", "onesb"], writes=["ps_sq"])
                    else:
                        mk = m - KD
                        em.op("act", lambda e, ps=ps, mk=mk: e.copy(
                            out=kf[:, mk * TT:(mk + 1) * TT], in_=ps[:, :]),
                            reads=[psk], writes=[f"kf{mk}"])
                        sq = sqs[:, sl * TT:(sl + 1) * TT]
                        em.op("act", lambda e, ps=ps, sq=sq: e.activation(
                            out=sq, in_=ps[:, :], func=AF.Square),
                            reads=[psk], writes=[f"sqs{sl}"])
                        em.op("pe", lambda e, sq=sq, mk=mk: e.matmul(
                            ps_sk[:, :], lhsT=onesb[:, :], rhs=sq,
                            start=(mk == 0), stop=(mk == KD - 1),
                            skip_group_check=True),
                            reads=[f"sqs{sl}", "onesb"], writes=["ps_sk"])
                qrr, krr = rows[:, 3 * TT:4 * TT], rows[:, 4 * TT:5 * TT]
                t2r = rows[:, 5 * TT:6 * TT]

                def rms_row(out_r, ps_src, ps_key, okey):
                    em.op("dve", lambda e, out_r=out_r, ps_src=ps_src:
                          e.tensor_scalar_mul(out=t2r, in0=ps_src[:, :],
                                              scalar1=1.0 / D),
                          reads=[ps_key], writes=["row_t2"])
                    em.op("dve", lambda e: e.tensor_scalar_add(
                        out=t2r, in0=t2r, scalar1=RMS_EPS),
                        reads=["row_t2"], writes=["row_t2"])
                    em.op("dve", lambda e, out_r=out_r: e.reciprocal(
                        out=out_r, in_=t2r),
                        reads=["row_t2"], writes=[okey])
                    em.op("act", lambda e, out_r=out_r: e.activation(
                        out=out_r, in_=out_r, func=AF.Sqrt),
                        reads=[okey], writes=[okey])

                tsl = (t % 2) * TT
                if has_q:
                    rms_row(qrr, ps_sq, "ps_sq", "row_q")
                    em.op("pe", lambda e: e.matmul(
                        ps_bc[:, :], lhsT=onesrow[:, :], rhs=qrr,
                        start=True, stop=True),
                        reads=["row_q", "onesrow"], writes=["ps_bc"])
                    em.op("dve", lambda e: e.tensor_copy(
                        out=qr_b[:, :], in_=ps_bc[:, :]),
                        reads=["ps_bc"], writes=["qr_b"])
                    em.op("dve", lambda e, tsl=tsl: e.tensor_mul(
                        out=cq2[:, :], in0=cqt[:, tsl:tsl + TT], in1=qr_b[:, :]),
                        reads=[f"cqt{t % 2}", "qr_b"], writes=["cq2"])
                    em.op("dve", lambda e, tsl=tsl: e.tensor_mul(
                        out=sq2[:, :], in0=sqt[:, tsl:tsl + TT], in1=qr_b[:, :]),
                        reads=[f"sqt{t % 2}", "qr_b"], writes=["sq2"])
                rms_row(krr, ps_sk, "ps_sk", "row_k")
                em.op("pe", lambda e: e.matmul(
                    ps_bc[:, :], lhsT=onesrow[:, :], rhs=krr,
                    start=True, stop=True),
                    reads=["row_k", "onesrow"], writes=["ps_bc"])
                em.op("dve", lambda e: e.tensor_copy(
                    out=kr_b[:, :], in_=ps_bc[:, :]),
                    reads=["ps_bc"], writes=["kr_b"])
                em.op("dve", lambda e, tsl=tsl: e.tensor_mul(
                    out=ck2[:, :], in0=ckt[:, tsl:tsl + TT], in1=kr_b[:, :]),
                    reads=[f"ckt{t % 2}", "kr_b"], writes=["ck2"])
                em.op("dve", lambda e, tsl=tsl: e.tensor_mul(
                    out=sk2[:, :], in0=skt[:, tsl:tsl + TT], in1=kr_b[:, :]),
                    reads=[f"skt{t % 2}", "kr_b"], writes=["sk2"])

            def rope_out(t, src_f, src_key, w_ap, w_key,
                         ct, ct_key, st_, st_key, dst, dst_key, sl):
                em.op("dve", lambda e: e.tensor_scalar_mul(
                    out=ub[:, :], in0=src_f, scalar1=w_ap),
                    reads=[src_key, w_key], writes=["ub"])
                em.op("pe", lambda e: e.matmul(
                    ps_sw[:, :], lhsT=pswapb[:, :], rhs=ub[:, :],
                    start=True, stop=True),
                    reads=["ub", "pswapb"], writes=["ps_sw"])
                em.op("act", lambda e: e.copy(out=swf[:, :], in_=ps_sw[:, :]),
                      reads=["ps_sw"], writes=["swf"])
                em.op("dve", lambda e, ct=ct: e.tensor_mul(
                    out=aa[:, :], in0=ub[:, :], in1=ct),
                    reads=["ub", ct_key], writes=["aa"])
                em.op("dve", lambda e, st_=st_: e.tensor_mul(
                    out=bb2[:, :], in0=swf[:, :], in1=st_),
                    reads=["swf", st_key], writes=["bb2"])
                qk = qkb[:, sl * TT:(sl + 1) * TT]
                em.op("dve", lambda e, qk=qk: e.tensor_add(
                    out=qk, in0=aa[:, :], in1=bb2[:, :]),
                    reads=["aa", "bb2"], writes=[f"qkb{sl}"])
                em.dma("pool", dst, qk, reads=[f"qkb{sl}"], writes=[dst_key])

            def a2_tail_pieces(t):
                c0 = t * TT
                has_q = True
                tsl = (t % 2) * TT
                rope_pieces = []
                if has_q:
                    for h in range(NH):
                        def pq(h=h, c0=c0):
                            rope_out(t, qf[:, h * TT:(h + 1) * TT], f"qf{h}",
                                     qnw_s[:, h:h + 1], "qnw_s",
                                     cq2[:, :], "cq2", sq2[:, :], "sq2",
                                     q_scr[:, h, c0:c0 + TT],
                                     "q_scr", h % 2)
                        rope_pieces.append(pq)
                for h in range(NH):
                    def pk_(h=h, c0=c0):
                        rope_out(t, kf[:, h * TT:(h + 1) * TT], f"kf{h}",
                                 knw_s[:, h:h + 1], "knw_s",
                                 ck2[:, :], "ck2", sk2[:, :], "sk2",
                                 k_scr[h, :, WINDOW + c0:WINDOW + c0 + TT],
                                 "k_scr", h % 2)
                    rope_pieces.append(pk_)
                pieces = []
                nrp = len(rope_pieces)
                for mv in range(NH):
                    def pv_(mv=mv, c0=c0, j0=mv * nrp // NH,
                            j1=(mv + 1) * nrp // NH):
                        ps, psk, sl = qkv_mm(t, 2 * KD + mv)
                        em.op("act", lambda e, ps=ps: e.copy(
                            out=vbs[:, :], in_=ps[:, :]),
                            reads=[psk], writes=["vbs"])
                        vsl = (mv % 2) * 4 * 128
                        for s4 in range(TT // 128):
                            ptr = ps_tr[s4 % 2]
                            ptrk = f"ps_tr{s4 % 2}"
                            vt = vts[:, vsl + s4 * 128:vsl + (s4 + 1) * 128]
                            em.op("pe", lambda e, ptr=ptr, s4=s4: e.matmul(
                                ptr[:, :], lhsT=vbs[:, s4 * 128:(s4 + 1) * 128],
                                rhs=identb[:, :], start=True, stop=True),
                                reads=["vbs", "identb"], writes=[ptrk])
                            em.op("act", lambda e, ptr=ptr, vt=vt: e.copy(
                                out=vt, in_=ptr[:, :]),
                                reads=[ptrk], writes=[f"vts{mv % 2}"])
                        em.dma("sp",
                               v_scr[mv, :, WINDOW // 128 + c0 // 128:
                                     WINDOW // 128 + c0 // 128 + 4, :],
                               vts[:, vsl:vsl + 4 * 128].rearrange(
                                   "p (kb c) -> p kb c", kb=4),
                               reads=[f"vts{mv % 2}"], writes=["v_scr"])
                        for j in range(j0, j1):
                            rope_pieces[j]()
                    pieces.append(pv_)
                return pieces

            prev_tail = None
            for t in range(n_own_tiles):
                a2_loads(t)
                pro = a2_prologue_pieces(t)
                if prev_tail is None:
                    for p in pro:
                        p()
                else:
                    nd = len(prev_tail)
                    npr = len(pro)
                    j = 0
                    for i, pd in enumerate(prev_tail):
                        pd()
                        jn = min(npr, (i + 1) * 2 * npr // nd)
                        while j < jn:
                            pro[j]()
                            j += 1
                qk_section(t)
                prev_tail = a2_tail_pieces(t)
            for p in prev_tail:
                p()

            em.barrier()

        with ExitStack() as ph:
            psb = lambda n, s, d: ph.enter_context(nc.sbuf_tensor("b" + n, s, d))
            KW = WINDOW + TT
            NKC = KW // 128
            qsb = psb("qsb", [128, NH * TT], BF16)
            ksb = psb("ksb", [128, 4 * KW], BF16)
            vsb = psb("vsb", [128, 4 * NKC * 128], BF16)
            pf32 = psb("pf32", [128, 2 * 1152], F32)
            pb = psb("pb", [128, 2 * 1152], BF16)
            ptb = psb("ptb", [128, 2 * 1152], BF16)
            nmax = psb("nmax", [128, 2], F32)
            rows_nm = psb("rows_nm", [128, 6], F32)
            ssum = psb("ssum", [128, 2], F32)
            rsum = psb("rsum", [128, 2], F32)
            attn = psb("attn", [128, NH * TT], BF16)
            attn_tm = psb("attn_tm", [128, NH * 4 * 128], BF16)
            wosb = psb("wosb", [128, KD * KD * 128], BF16)
            x1c = psb("x1c", [128, 4 * TT], F32)
            x2c = psb("x2c", [128, 2 * TT], F32)
            ps_scs = [ph.enter_context(nc.psum_tensor(f"bps_sc{i}", [128, 1152],
                                                       F32)) for i in range(2)]
            ps_pv = ph.enter_context(nc.psum_tensor("bps_pv", [128, TT], F32))
            ps_wo = [ph.enter_context(nc.psum_tensor("bps_wo", [128, TT], F32))]

            idxt = ph.enter_context(nc.sbuf_tensor("bidxt", [1, 1],
                                                   mybir.dt.int32))
            em.dma("sp", ktail[:, :, :],
                   k_scr[:, :, t_all - WINDOW:t_all], writes=["ktail"])
            em.dma("sp", vtail[:, :, :, :],
                   v_scr[:, :, (t_all - WINDOW) // 128:t_all // 128, :],
                   writes=["vtail"])
            if ways > 1:
                rg = [list(range(ways))]
                em.collective(
                    lambda e: e.collective_compute(
                        "AllGather", ALU.bypass, replica_groups=rg,
                        ins=[ktail.rearrange("a b c -> (a b) c").opt()],
                        outs=[kgat.rearrange("r a b c -> (r a b) c").opt()]),
                    reads=["ktail"], writes=["kgat"])
                em.collective(
                    lambda e: e.collective_compute(
                        "AllGather", ALU.bypass, replica_groups=rg,
                        ins=[vtail.rearrange("a b c d -> (a b c) d").opt()],
                        outs=[vgat.rearrange("r a b c d -> (r a b c) d").opt()]),
                    reads=["vtail"], writes=["vgat"])
            else:
                em.dma("sp", kgat[0], ktail[:, :, :],
                       reads=["ktail"], writes=["kgat"])
                em.dma("sp", vgat[0], vtail[:, :, :, :],
                       reads=["vtail"], writes=["vgat"])
            em.dma("sp", idxt[:, :], prevrank[:, :], writes=["idxt"])

            kstride = NH * 128 * WINDOW
            vstride = NH * 128 * (WINDOW // 128) * 128

            def dyn_fetch(e, gat_h, stride, dst_ap, ndims_ap):
                with e.register() as reg:
                    e.reg_load(reg, idxt[:1, :1])
                    with e.register() as ofs:
                        e.reg_mul(ofs, reg, stride)
                        src_dyn = bass.AP(gat_h, ofs, ndims_ap)
                        return e.dma_start(out=dst_ap, in_=src_dyn)

            k_ap = [[128 * WINDOW, NH], [WINDOW, 128], [1, WINDOW]]
            v_ap = [[128 * WINDOW, NH], [WINDOW, 128], [1, WINDOW]]

            def bk(e):
                return dyn_fetch(e, kgat_h, kstride,
                                 k_scr[:, :, 0:WINDOW], k_ap)

            def bv(e):
                return dyn_fetch(e, vgat_h, vstride,
                                 v_scr[:, :, 0:WINDOW // 128, :].rearrange(
                                     "a b c d -> a b (c d)"), v_ap)

            em.dma_custom("pool", bk, reads=["kgat", "idxt"], writes=["k_scr_h"])
            em.dma_custom("pool", bv, reads=["vgat", "idxt"], writes=["v_scr_h"])

            for m in range(KD):
                em.dma("pool", wosb[:, m * KD * 128:(m + 1) * KD * 128],
                       wo_in[m, :, :], writes=[f"wosb{m}"])

            qt_order = [qt for qt in range(n_own_tiles) if qt * TT >= WINDOW]
            qt_order += [qt for qt in range(n_own_tiles) if qt * TT < WINDOW]
            wo_pieces = []
            for qt in qt_order:
                kw0 = qt * TT
                em.dma("sp", qsb.ap().rearrange("p (h c) -> p h c", h=NH),
                       q_scr[:, :, qt * TT:(qt + 1) * TT],
                       reads=[],
                       writes=[f"qsb{h}" for h in range(NH)])
                prev_tail = None
                for h in range(NH):
                    sl = h % 2
                    if h % 2 == 0:
                        psl2 = (h // 2) % 2
                        em.dma("sp",
                               ksb[:, psl2 * 2 * KW:(psl2 * 2 + 2) * KW]
                               .rearrange("p (hh c) -> p hh c", hh=2),
                               k_scr[h:h + 2, :, kw0:kw0 + KW].rearrange(
                                   "hh p c -> p hh c"),
                               reads=(["k_scr_h"] if kw0 < WINDOW else []),
                               writes=[f"ksb{psl2 * 2}", f"ksb{psl2 * 2 + 1}"])
                        em.dma("sp",
                               vsb[:, psl2 * 2 * NKC * 128:
                                   (psl2 * 2 + 2) * NKC * 128].rearrange(
                                   "p (hh kb c) -> p hh kb c", hh=2, kb=NKC),
                               v_scr[h:h + 2, :, kw0 // 128:kw0 // 128 + NKC,
                                     :].rearrange("hh p kb c -> p hh kb c"),
                               reads=(["v_scr_h"] if kw0 < WINDOW else []),
                               writes=[f"vsb{psl2 * 2}", f"vsb{psl2 * 2 + 1}"])
                    slot = (h // 2) % 2 * 2 + h % 2
                    kwb = ksb[:, slot * KW:(slot + 1) * KW]
                    vwb = vsb[:, slot * NKC * 128:(slot + 1) * NKC * 128]
                    kwk, vwk = f"ksb{slot}", f"vsb{slot}"
                    for qs in range(TT // 128):
                        psl = qs % 2
                        ps_sc = ps_scs[psl]
                        sck = f"ps_sc{psl}"
                        pbx = pb[:, psl * 1152:(psl + 1) * 1152]
                        pfx = pf32[:, psl * 1152:(psl + 1) * 1152]
                        nmx = nmax[:, psl:psl + 1]
                        ssx = ssum[:, psl:psl + 1]
                        rsx = rsum[:, psl:psl + 1]
                        qlhs = qsb[:, h * TT + qs * 128: h * TT + qs * 128 + 128]
                        kof = qs * 128
                        for o, n in ((0, 512), (512, 512), (1024, 128)):
                            em.op("pe", lambda e, o=o, n=n, kwb=kwb, qlhs=qlhs,
                                  kof=kof, ps_sc=ps_sc: e.matmul(
                                      ps_sc[:, o:o + n], lhsT=qlhs,
                                      rhs=kwb[:, kof + o:kof + o + n],
                                      start=True, stop=True),
                                  reads=[f"qsb{h}", kwk], writes=[sck])
                        em.op("pe", lambda e, ps_sc=ps_sc: e.matmul(
                            ps_sc[:, 0:128], lhsT=identb[:, :], rhs=maskab[:, :],
                            start=False, stop=True, skip_group_check=True),
                            reads=["maskab", "identb"], writes=[sck])
                        em.op("pe", lambda e, ps_sc=ps_sc: e.matmul(
                            ps_sc[:, 1024:1152], lhsT=identb[:, :], rhs=maskcb[:, :],
                            start=False, stop=True, skip_group_check=True),
                            reads=["maskcb", "identb"], writes=[sck])
                        if kw0 < WINDOW:
                            hb0 = kw0 + kof
                            for o, n in ((0, 512), (512, 512), (1024, 128)):
                                em.op("pe", lambda e, o=o, n=n, hb0=hb0,
                                      ps_sc=ps_sc: e.matmul(
                                          ps_sc[:, o:o + n], lhsT=onesrb[:, :],
                                          rhs=hbiasb[:, hb0 + o:hb0 + o + n],
                                          start=False, stop=True,
                                          skip_group_check=True),
                                      reads=["hbiasb", "onesrb"], writes=[sck])
                        em.op("dve", lambda e, nmx=nmx, ps_sc=ps_sc: e.reduce_max(
                            out=nmx, in_=ps_sc[:, 0:1152], axis=AX.X,
                            negate=True),
                            reads=[sck], writes=[f"nmax{psl}"])
                        em.op("act", lambda e, pbx=pbx, nmx=nmx, ssx=ssx,
                              ps_sc=ps_sc: e.activation(
                                  out=pbx, in_=ps_sc[:, 0:1152], func=AF.Exp,
                                  bias=nmx, accum_out=ssx),
                              reads=[sck, f"nmax{psl}"],
                              writes=[f"pb{psl}", f"ssum{psl}"])
                        em.op("dve", lambda e, rsx=rsx, ssx=ssx: e.reciprocal(
                            out=rsx, in_=ssx),
                            reads=[f"ssum{psl}"], writes=[f"rsum{psl}"])

                        def tail(h=h, qs=qs, psl=psl, pbx=pbx, vwb=vwb, vwk=vwk,
                                 ps_sc=ps_sc, sck=sck, rsx=rsx):
                            ptx = ptb[:, psl * 1152:(psl + 1) * 1152]
                            for c in range(9):
                                em.op("pe", lambda e, c=c, pbx=pbx, ps_sc=ps_sc:
                                      e.matmul(
                                          ps_sc[:, c * 128:(c + 1) * 128],
                                          lhsT=pbx[:, c * 128:(c + 1) * 128],
                                          rhs=identb[:, :], start=True, stop=True),
                                      reads=[f"pb{psl}", "identb"], writes=[sck])
                            em.op("act", lambda e, ptx=ptx, ps_sc=ps_sc: e.copy(
                                out=ptx, in_=ps_sc[:, 0:1152]),
                                reads=[sck], writes=[f"ptb{psl}"])
                            for c in range(9):
                                em.op("pe", lambda e, c=c, vwb=vwb, qs=qs,
                                      ptx=ptx: e.matmul(
                                          ps_pv[:, qs * 128:(qs + 1) * 128],
                                          lhsT=ptx[:, c * 128:(c + 1) * 128],
                                          rhs=vwb[:, (qs + c) * 128:
                                                   (qs + c + 1) * 128],
                                          start=(c == 0), stop=(c == 8),
                                          skip_group_check=True),
                                      reads=[vwk, f"ptb{psl}"], writes=["ps_pv"])
                            em.op("dve", lambda e, h=h, qs=qs, rsx=rsx:
                                  e.tensor_scalar_mul(
                                      out=attn_tm[:, (h * 4 + qs) * 128:
                                                  (h * 4 + qs + 1) * 128],
                                      in0=ps_pv[:, qs * 128:(qs + 1) * 128],
                                      scalar1=rsx),
                                  reads=["ps_pv", f"rsum{psl}"],
                                  writes=[f"attm{h}_{qs}"])

                        if prev_tail is not None:
                            prev_tail()
                        if wo_pieces:
                            wo_pieces.pop(0)()
                        prev_tail = tail
                if prev_tail is not None:
                    prev_tail()
                while wo_pieces:
                    wo_pieces.pop(0)()
                for h in range(NH):
                    reg = ps_scs[h % 2]
                    rk = f"ps_sc{h % 2}"
                    for qs in range(4):
                        em.op("pe", lambda e, h=h, qs=qs, reg=reg: e.matmul(
                            reg[:, qs * 128:(qs + 1) * 128],
                            lhsT=attn_tm[:, (h * 4 + qs) * 128:
                                         (h * 4 + qs + 1) * 128],
                            rhs=identb[:, :], start=True, stop=True),
                            reads=[f"attm{h}_{qs}", "identb"], writes=[rk])
                    em.op("act", lambda e, h=h, reg=reg: e.copy(
                        out=attn[:, h * TT:(h + 1) * TT], in_=reg[:, 0:512]),
                        reads=[rk], writes=[f"attn{h}"])
                def wo_piece(m, qt=qt):
                    sl = m % 2
                    if m % 4 == 0:
                        em.dma("sp",
                               x1c.ap().rearrange("p (blk c) -> p blk c", blk=4),
                               x1_scr[m * 128:(m + 4) * 128,
                                      qt * TT:(qt + 1) * TT]
                               .rearrange("(blk p) c -> p blk c", p=128),
                               reads=["x1_scr"],
                               writes=[f"x1c{j}" for j in range(4)])
                    psw = ps_wo[0]
                    pswk = "ps_wo0"
                    for k in range(KD):
                        em.op("pe", lambda e, psw=psw, m=m, k=k: e.matmul(
                            psw[:, :],
                            lhsT=wosb[:, (m * KD + k) * 128:(m * KD + k + 1) * 128],
                            rhs=attn[:, k * TT:(k + 1) * TT],
                            start=(k == 0), stop=(k == KD - 1)),
                            reads=[f"wosb{m}", f"attn{k}"], writes=[pswk])
                    xc1 = x1c[:, (m % 4) * TT:(m % 4 + 1) * TT]
                    xc2 = x2c[:, sl * TT:(sl + 1) * TT]
                    em.op("dve", lambda e, psw=psw, xc1=xc1, xc2=xc2: e.tensor_add(
                        out=xc2, in0=psw[:, :], in1=xc1),
                        reads=[pswk, f"x1c{m % 4}"], writes=[f"x2c{sl}"])
                    em.dma("pool", x2_scr[m * 128:(m + 1) * 128,
                                          qt * TT:(qt + 1) * TT], xc2,
                           reads=[f"x2c{sl}"], writes=["x2_scr"])

                wo_pieces = [lambda m=m: wo_piece(m) for m in range(KD)]
            while wo_pieces:
                wo_pieces.pop(0)()
            em.barrier()

        ffn_pass("c", x2_scr, "x2_scr", o_out, "o_out", n_own_tiles)

        em.barrier()
        with nc.Block() as block:
            em.finalize(block)

    return nc


def _sylvester(n):
    h = np.array([[1.0]], dtype=np.float32)
    while h.shape[0] < n:
        h = np.block([[h, h], [h, -h]])
    return h.astype(np.float32)


def _rope_tables(pos, scale):
    inv = 1.0 / (ROPE_THETA ** (np.arange(0, HD, 2, dtype=np.float32) / HD))
    f = pos[None, :].astype(np.float32) * inv[:, None]
    c = np.cos(f) * scale
    s = np.sin(f) * scale
    return (np.concatenate([c, c], axis=0).astype(np.float32),
            np.concatenate([s, s], axis=0).astype(np.float32))


def make_core_inputs(x_full, weights, t_own=2048, cores=None, ways=None):
    B, S, _ = x_full.shape
    n_chunks = S // t_own
    t_all = t_own + WINDOW

    w_qkv = np.asarray(weights["w_qkv"], np.float32)
    q_norm_w = np.asarray(weights["q_norm_w"], np.float32)
    k_norm_w = np.asarray(weights["k_norm_w"], np.float32)
    w_o = np.asarray(weights["w_o"], np.float32)
    SU_ug = np.asarray(weights["SU_ug"], np.float32)
    SU_dp = np.asarray(weights["SU_dp"], np.float32)
    Wsc_ug = np.asarray(weights["Wscale_ug"], np.float32)
    Wsc_dp = np.asarray(weights["Wscale_dp"], np.float32)
    W_up = np.asarray(weights["W_up"], np.float32)
    W_gate = np.asarray(weights["W_gate"], np.float32)
    W_down = np.asarray(weights["W_down"], np.float32)

    com = {}
    wup_full = W_up.T.reshape(KD, 128, INTER)
    wgt_full = W_gate.T.reshape(KD, 128, INTER)
    com["wqkv_in"] = np.ascontiguousarray(
        w_qkv.reshape(3 * KD, 128, KD, 128).transpose(0, 3, 2, 1)
        .reshape(3 * KD, 128, KD * 128))
    com["wo_in"] = np.ascontiguousarray(
        w_o.reshape(KD, 128, KD, 128).transpose(0, 3, 2, 1).reshape(KD, 128, KD * 128))
    wdn_full = W_down.T.reshape(KI, 128, D)
    com["su_ug"] = np.ascontiguousarray(SU_ug.reshape(KD, 128).T)
    com["su_dp"] = np.ascontiguousarray(SU_dp.reshape(KI, 128).T)
    com["wsc_u"] = np.ascontiguousarray(Wsc_ug[:INTER].reshape(KI, 128).T)
    com["wsc_g"] = np.ascontiguousarray(Wsc_ug[INTER:].reshape(KI, 128).T)
    com["wsc_dp"] = np.ascontiguousarray(Wsc_dp.reshape(KD, 128).T)
    com["qnw"] = np.ascontiguousarray(q_norm_w.reshape(KD, 128).T)
    com["knw"] = np.ascontiguousarray(k_norm_w.reshape(KD, 128).T)
    com["h128"] = _sylvester(128)
    com["ident"] = np.eye(128, dtype=np.float32)
    psw = np.zeros((128, 128), np.float32)
    for mcol in range(64):
        psw[mcol + 64, mcol] = -1.0
        psw[mcol, mcol + 64] = 1.0
    com["pswap"] = psw
    p_i = np.arange(128)[:, None]
    k_i = np.arange(128)[None, :]
    com["mask_c"] = np.where(p_i >= k_i, 0.0, NEG).astype(np.float32)
    com["mask_a"] = np.where(p_i < k_i, 0.0, NEG).astype(np.float32)

    if cores is None:
        cores = [(b, ch) for b in range(B) for ch in range(n_chunks)]
    if ways is None:
        ways = len(cores)
    in_maps = []
    for ci, (b, ch) in enumerate(cores):
        d = dict(com)
        mu = INTER // ways
        md = D // ways
        d["wup_in"] = np.ascontiguousarray(wup_full[:, :, ci * mu:(ci + 1) * mu])
        d["wgt_in"] = np.ascontiguousarray(wgt_full[:, :, ci * mu:(ci + 1) * mu])
        d["wdn_in"] = np.ascontiguousarray(wdn_full[:, :, ci * md:(ci + 1) * md])
        own = x_full[b, ch * t_own:(ch + 1) * t_own, :]
        d["x_in"] = np.ascontiguousarray(own.T)
        g0 = ch * t_own
        own_pos = np.arange(g0, g0 + t_own)
        cq, sq = _rope_tables(own_pos, 1.0 / np.sqrt(HD))
        ck, sk = _rope_tables(own_pos, 1.0)
        d["cosq"], d["sinq"] = cq, sq
        d["cosk"], d["sink"] = ck, sk
        hb = np.zeros((1, t_all), np.float32)
        n_inv = max(0, min(WINDOW, WINDOW - g0))
        if n_inv > 0:
            hb[0, :n_inv] = NEG
        d["hbias"] = hb
        d["prevrank"] = np.array([[(ci - 1) % max(len(cores), 1)]], np.int32)
        in_maps.append(d)
    return in_maps


def kernel(L0_attn_hidden_states, L0_attn_w_qkv, L0_attn_q_norm_w,
           L0_attn_k_norm_w, L0_attn_w_o, L0_ffn_SU_ug, L0_ffn_SU_dp,
           L0_ffn_Wscale_ug, L0_ffn_Wscale_dp, L0_ffn_W_up, L0_ffn_W_gate,
           L0_ffn_W_down):
    x = np.asarray(L0_attn_hidden_states, np.float32)
    B, S, _ = x.shape
    t_own = S // 4
    weights = {
        "w_qkv": L0_attn_w_qkv, "q_norm_w": L0_attn_q_norm_w,
        "k_norm_w": L0_attn_k_norm_w, "w_o": L0_attn_w_o,
        "SU_ug": L0_ffn_SU_ug, "SU_dp": L0_ffn_SU_dp,
        "Wscale_ug": L0_ffn_Wscale_ug, "Wscale_dp": L0_ffn_Wscale_dp,
        "W_up": L0_ffn_W_up, "W_gate": L0_ffn_W_gate, "W_down": L0_ffn_W_down,
    }
    in_maps = make_core_inputs(x, weights, t_own=t_own, ways=N_CORES)
    nc = build_program(t_own, ways=N_CORES)
    res = run_bass_kernel_spmd(nc, in_maps, core_ids=list(range(N_CORES)))
    out = np.empty((B, S, D), np.float32)
    ci = 0
    for b in range(B):
        for ch in range(4):
            out[b, ch * t_own:(ch + 1) * t_own, :] = \
                np.asarray(res.results[ci]["o_out"]).T
            ci += 1
    return out


# version 21
# speedup vs baseline: 1.0225x; 1.0029x over previous
import numpy as np
from contextlib import ExitStack

import concourse.bass as bass
import concourse.mybir as mybir
from concourse.bass_utils import run_bass_kernel_spmd

F32 = mybir.dt.float32
BF16 = mybir.dt.bfloat16
AF = mybir.ActivationFunctionType
ALU = mybir.AluOpType
AX = mybir.AxisListType

D = 2048
NH = 16
HD = 128
INTER = 8192
WINDOW = 1024
KD = D // 128
KI = INTER // 128
TT = 512
NEG = -1.0e30
LN_EPS = 1e-5
RMS_EPS = 1e-6
ROPE_THETA = 10000.0
N_CORES = 8


class Emitter:
    ENGS = ("pe", "act", "dve", "pool", "sp")
    NDMA = 8

    def __init__(self, nc, ctx):
        self.nc = nc
        self.ops = {e: [] for e in self.ENGS}
        self.sems = {}
        names = ["s_pe", "s_act", "s_dve", "s_pool", "s_sp"]
        names += [f"dh{i}" for i in range(self.NDMA)]
        names += [f"dg{i}" for i in range(self.NDMA)]
        names += ["ccs"]
        for name in names:
            self.sems[name] = ctx.enter_context(nc.semaphore(name))
        self.count = {name: 0 for name in self.sems}
        self.clock = {e: {name: 0 for name in self.sems} for e in self.ENGS}
        self.res = {}
        self.eng_sem = {"pe": "s_pe", "act": "s_act", "dve": "s_dve",
                        "pool": "s_pool", "sp": "s_sp"}
        self.dma_idx = {"sp": 0, "pool": 0}

    def _entry(self, key):
        e = self.res.get(key)
        if e is None:
            e = {"w": None, "r": []}
            self.res[key] = e
        return e

    def _emit(self, eng, builder, reads, writes, done_dep, extra_waits=()):
        waits = {}

        def need(dep):
            if dep is not None and waits.get(dep[0], 0) < dep[1]:
                waits[dep[0]] = dep[1]

        for dep in extra_waits:
            need(dep)
        for k in reads:
            need(self._entry(k)["w"])
        for k in writes:
            e = self._entry(k)
            need(e["w"])
            for rd in e["r"]:
                need(rd)
        ck = self.clock[eng]
        emitted = []
        for sem, val in waits.items():
            if eng == "pe" and sem == "s_pe":
                continue
            if ck[sem] < val:
                emitted.append((self.sems[sem], val))
                ck[sem] = val
        self.ops[eng].append((emitted, builder))
        for k in reads:
            self._entry(k)["r"].append(done_dep)
        for k in writes:
            e = self._entry(k)
            e["w"] = done_dep
            e["r"] = []

    def op(self, eng, builder, reads=(), writes=()):
        sem = self.eng_sem[eng]
        self.count[sem] += 1
        s, v = self.sems[sem], self.count[sem]

        def b(e, _b=builder, _s=s):
            _b(e).then_inc(_s, 1)

        self._emit(eng, b, reads, writes, (sem, self.count[sem]))

    def dma(self, queue, out, in_, reads=(), writes=()):
        pref = "dh" if queue == "sp" else "dg"
        dsem = f"{pref}{self.dma_idx[queue] % self.NDMA}"
        self.dma_idx[queue] += 1
        prev = self.count[dsem]
        self.count[dsem] = prev + 16
        s = self.sems[dsem]

        def b(e, _o=out, _i=in_, _s=s):
            e.dma_start(out=_o, in_=_i).then_inc(_s, 16)

        self._emit(queue, b, reads, writes, (dsem, prev + 16),
                   extra_waits=((dsem, prev),) if prev else ())

    def dma_custom(self, queue, builder_fn, reads=(), writes=()):
        pref = "dh" if queue == "sp" else "dg"
        dsem = f"{pref}{self.dma_idx[queue] % self.NDMA}"
        self.dma_idx[queue] += 1
        prev = self.count[dsem]
        self.count[dsem] = prev + 16
        s = self.sems[dsem]

        def b(e, _s=s):
            builder_fn(e).then_inc(_s, 16)

        self._emit(queue, b, reads, writes, (dsem, prev + 16),
                   extra_waits=((dsem, prev),) if prev else ())

    def collective(self, nc_fn, reads=(), writes=()):
        dsem = "ccs"
        if dsem not in self.sems:
            raise RuntimeError("ccs semaphore not allocated")
        self.count[dsem] += 1
        s, v = self.sems[dsem], self.count[dsem]

        def b(e, _s=s):
            nc_fn(e).then_inc(_s)

        self._emit("pool", b, reads, writes, (dsem, v))

    def barrier(self):
        snap = dict(self.count)
        for eng in self.ENGS:
            ck = self.clock[eng]
            emitted = []
            for sem, val in snap.items():
                if sem != self.eng_sem[eng] and ck[sem] < val:
                    emitted.append((self.sems[sem], val))
                    ck[sem] = val
            if emitted:
                self.ops[eng].append((emitted, None))
        self.res = {}

    def finalize(self, block):
        def mk(eng_name):
            ops = self.ops[eng_name]

            def body(e):
                for waits, builder in ops:
                    for s, v in waits:
                        e.wait_ge(s, v)
                    if builder is not None:
                        builder(e)
            return body
        block.tensor(mk("pe"))
        block.scalar(mk("act"))
        block.vector(mk("dve"))
        block.gpsimd(mk("pool"))
        block.sync(mk("sp"))


def build_program(t_own, ways=N_CORES):
    t_all = t_own + WINDOW
    n_tiles = t_all // TT
    n_own_tiles = t_own // TT
    own0 = WINDOW

    nc = bass.Bass("TRN2", target_bir_lowering=False)

    def din(name, shape, dt=F32):
        return nc.dram_tensor(name, shape, dt, kind="ExternalInput").ap()

    x_in = din("x_in", [D, t_own])
    wup_in = din("wup_in", [KD, 128, INTER // ways])
    wgt_in = din("wgt_in", [KD, 128, INTER // ways])
    wqkv_in = din("wqkv_in", [3 * KD, 128, KD * 128])
    wo_in = din("wo_in", [KD, 128, KD * 128])
    wdn_in = din("wdn_in", [KI, 128, D // ways])
    su_ug = din("su_ug", [128, KD])
    su_dp = din("su_dp", [128, KI])
    wsc_u = din("wsc_u", [128, KI])
    wsc_g = din("wsc_g", [128, KI])
    wsc_dp = din("wsc_dp", [128, KD])
    qnw = din("qnw", [128, KD])
    knw = din("knw", [128, KD])
    cosq = din("cosq", [128, t_own])
    sinq = din("sinq", [128, t_own])
    cosk = din("cosk", [128, t_own])
    sink = din("sink", [128, t_own])
    prevrank = din("prevrank", [1, 1], mybir.dt.int32)
    hbias = din("hbias", [1, t_all])
    h128 = din("h128", [128, 128])
    ident = din("ident", [128, 128])
    pswap = din("pswap", [128, 128])
    mask_c = din("mask_c", [128, 128])
    mask_a = din("mask_a", [128, 128])

    o_out = nc.dram_tensor("o_out", [D, t_own], F32, kind="ExternalOutput").ap()

    def dscr(name, shape, dt):
        return nc.dram_tensor(name, shape, dt, kind="Internal").ap()

    scr_up = dscr("scr_up", [KI, 128, KD * 128], BF16)
    scr_gt = dscr("scr_gt", [KI, 128, KD * 128], BF16)
    if ways > 1:
        sup_my = dscr("sup_my", [KI // ways, 128, KD * 128], BF16)
        sgt_my = dscr("sgt_my", [KI // ways, 128, KD * 128], BF16)
        sdn_my = dscr("sdn_my", [KD // ways, 128, KI * 128], BF16)
    scr_qkv = dscr("scr_qkv", [3 * KD, 128, KD * 128], BF16)
    scr_dn = dscr("scr_dn", [KD, 128, KI * 128], BF16)
    x1_scr = dscr("x1_scr", [D, t_own], F32)
    ktail = dscr("ktail", [NH, 128, WINDOW], BF16)
    vtail = dscr("vtail", [NH, 128, WINDOW // 128, 128], BF16)
    kgat_h = nc.dram_tensor("kgat", [ways, NH, 128, WINDOW], BF16,
                            kind="Internal")
    vgat_h = nc.dram_tensor("vgat", [ways, NH, 128, WINDOW // 128, 128],
                            BF16, kind="Internal")
    kgat = kgat_h.ap()
    vgat = vgat_h.ap()
    x2_scr = dscr("x2_scr", [D, t_own], F32)
    q_scr = dscr("q_scr", [128, NH, t_own], BF16)
    k_scr = dscr("k_scr", [NH, 128, t_all], BF16)
    v_scr = dscr("v_scr", [NH, 128, t_all // 128, 128], BF16)

    with ExitStack() as top:
        em = Emitter(nc, top)
        tsb = lambda n, s, d: top.enter_context(nc.sbuf_tensor(n, s, d))

        h128b = tsb("h128b", [128, 128], BF16)
        identb = tsb("identb", [128, 128], BF16)
        pswapb = tsb("pswapb", [128, 128], BF16)
        maskcb = tsb("maskcb", [128, 128], BF16)
        maskab = tsb("maskab", [128, 128], BF16)
        hbiasb = tsb("hbiasb", [1, t_all], BF16)
        onesb = tsb("onesb", [128, 1], BF16)
        onesrow = tsb("onesrow", [1, 128], F32)
        onesrb = tsb("onesrb", [1, 128], BF16)
        su_ug_s = tsb("su_ug_s", [128, KD], F32)
        su_dp_s = tsb("su_dp_s", [128, KI], F32)
        wsc_u_s = tsb("wsc_u_s", [128, KI], F32)
        wsc_g_s = tsb("wsc_g_s", [128, KI], F32)
        wsc_dp_s = tsb("wsc_dp_s", [128, KD], F32)
        qnw_s = tsb("qnw_s", [128, KD], F32)
        knw_s = tsb("knw_s", [128, KD], F32)

        for buf, src, key in ((h128b, h128, "h128b"), (identb, ident, "identb"),
                              (pswapb, pswap, "pswapb"), (maskcb, mask_c, "maskcb"),
                              (maskab, mask_a, "maskab"), (hbiasb, hbias, "hbiasb")):
            em.dma("pool", buf[:, :], src[:, :], writes=[key])
        for buf, src, key in ((su_ug_s, su_ug, "su_ug_s"), (su_dp_s, su_dp, "su_dp_s"),
                              (wsc_u_s, wsc_u, "wsc_u_s"), (wsc_g_s, wsc_g, "wsc_g_s"),
                              (wsc_dp_s, wsc_dp, "wsc_dp_s"), (qnw_s, qnw, "qnw_s"),
                              (knw_s, knw, "knw_s")):
            em.dma("sp", buf[:, :], src[:, :], writes=[key])
        em.op("dve", lambda e: e.memset(onesb[:, :], 1.0), writes=["onesb"])
        zerot = tsb("zerot", [128, WINDOW], BF16)
        em.op("dve", lambda e: e.memset(zerot[:, :], 0.0), writes=["zerot"])
        for h_ in range(NH):
            em.dma("sp", k_scr[h_, :, 0:WINDOW], zerot[:, :],
                   reads=["zerot"], writes=["k_scr"])
            em.dma("sp",
                   v_scr[h_, :, 0:WINDOW // 128, :].rearrange(
                       "b c d -> b (c d)"),
                   zerot[:, :], reads=["zerot"], writes=["v_scr"])
        em.op("dve", lambda e: e.memset(onesrow[:, :], 1.0), writes=["onesrow"])
        em.op("dve", lambda e: e.memset(onesrb[:, :], 1.0), writes=["onesrb"])
        em.op("dve", lambda e: e.tensor_scalar_mul(
            out=su_ug_s[:, :], in0=su_ug_s[:, :], scalar1=float(1.0 / np.sqrt(D))),
            reads=["su_ug_s"], writes=["su_ug_s"])
        em.op("dve", lambda e: e.tensor_scalar_mul(
            out=su_dp_s[:, :], in0=su_dp_s[:, :], scalar1=float(1.0 / np.sqrt(INTER))),
            reads=["su_dp_s"], writes=["su_dp_s"])
        em.op("dve", lambda e: e.tensor_scalar_mul(
            out=wsc_dp_s[:, :], in0=wsc_dp_s[:, :], scalar1=0.5),
            reads=["wsc_dp_s"], writes=["wsc_dp_s"])
        comb_us = tsb("comb_us", [128, KI], F32)
        em.op("dve", lambda e: e.tensor_mul(
            out=comb_us[:, :], in0=wsc_u_s[:, :], in1=su_dp_s[:, :]),
            reads=["wsc_u_s", "su_dp_s"], writes=["comb_us"])

        with ExitStack() as ph:
            psb = lambda n, s, d: ph.enter_context(nc.sbuf_tensor(n, s, d))
            cb = psb("p0cb", [128, 64 * 256], BF16)
            pbuf = psb("p0pb", [128, 64 * 256], BF16)
            wst = [psb(f"p0wst{i}", [128, KI * 128], BF16) for i in range(4)]
            ps0 = [ph.enter_context(nc.psum_tensor(f"p0ps{i}", [128, 512], F32))
                   for i in range(2)]

            def transform(src_ap, nt, span, scale_ap, scale_key, scr, scr_pref,
                          m_total):
                nmc = span // 128
                for s in range(m_total // span):
                    for i in range(nt):
                        em.dma("pool", cb[:, i * span:(i + 1) * span],
                               src_ap[i, :, s * span:(s + 1) * span],
                               writes=[f"p0cb{i}"])
                    bufs = {"p0cb": cb, "p0pb": pbuf}
                    srck, dstk = "p0cb", "p0pb"
                    h = 1
                    while h < nt:
                        run = h * span
                        sv = bufs[srck][:, 0:nt * span].rearrange(
                            "p (n c) -> p n c", c=2 * run)
                        dv = bufs[dstk][:, 0:nt * span].rearrange(
                            "p (n c) -> p n c", c=2 * run)
                        rall = [f"{srck}{i}" for i in range(nt)]
                        em.op("dve", lambda e, dv=dv, sv=sv, run=run:
                              e.tensor_add(out=dv[:, :, 0:run],
                                           in0=sv[:, :, 0:run],
                                           in1=sv[:, :, run:2 * run]),
                              reads=rall,
                              writes=[f"{dstk}{i}" for i in range(nt)
                                      if (i % (2 * h)) < h])
                        em.op("dve", lambda e, dv=dv, sv=sv, run=run:
                              e.tensor_sub(out=dv[:, :, run:2 * run],
                                           in0=sv[:, :, 0:run],
                                           in1=sv[:, :, run:2 * run]),
                              reads=rall,
                              writes=[f"{dstk}{i}" for i in range(nt)
                                      if (i % (2 * h)) >= h])
                        srck, dstk = dstk, srck
                        h *= 2
                    for i in range(nt):
                        ps = ps0[i % 2]
                        pk = f"p0ps{i % 2}"
                        em.op("pe", lambda e, ps=ps, span=span,
                              r=bufs[srck][:, i * span:(i + 1) * span]:
                              e.matmul(ps[:, 0:span], lhsT=h128b[:, :], rhs=r,
                                       start=True, stop=True),
                              reads=[f"{srck}{i}", "h128b"], writes=[pk])
                        for c in range(nmc):
                            dst = wst[c][:, i * 128:(i + 1) * 128]
                            if scale_ap is None:
                                em.op("act", lambda e, ps=ps, dst=dst, c=c:
                                      e.copy(out=dst, in_=ps[:, c * 128:
                                                            (c + 1) * 128]),
                                      reads=[pk], writes=[f"p0w{c}_{i}"])
                            else:
                                em.op("dve", lambda e, ps=ps, dst=dst, c=c, i=i:
                                      e.tensor_scalar_mul(
                                          out=dst, in0=ps[:, c * 128:(c + 1) * 128],
                                          scalar1=scale_ap[:, i:i + 1]),
                                      reads=[pk, scale_key],
                                      writes=[f"p0w{c}_{i}"])
                    for c in range(nmc):
                        mc = s * nmc + c
                        em.dma("pool", scr[mc, :, 0:nt * 128],
                               wst[c][:, 0:nt * 128],
                               reads=[f"p0w{c}_{i}" for i in range(nt)],
                               writes=[f"{scr_pref}{mc}"])

            if ways == 1:
                transform(wup_in, KD, 512, su_ug_s, "su_ug_s", scr_up, "scrup",
                          INTER)
                transform(wgt_in, KD, 512, su_ug_s, "su_ug_s", scr_gt, "scrgt",
                          INTER)
                transform(wdn_in, KI, 256, None, None, scr_dn, "scrdn", D)
            else:
                rg = [list(range(ways))]
                nmu = KI // ways
                nmd = KD // ways
                transform(wup_in, KD, 512, su_ug_s, "su_ug_s", sup_my, "supmy",
                          INTER // ways)
                em.collective(
                    lambda e: e.collective_compute(
                        "AllGather", ALU.bypass, replica_groups=rg,
                        ins=[sup_my.rearrange("a b c -> (a b) c").opt()],
                        outs=[scr_up.rearrange("a b c -> (a b) c").opt()]),
                    reads=[f"supmy{i}" for i in range(nmu)],
                    writes=[f"scrup{m}" for m in range(KI)])
                transform(wgt_in, KD, 512, su_ug_s, "su_ug_s", sgt_my, "sgtmy",
                          INTER // ways)
                em.collective(
                    lambda e: e.collective_compute(
                        "AllGather", ALU.bypass, replica_groups=rg,
                        ins=[sgt_my.rearrange("a b c -> (a b) c").opt()],
                        outs=[scr_gt.rearrange("a b c -> (a b) c").opt()]),
                    reads=[f"sgtmy{i}" for i in range(nmu)],
                    writes=[f"scrgt{m}" for m in range(KI)])
                transform(wdn_in, KI, 256, None, None, sdn_my, "sdnmy", D // ways)
                em.collective(
                    lambda e: e.collective_compute(
                        "AllGather", ALU.bypass, replica_groups=rg,
                        ins=[sdn_my.rearrange("a b c -> (a b) c").opt()],
                        outs=[scr_dn.rearrange("a b c -> (a b) c").opt()]),
                    reads=[f"sdnmy{i}" for i in range(nmd)],
                    writes=[f"scrdn{m}" for m in range(KD)])
            em.barrier()

        def ffn_pass(tag, src_ap, src_key, dst_ap, dst_key, tiles):
            with ExitStack() as ph:
                psb = lambda n, s, d: ph.enter_context(nc.sbuf_tensor(tag + n, s, d))
                xbf = psb("xbf", [128, KD * TT], BF16)
                sqs = psb("sqs", [128, 2 * TT], BF16)
                xcs = psb("xcs", [128, 2 * TT], F32)
                rows = psb("rows", [1, 3 * TT], F32)
                m_b = psb("m_b", [128, TT], F32)
                r_b = psb("r_b", [128, TT], F32)
                xf = psb("xf", [128, KD * TT], BF16)
                yb = psb("yb", [128, KI * TT], BF16)
                gst = psb("gst", [128, 2 * TT], F32)
                wub = psb("wub", [128, 4 * KD * 128], BF16)
                wgb = psb("wgb", [128, 4 * KD * 128], BF16)
                wdb = psb("wdb", [128, 2 * KI * 128], BF16)
                xrs = psb("xrs", [128, 2 * TT], F32)
                x1s = psb("x1s", [128, 2 * TT], F32)
                ps_s1 = ph.enter_context(nc.psum_tensor(tag + "ps_s1", [1, TT], F32))
                ps_s2 = ph.enter_context(nc.psum_tensor(tag + "ps_s2", [1, TT], F32))
                ps_bc = ph.enter_context(nc.psum_tensor(tag + "ps_bc", [128, TT], F32))
                ps_u = [ph.enter_context(
                    nc.psum_tensor(tag + f"ps_u{i}", [128, TT], F32)) for i in range(2)]
                ps_g = [ph.enter_context(
                    nc.psum_tensor(tag + f"ps_g{i}", [128, TT], F32)) for i in range(2)]

                def prologue_pieces(t):
                    c0 = t * TT
                    loads, pieces = [], []
                    for i0 in range(0, KD, 8):
                        def pld(i0=i0, c0=c0):
                            em.dma("pool",
                                   xbf[:, i0 * TT:(i0 + 8) * TT].rearrange(
                                       "p (blk c) -> p blk c", blk=8),
                                   src_ap[i0 * 128:(i0 + 8) * 128, c0:c0 + TT]
                                   .rearrange("(blk p) c -> p blk c", p=128),
                                   reads=[src_key],
                                   writes=[f"xbf{i}" for i in range(i0, i0 + 8)])
                        loads.append(pld)
                    for i in range(KD):
                        def pst(i=i):
                            sl = i % 2
                            sq = sqs[:, sl * TT:(sl + 1) * TT]
                            em.op("act", lambda e, sq=sq, i=i: e.activation(
                                out=sq, in_=xbf[:, i * TT:(i + 1) * TT],
                                func=AF.Square),
                                reads=[f"xbf{i}"], writes=[f"sqs{sl}"])
                            em.op("pe", lambda e, i=i: e.matmul(
                                ps_s1[:, :], lhsT=onesb[:, :],
                                rhs=xbf[:, i * TT:(i + 1) * TT],
                                start=(i == 0), stop=(i == KD - 1),
                                skip_group_check=True),
                                reads=[f"xbf{i}", "onesb"], writes=["ps_s1"])
                            em.op("pe", lambda e, sq=sq, i=i: e.matmul(
                                ps_s2[:, :], lhsT=onesb[:, :], rhs=sq,
                                start=(i == 0), stop=(i == KD - 1),
                                skip_group_check=True),
                                reads=[f"sqs{sl}", "onesb"], writes=["ps_s2"])
                        pieces.append(pst)

                    def prows():
                        mr, vr = rows[:, 0:TT], rows[:, TT:2 * TT]
                        tr = rows[:, 2 * TT:3 * TT]
                        rr = mr
                        em.op("dve", lambda e: e.tensor_scalar_mul(
                            out=mr, in0=ps_s1[:, :], scalar1=1.0 / D),
                            reads=["ps_s1"], writes=["row_m"])
                        em.op("dve", lambda e: e.tensor_scalar_mul(
                            out=vr, in0=ps_s2[:, :], scalar1=1.0 / D),
                            reads=["ps_s2"], writes=["row_v"])
                        em.op("dve", lambda e: e.tensor_mul(out=tr, in0=mr, in1=mr),
                              reads=["row_m"], writes=["row_t"])
                        em.op("dve", lambda e: e.tensor_sub(out=vr, in0=vr, in1=tr),
                              reads=["row_v", "row_t"], writes=["row_v"])
                        em.op("dve", lambda e: e.tensor_scalar_add(
                            out=vr, in0=vr, scalar1=LN_EPS),
                            reads=["row_v"], writes=["row_v"])
                        em.op("dve", lambda e: e.reciprocal(out=tr, in_=vr),
                              reads=["row_v"], writes=["row_t"])
                        em.op("pe", lambda e: e.matmul(
                            ps_bc[:, :], lhsT=onesrow[:, :], rhs=mr,
                            start=True, stop=True),
                            reads=["row_m", "onesrow"], writes=["ps_bc"])
                        em.op("dve", lambda e: e.tensor_copy(
                            out=m_b[:, :], in_=ps_bc[:, :]),
                            reads=["ps_bc"], writes=["m_b"])
                        em.op("act", lambda e: e.activation(
                            out=rr, in_=tr, func=AF.Sqrt),
                            reads=["row_t"], writes=["row_m"])
                        em.op("pe", lambda e: e.matmul(
                            ps_bc[:, :], lhsT=onesrow[:, :], rhs=rr,
                            start=True, stop=True),
                            reads=["row_m", "onesrow"], writes=["ps_bc"])
                        em.op("dve", lambda e: e.tensor_copy(
                            out=r_b[:, :], in_=ps_bc[:, :]),
                            reads=["ps_bc"], writes=["r_b"])
                    pieces.append(prows)
                    for i in range(KD):
                        def papp(i=i):
                            sl = i % 2
                            xc = xcs[:, sl * TT:(sl + 1) * TT]
                            em.op("dve", lambda e, xc=xc, i=i: e.tensor_sub(
                                out=xc, in0=xbf[:, i * TT:(i + 1) * TT],
                                in1=m_b[:, :]),
                                reads=[f"xbf{i}", "m_b"], writes=[f"xcs{sl}"])
                            em.op("dve", lambda e, xc=xc, i=i: e.tensor_mul(
                                out=xf[:, i * TT:(i + 1) * TT], in0=xc,
                                in1=r_b[:, :]),
                                reads=[f"xcs{sl}", "r_b"], writes=[f"xf{i}"])
                        pieces.append(papp)
                    return loads, pieces

                def emit_upgate(t):
                    for m in range(KI):
                        sl = m % 4
                        wu = wub[:, sl * KD * 128:(sl + 1) * KD * 128]
                        wg = wgb[:, sl * KD * 128:(sl + 1) * KD * 128]
                        wuk, wgk = f"wub{sl}", f"wgb{sl}"
                        em.dma("sp", wu, scr_up[m, :, :], reads=[f"scrup{m}"],
                               writes=[wuk])
                        em.dma("sp", wg, scr_gt[m, :, :], reads=[f"scrgt{m}"],
                               writes=[wgk])
                        psu, psg = ps_u[m % 2], ps_g[m % 2]
                        psuk, psgk = f"ps_u{m % 2}", f"ps_g{m % 2}"
                        for k in range(KD):
                            em.op("pe", lambda e, psu=psu, wu=wu, k=k: e.matmul(
                                psu[:, :], lhsT=wu[:, k * 128:(k + 1) * 128],
                                rhs=xf[:, k * TT:(k + 1) * TT],
                                start=(k == 0), stop=(k == KD - 1)),
                                reads=[wuk, f"xf{k}"], writes=[psuk])
                        for k in range(KD):
                            em.op("pe", lambda e, psg=psg, wg=wg, k=k: e.matmul(
                                psg[:, :], lhsT=wg[:, k * 128:(k + 1) * 128],
                                rhs=xf[:, k * TT:(k + 1) * TT],
                                start=(k == 0), stop=(k == KD - 1)),
                                reads=[wgk, f"xf{k}"], writes=[psgk])
                        gs = gst[:, (m % 2) * TT:(m % 2 + 1) * TT]
                        em.op("act", lambda e, psg=psg, gs=gs, m=m: e.activation(
                            out=gs, in_=psg[:, :], func=AF.Silu,
                            scale=wsc_g_s[:, m:m + 1]),
                            reads=[psgk, "wsc_g_s"], writes=[f"gst{m % 2}"])
                        em.op("dve", lambda e, psu=psu, gs=gs, m=m:
                              e.scalar_tensor_tensor(
                                  out=yb[:, m * TT:(m + 1) * TT], in0=psu[:, :],
                                  scalar=comb_us[:, m:m + 1], in1=gs,
                                  op0=ALU.mult, op1=ALU.mult),
                              reads=[psuk, f"gst{m % 2}", "comb_us"],
                              writes=[f"yb{m}"])

                def down_pieces(t):
                    c0 = t * TT
                    pieces = []
                    for m in range(KD):
                        def pdn(m=m, c0=c0):
                            sl = m % 2
                            wd = wdb[:, sl * KI * 128:(sl + 1) * KI * 128]
                            wdk = f"wdb{sl}"
                            em.dma("sp", wd, scr_dn[m, :, :], reads=[f"scrdn{m}"],
                                   writes=[wdk])
                            xr = xrs[:, sl * TT:(sl + 1) * TT]
                            em.dma("sp", xr,
                                   src_ap[m * 128:(m + 1) * 128, c0:c0 + TT],
                                   reads=[src_key], writes=[f"xrs{sl}"])
                            psd = ps_u[sl]
                            psdk = f"ps_u{sl}"
                            for k in range(KI):
                                em.op("pe", lambda e, psd=psd, wd=wd, k=k: e.matmul(
                                    psd[:, :], lhsT=wd[:, k * 128:(k + 1) * 128],
                                    rhs=yb[:, k * TT:(k + 1) * TT],
                                    start=(k == 0), stop=(k == KI - 1)),
                                    reads=[wdk, f"yb{k}"], writes=[psdk])
                            x1c = x1s[:, sl * TT:(sl + 1) * TT]
                            em.op("dve", lambda e, psd=psd, x1c=x1c, xr=xr, m=m:
                                  e.scalar_tensor_tensor(
                                      out=x1c, in0=psd[:, :],
                                      scalar=wsc_dp_s[:, m:m + 1], in1=xr,
                                      op0=ALU.mult, op1=ALU.add),
                                  reads=[psdk, f"xrs{sl}", "wsc_dp_s"],
                                  writes=[f"x1s{sl}"])
                            em.dma("pool",
                                   dst_ap[m * 128:(m + 1) * 128, c0:c0 + TT], x1c,
                                   reads=[f"x1s{sl}"], writes=[dst_key])
                        pieces.append(pdn)
                    return pieces

                prev_down = None
                for t in range(tiles):
                    loads, pro = prologue_pieces(t)
                    for p in loads:
                        p()
                    if prev_down is None:
                        for p in pro:
                            p()
                    else:
                        nd = len(prev_down)
                        npr = len(pro)
                        j = 0
                        for i, pd in enumerate(prev_down):
                            pd()
                            jn = (i + 1) * npr // nd
                            while j < jn:
                                pro[j]()
                                j += 1
                    emit_upgate(t)
                    prev_down = down_pieces(t)
                for p in prev_down:
                    p()
                em.barrier()

        ffn_pass("a1", x_in, "x_in", x1_scr, "x1_scr", n_own_tiles)

        with ExitStack() as ph:
            psb = lambda n, s, d: ph.enter_context(nc.sbuf_tensor("a2" + n, s, d))
            x1bf = psb("x1bf", [128, 2 * KD * TT], BF16)
            sqs = psb("sqs", [128, 2 * TT], BF16)
            xn2 = psb("xn2", [128, 2 * KD * TT], BF16)
            qf = psb("qf", [128, KD * TT], BF16)
            kf = psb("kf", [128, KD * TT], BF16)
            cq2 = psb("cq2", [128, TT], F32)
            sq2 = psb("sq2", [128, TT], F32)
            ck2 = psb("ck2", [128, TT], F32)
            sk2 = psb("sk2", [128, TT], F32)
            rows = psb("rows", [1, 6 * TT], F32)
            m_b = psb("m_b", [128, TT], F32)
            r_b = psb("r_b", [128, TT], F32)
            qr_b = psb("qr_b", [128, TT], F32)
            kr_b = psb("kr_b", [128, TT], F32)
            wqb = psb("wqb", [128, 4 * KD * 128], BF16)
            cqt = psb("cqt", [128, 2 * TT], F32)
            sqt = psb("sqt", [128, 2 * TT], F32)
            ckt = psb("ckt", [128, 2 * TT], F32)
            skt = psb("skt", [128, 2 * TT], F32)
            uu = psb("uu", [128, TT], F32)
            ub = psb("ub", [128, TT], BF16)
            swf = psb("swf", [128, TT], F32)
            aa = psb("aa", [128, TT], F32)
            bb2 = psb("bb2", [128, TT], F32)
            qkb = psb("qkb", [128, 2 * TT], BF16)
            vbs = psb("vbs", [128, TT], BF16)
            vts = psb("vts", [128, 2 * 4 * 128], BF16)
            ps_qk = [ph.enter_context(nc.psum_tensor(f"a2ps{i}", [128, TT], F32))
                     for i in range(2)]
            ps_sq = ph.enter_context(nc.psum_tensor("a2psq", [1, TT], F32))
            ps_sk = ph.enter_context(nc.psum_tensor("a2psk", [1, TT], F32))
            ps_bc = ph.enter_context(nc.psum_tensor("a2psbc", [128, TT], F32))
            ps_sw = ph.enter_context(nc.psum_tensor("a2pssw", [128, TT], F32))
            ps_tr = [ph.enter_context(nc.psum_tensor(f"a2pstr{i}", [128, 128], F32))
                     for i in range(2)]

            def a2_loads(t):
                c0 = t * TT
                xsl = (t % 2) * KD * TT
                tsl = (t % 2) * TT
                for i0 in range(0, KD, 8):
                    em.dma("pool",
                           x1bf[:, xsl + i0 * TT:xsl + (i0 + 8) * TT].rearrange(
                               "p (blk c) -> p blk c", blk=8),
                           x1_scr[i0 * 128:(i0 + 8) * 128, c0:c0 + TT]
                           .rearrange("(blk p) c -> p blk c", p=128),
                           reads=["x1_scr"],
                           writes=[f"x1bf{t % 2}_{i}" for i in range(i0, i0 + 8)])
                em.dma("sp", ckt[:, tsl:tsl + TT], cosk[:, c0:c0 + TT],
                       writes=[f"ckt{t % 2}"])
                em.dma("sp", skt[:, tsl:tsl + TT], sink[:, c0:c0 + TT],
                       writes=[f"skt{t % 2}"])
                em.dma("sp", cqt[:, tsl:tsl + TT], cosq[:, c0:c0 + TT],
                       writes=[f"cqt{t % 2}"])
                em.dma("sp", sqt[:, tsl:tsl + TT], sinq[:, c0:c0 + TT],
                       writes=[f"sqt{t % 2}"])

            def a2_prologue_pieces(t):
                xsl = (t % 2) * KD * TT
                xk = lambda i: f"x1bf{t % 2}_{i}"
                pieces = []
                for i in range(KD):
                    def pst(i=i):
                        sl = i % 2
                        sq = sqs[:, sl * TT:(sl + 1) * TT]
                        xb_ = x1bf[:, xsl + i * TT:xsl + (i + 1) * TT]
                        em.op("act", lambda e, xb_=xb_, sq=sq: e.activation(
                            out=sq, in_=xb_, func=AF.Square),
                            reads=[xk(i)], writes=[f"sqs{sl}"])
                        em.op("pe", lambda e, xb_=xb_, i=i: e.matmul(
                            ps_sq[:, :], lhsT=onesb[:, :], rhs=xb_,
                            start=(i == 0), stop=(i == KD - 1),
                            skip_group_check=True),
                            reads=[xk(i), "onesb"], writes=["ps_sq"])
                        em.op("pe", lambda e, sq=sq, i=i: e.matmul(
                            ps_sk[:, :], lhsT=onesb[:, :], rhs=sq,
                            start=(i == 0), stop=(i == KD - 1),
                            skip_group_check=True),
                            reads=[f"sqs{sl}", "onesb"], writes=["ps_sk"])
                    pieces.append(pst)

                def prows():
                    mr, vr = rows[:, 0:TT], rows[:, TT:2 * TT]
                    tr = rows[:, 2 * TT:3 * TT]
                    rr = mr
                    em.op("dve", lambda e: e.tensor_scalar_mul(
                        out=mr, in0=ps_sq[:, :], scalar1=1.0 / D),
                        reads=["ps_sq"], writes=["row_m"])
                    em.op("dve", lambda e: e.tensor_scalar_mul(
                        out=vr, in0=ps_sk[:, :], scalar1=1.0 / D),
                        reads=["ps_sk"], writes=["row_v"])
                    em.op("dve", lambda e: e.tensor_mul(out=tr, in0=mr, in1=mr),
                          reads=["row_m"], writes=["row_t"])
                    em.op("dve", lambda e: e.tensor_sub(out=vr, in0=vr, in1=tr),
                          reads=["row_v", "row_t"], writes=["row_v"])
                    em.op("dve", lambda e: e.tensor_scalar_add(
                        out=vr, in0=vr, scalar1=LN_EPS),
                        reads=["row_v"], writes=["row_v"])
                    em.op("dve", lambda e: e.reciprocal(out=tr, in_=vr),
                          reads=["row_v"], writes=["row_t"])
                    em.op("pe", lambda e: e.matmul(
                        ps_bc[:, :], lhsT=onesrow[:, :], rhs=mr,
                        start=True, stop=True),
                        reads=["row_m", "onesrow"], writes=["ps_bc"])
                    em.op("dve", lambda e: e.tensor_copy(
                        out=m_b[:, :], in_=ps_bc[:, :]),
                        reads=["ps_bc"], writes=["m_b"])
                    em.op("act", lambda e: e.activation(
                        out=rr, in_=tr, func=AF.Sqrt),
                        reads=["row_t"], writes=["row_m"])
                    em.op("pe", lambda e: e.matmul(
                        ps_bc[:, :], lhsT=onesrow[:, :], rhs=rr,
                        start=True, stop=True),
                        reads=["row_m", "onesrow"], writes=["ps_bc"])
                    em.op("dve", lambda e: e.tensor_copy(
                        out=r_b[:, :], in_=ps_bc[:, :]),
                        reads=["ps_bc"], writes=["r_b"])
                pieces.append(prows)
                for i in range(KD):
                    def papp(i=i):
                        xb_ = x1bf[:, xsl + i * TT:xsl + (i + 1) * TT]
                        em.op("dve", lambda e, xb_=xb_: e.tensor_sub(
                            out=uu[:, :], in0=xb_, in1=m_b[:, :]),
                            reads=[xk(i), "m_b"], writes=["uu"])
                        em.op("dve", lambda e, i=i: e.tensor_mul(
                            out=xn2[:, xsl + i * TT:xsl + (i + 1) * TT],
                            in0=uu[:, :], in1=r_b[:, :]),
                            reads=["uu", "r_b"], writes=[f"xn2{t % 2}_{i}"])
                    pieces.append(papp)
                return pieces

            def qkv_mm(t, m):
                xsl = (t % 2) * KD * TT
                sl = m % 4
                wq = wqb[:, sl * KD * 128:(sl + 1) * KD * 128]
                wqk = f"wqb{sl}"
                if t == 0:
                    em.dma("pool", wq, wqkv_in[m, :, :], writes=[wqk])
                    em.dma("pool", scr_qkv[m, :, :], wq, reads=[wqk],
                           writes=[f"scrqkv{m}"])
                else:
                    em.dma("sp", wq, scr_qkv[m, :, :], reads=[f"scrqkv{m}"],
                           writes=[wqk])
                ps = ps_qk[m % 2]
                psk = f"ps_qk{m % 2}"
                for k in range(KD):
                    em.op("pe", lambda e, ps=ps, wq=wq, k=k, xsl=xsl: e.matmul(
                        ps[:, :], lhsT=wq[:, k * 128:(k + 1) * 128],
                        rhs=xn2[:, xsl + k * TT:xsl + (k + 1) * TT],
                        start=(k == 0), stop=(k == KD - 1)),
                        reads=[wqk, f"xn2{t % 2}_{k}"], writes=[psk])
                return ps, psk, m % 2

            def qk_section(t):
                has_q = True
                for m in range(2 * KD):
                    ps, psk, sl = qkv_mm(t, m)
                    if m < KD:
                        if not has_q:
                            continue
                        em.op("act", lambda e, ps=ps, m=m: e.copy(
                            out=qf[:, m * TT:(m + 1) * TT], in_=ps[:, :]),
                            reads=[psk], writes=[f"qf{m}"])
                        sq = sqs[:, sl * TT:(sl + 1) * TT]
                        em.op("act", lambda e, ps=ps, sq=sq: e.activation(
                            out=sq, in_=ps[:, :], func=AF.Square),
                            reads=[psk], writes=[f"sqs{sl}"])
                        em.op("pe", lambda e, sq=sq, m=m: e.matmul(
                            ps_sq[:, :], lhsT=onesb[:, :], rhs=sq,
                            start=(m == 0), stop=(m == KD - 1),
                            skip_group_check=True),
                            reads=[f"sqs{sl}", "onesb"], writes=["ps_sq"])
                    else:
                        mk = m - KD
                        em.op("act", lambda e, ps=ps, mk=mk: e.copy(
                            out=kf[:, mk * TT:(mk + 1) * TT], in_=ps[:, :]),
                            reads=[psk], writes=[f"kf{mk}"])
                        sq = sqs[:, sl * TT:(sl + 1) * TT]
                        em.op("act", lambda e, ps=ps, sq=sq: e.activation(
                            out=sq, in_=ps[:, :], func=AF.Square),
                            reads=[psk], writes=[f"sqs{sl}"])
                        em.op("pe", lambda e, sq=sq, mk=mk: e.matmul(
                            ps_sk[:, :], lhsT=onesb[:, :], rhs=sq,
                            start=(mk == 0), stop=(mk == KD - 1),
                            skip_group_check=True),
                            reads=[f"sqs{sl}", "onesb"], writes=["ps_sk"])
                qrr, krr = rows[:, 3 * TT:4 * TT], rows[:, 4 * TT:5 * TT]
                t2r = rows[:, 5 * TT:6 * TT]

                def rms_row(out_r, ps_src, ps_key, okey):
                    em.op("dve", lambda e, out_r=out_r, ps_src=ps_src:
                          e.tensor_scalar_mul(out=t2r, in0=ps_src[:, :],
                                              scalar1=1.0 / D),
                          reads=[ps_key], writes=["row_t2"])
                    em.op("dve", lambda e: e.tensor_scalar_add(
                        out=t2r, in0=t2r, scalar1=RMS_EPS),
                        reads=["row_t2"], writes=["row_t2"])
                    em.op("dve", lambda e, out_r=out_r: e.reciprocal(
                        out=out_r, in_=t2r),
                        reads=["row_t2"], writes=[okey])
                    em.op("act", lambda e, out_r=out_r: e.activation(
                        out=out_r, in_=out_r, func=AF.Sqrt),
                        reads=[okey], writes=[okey])

                tsl = (t % 2) * TT
                if has_q:
                    rms_row(qrr, ps_sq, "ps_sq", "row_q")
                    em.op("pe", lambda e: e.matmul(
                        ps_bc[:, :], lhsT=onesrow[:, :], rhs=qrr,
                        start=True, stop=True),
                        reads=["row_q", "onesrow"], writes=["ps_bc"])
                    em.op("dve", lambda e: e.tensor_copy(
                        out=qr_b[:, :], in_=ps_bc[:, :]),
                        reads=["ps_bc"], writes=["qr_b"])
                    em.op("dve", lambda e, tsl=tsl: e.tensor_mul(
                        out=cq2[:, :], in0=cqt[:, tsl:tsl + TT], in1=qr_b[:, :]),
                        reads=[f"cqt{t % 2}", "qr_b"], writes=["cq2"])
                    em.op("dve", lambda e, tsl=tsl: e.tensor_mul(
                        out=sq2[:, :], in0=sqt[:, tsl:tsl + TT], in1=qr_b[:, :]),
                        reads=[f"sqt{t % 2}", "qr_b"], writes=["sq2"])
                rms_row(krr, ps_sk, "ps_sk", "row_k")
                em.op("pe", lambda e: e.matmul(
                    ps_bc[:, :], lhsT=onesrow[:, :], rhs=krr,
                    start=True, stop=True),
                    reads=["row_k", "onesrow"], writes=["ps_bc"])
                em.op("dve", lambda e: e.tensor_copy(
                    out=kr_b[:, :], in_=ps_bc[:, :]),
                    reads=["ps_bc"], writes=["kr_b"])
                em.op("dve", lambda e, tsl=tsl: e.tensor_mul(
                    out=ck2[:, :], in0=ckt[:, tsl:tsl + TT], in1=kr_b[:, :]),
                    reads=[f"ckt{t % 2}", "kr_b"], writes=["ck2"])
                em.op("dve", lambda e, tsl=tsl: e.tensor_mul(
                    out=sk2[:, :], in0=skt[:, tsl:tsl + TT], in1=kr_b[:, :]),
                    reads=[f"skt{t % 2}", "kr_b"], writes=["sk2"])

            def rope_out(t, src_f, src_key, w_ap, w_key,
                         ct, ct_key, st_, st_key, dst, dst_key, sl):
                em.op("dve", lambda e: e.tensor_scalar_mul(
                    out=ub[:, :], in0=src_f, scalar1=w_ap),
                    reads=[src_key, w_key], writes=["ub"])
                em.op("pe", lambda e: e.matmul(
                    ps_sw[:, :], lhsT=pswapb[:, :], rhs=ub[:, :],
                    start=True, stop=True),
                    reads=["ub", "pswapb"], writes=["ps_sw"])
                em.op("act", lambda e: e.copy(out=swf[:, :], in_=ps_sw[:, :]),
                      reads=["ps_sw"], writes=["swf"])
                em.op("dve", lambda e, ct=ct: e.tensor_mul(
                    out=aa[:, :], in0=ub[:, :], in1=ct),
                    reads=["ub", ct_key], writes=["aa"])
                em.op("dve", lambda e, st_=st_: e.tensor_mul(
                    out=bb2[:, :], in0=swf[:, :], in1=st_),
                    reads=["swf", st_key], writes=["bb2"])
                qk = qkb[:, sl * TT:(sl + 1) * TT]
                em.op("dve", lambda e, qk=qk: e.tensor_add(
                    out=qk, in0=aa[:, :], in1=bb2[:, :]),
                    reads=["aa", "bb2"], writes=[f"qkb{sl}"])
                em.dma("pool", dst, qk, reads=[f"qkb{sl}"], writes=[dst_key])

            def a2_tail_pieces(t):
                c0 = t * TT
                has_q = True
                tsl = (t % 2) * TT
                rope_pieces = []
                if has_q:
                    for h in range(NH):
                        def pq(h=h, c0=c0):
                            rope_out(t, qf[:, h * TT:(h + 1) * TT], f"qf{h}",
                                     qnw_s[:, h:h + 1], "qnw_s",
                                     cq2[:, :], "cq2", sq2[:, :], "sq2",
                                     q_scr[:, h, c0:c0 + TT],
                                     "q_scr", h % 2)
                        rope_pieces.append(pq)
                for h in range(NH):
                    def pk_(h=h, c0=c0):
                        rope_out(t, kf[:, h * TT:(h + 1) * TT], f"kf{h}",
                                 knw_s[:, h:h + 1], "knw_s",
                                 ck2[:, :], "ck2", sk2[:, :], "sk2",
                                 k_scr[h, :, WINDOW + c0:WINDOW + c0 + TT],
                                 "k_scr", h % 2)
                    rope_pieces.append(pk_)
                pieces = []
                nrp = len(rope_pieces)
                for mv in range(NH):
                    def pv_(mv=mv, c0=c0, j0=mv * nrp // NH,
                            j1=(mv + 1) * nrp // NH):
                        ps, psk, sl = qkv_mm(t, 2 * KD + mv)
                        em.op("act", lambda e, ps=ps: e.copy(
                            out=vbs[:, :], in_=ps[:, :]),
                            reads=[psk], writes=["vbs"])
                        vsl = (mv % 2) * 4 * 128
                        for s4 in range(TT // 128):
                            ptr = ps_tr[s4 % 2]
                            ptrk = f"ps_tr{s4 % 2}"
                            vt = vts[:, vsl + s4 * 128:vsl + (s4 + 1) * 128]
                            em.op("pe", lambda e, ptr=ptr, s4=s4: e.matmul(
                                ptr[:, :], lhsT=vbs[:, s4 * 128:(s4 + 1) * 128],
                                rhs=identb[:, :], start=True, stop=True),
                                reads=["vbs", "identb"], writes=[ptrk])
                            em.op("act", lambda e, ptr=ptr, vt=vt: e.copy(
                                out=vt, in_=ptr[:, :]),
                                reads=[ptrk], writes=[f"vts{mv % 2}"])
                        em.dma("sp",
                               v_scr[mv, :, WINDOW // 128 + c0 // 128:
                                     WINDOW // 128 + c0 // 128 + 4, :],
                               vts[:, vsl:vsl + 4 * 128].rearrange(
                                   "p (kb c) -> p kb c", kb=4),
                               reads=[f"vts{mv % 2}"], writes=["v_scr"])
                        for j in range(j0, j1):
                            rope_pieces[j]()
                    pieces.append(pv_)
                return pieces

            prev_tail = None
            for t in range(n_own_tiles):
                a2_loads(t)
                pro = a2_prologue_pieces(t)
                if prev_tail is None:
                    for p in pro:
                        p()
                else:
                    nd = len(prev_tail)
                    npr = len(pro)
                    j = 0
                    for i, pd in enumerate(prev_tail):
                        pd()
                        jn = min(npr, (i + 1) * 2 * npr // nd)
                        while j < jn:
                            pro[j]()
                            j += 1
                qk_section(t)
                prev_tail = a2_tail_pieces(t)
            for p in prev_tail:
                p()

            em.barrier()

        with ExitStack() as ph:
            psb = lambda n, s, d: ph.enter_context(nc.sbuf_tensor("b" + n, s, d))
            KW = WINDOW + TT
            NKC = KW // 128
            qsb = psb("qsb", [128, NH * TT], BF16)
            ksb = psb("ksb", [128, 4 * KW], BF16)
            vsb = psb("vsb", [128, 4 * NKC * 128], BF16)
            pf32 = psb("pf32", [128, 2 * 1152], F32)
            pb = psb("pb", [128, 2 * 1152], BF16)
            ptb = psb("ptb", [128, 2 * 1152], BF16)
            nmax = psb("nmax", [128, 2], F32)
            rows_nm = psb("rows_nm", [128, 6], F32)
            ssum = psb("ssum", [128, 2], F32)
            rsum = psb("rsum", [128, 2], F32)
            attn = psb("attn", [128, NH * TT], BF16)
            attn_tm = psb("attn_tm", [128, NH * 4 * 128], BF16)
            wosb = psb("wosb", [128, KD * KD * 128], BF16)
            x1c = psb("x1c", [128, 4 * TT], F32)
            x2c = psb("x2c", [128, 2 * TT], F32)
            ps_scs = [ph.enter_context(nc.psum_tensor(f"bps_sc{i}", [128, 1152],
                                                       F32)) for i in range(2)]
            ps_pv = ph.enter_context(nc.psum_tensor("bps_pv", [128, TT], F32))
            ps_wo = [ph.enter_context(nc.psum_tensor("bps_wo", [128, TT], F32))]

            idxt = ph.enter_context(nc.sbuf_tensor("bidxt", [1, 1],
                                                   mybir.dt.int32))
            em.dma("sp", ktail[:, :, :],
                   k_scr[:, :, t_all - WINDOW:t_all], writes=["ktail"])
            em.dma("sp", vtail[:, :, :, :],
                   v_scr[:, :, (t_all - WINDOW) // 128:t_all // 128, :],
                   writes=["vtail"])
            if ways > 1:
                rg = [list(range(ways))]
                em.collective(
                    lambda e: e.collective_compute(
                        "AllGather", ALU.bypass, replica_groups=rg,
                        ins=[ktail.rearrange("a b c -> (a b) c").opt()],
                        outs=[kgat.rearrange("r a b c -> (r a b) c").opt()]),
                    reads=["ktail"], writes=["kgat"])
                em.collective(
                    lambda e: e.collective_compute(
                        "AllGather", ALU.bypass, replica_groups=rg,
                        ins=[vtail.rearrange("a b c d -> (a b c) d").opt()],
                        outs=[vgat.rearrange("r a b c d -> (r a b c) d").opt()]),
                    reads=["vtail"], writes=["vgat"])
            else:
                em.dma("sp", kgat[0], ktail[:, :, :],
                       reads=["ktail"], writes=["kgat"])
                em.dma("sp", vgat[0], vtail[:, :, :, :],
                       reads=["vtail"], writes=["vgat"])
            em.dma("sp", idxt[:, :], prevrank[:, :], writes=["idxt"])

            kstride = NH * 128 * WINDOW
            vstride = NH * 128 * (WINDOW // 128) * 128

            def dyn_fetch(e, gat_h, stride, dst_ap, ndims_ap):
                with e.register() as reg:
                    e.reg_load(reg, idxt[:1, :1])
                    with e.register() as ofs:
                        e.reg_mul(ofs, reg, stride)
                        src_dyn = bass.AP(gat_h, ofs, ndims_ap)
                        return e.dma_start(out=dst_ap, in_=src_dyn)

            k_ap = [[128 * WINDOW, NH], [WINDOW, 128], [1, WINDOW]]
            v_ap = [[128 * WINDOW, NH], [WINDOW, 128], [1, WINDOW]]

            def bk(e):
                return dyn_fetch(e, kgat_h, kstride,
                                 k_scr[:, :, 0:WINDOW], k_ap)

            def bv(e):
                return dyn_fetch(e, vgat_h, vstride,
                                 v_scr[:, :, 0:WINDOW // 128, :].rearrange(
                                     "a b c d -> a b (c d)"), v_ap)

            em.dma_custom("pool", bk, reads=["kgat", "idxt"], writes=["k_scr_h"])
            em.dma_custom("pool", bv, reads=["vgat", "idxt"], writes=["v_scr_h"])

            for m in range(KD):
                em.dma("pool", wosb[:, m * KD * 128:(m + 1) * KD * 128],
                       wo_in[m, :, :], writes=[f"wosb{m}"])

            qt_order = [qt for qt in range(n_own_tiles) if qt * TT >= WINDOW]
            qt_order += [qt for qt in range(n_own_tiles) if qt * TT < WINDOW]
            wo_pieces = []
            for qt in qt_order:
                kw0 = qt * TT
                em.dma("sp", qsb.ap().rearrange("p (h c) -> p h c", h=NH),
                       q_scr[:, :, qt * TT:(qt + 1) * TT],
                       reads=[],
                       writes=[f"qsb{h}" for h in range(NH)])
                prev_tail = None
                for h in range(NH):
                    sl = h % 2
                    if h % 2 == 0:
                        psl2 = (h // 2) % 2
                        em.dma("sp",
                               ksb[:, psl2 * 2 * KW:(psl2 * 2 + 2) * KW]
                               .rearrange("p (hh c) -> p hh c", hh=2),
                               k_scr[h:h + 2, :, kw0:kw0 + KW].rearrange(
                                   "hh p c -> p hh c"),
                               reads=(["k_scr_h"] if kw0 < WINDOW else []),
                               writes=[f"ksb{psl2 * 2}", f"ksb{psl2 * 2 + 1}"])
                        em.dma("sp",
                               vsb[:, psl2 * 2 * NKC * 128:
                                   (psl2 * 2 + 2) * NKC * 128].rearrange(
                                   "p (hh kb c) -> p hh kb c", hh=2, kb=NKC),
                               v_scr[h:h + 2, :, kw0 // 128:kw0 // 128 + NKC,
                                     :].rearrange("hh p kb c -> p hh kb c"),
                               reads=(["v_scr_h"] if kw0 < WINDOW else []),
                               writes=[f"vsb{psl2 * 2}", f"vsb{psl2 * 2 + 1}"])
                    slot = (h // 2) % 2 * 2 + h % 2
                    kwb = ksb[:, slot * KW:(slot + 1) * KW]
                    vwb = vsb[:, slot * NKC * 128:(slot + 1) * NKC * 128]
                    kwk, vwk = f"ksb{slot}", f"vsb{slot}"
                    for qs in range(TT // 128):
                        psl = qs % 2
                        ps_sc = ps_scs[psl]
                        sck = f"ps_sc{psl}"
                        pbx = pb[:, psl * 1152:(psl + 1) * 1152]
                        pfx = pf32[:, psl * 1152:(psl + 1) * 1152]
                        nmx = nmax[:, psl:psl + 1]
                        ssx = ssum[:, psl:psl + 1]
                        rsx = rsum[:, psl:psl + 1]
                        qlhs = qsb[:, h * TT + qs * 128: h * TT + qs * 128 + 128]
                        kof = qs * 128
                        for o, n in ((0, 512), (512, 512), (1024, 128)):
                            em.op("pe", lambda e, o=o, n=n, kwb=kwb, qlhs=qlhs,
                                  kof=kof, ps_sc=ps_sc: e.matmul(
                                      ps_sc[:, o:o + n], lhsT=qlhs,
                                      rhs=kwb[:, kof + o:kof + o + n],
                                      start=True, stop=True),
                                  reads=[f"qsb{h}", kwk], writes=[sck])
                        em.op("pe", lambda e, ps_sc=ps_sc: e.matmul(
                            ps_sc[:, 0:128], lhsT=identb[:, :], rhs=maskab[:, :],
                            start=False, stop=True, skip_group_check=True),
                            reads=["maskab", "identb"], writes=[sck])
                        em.op("pe", lambda e, ps_sc=ps_sc: e.matmul(
                            ps_sc[:, 1024:1152], lhsT=identb[:, :], rhs=maskcb[:, :],
                            start=False, stop=True, skip_group_check=True),
                            reads=["maskcb", "identb"], writes=[sck])
                        if kw0 < WINDOW:
                            hb0 = kw0 + kof
                            for o, n in ((0, 512), (512, 512), (1024, 128)):
                                em.op("pe", lambda e, o=o, n=n, hb0=hb0,
                                      ps_sc=ps_sc: e.matmul(
                                          ps_sc[:, o:o + n], lhsT=onesrb[:, :],
                                          rhs=hbiasb[:, hb0 + o:hb0 + o + n],
                                          start=False, stop=True,
                                          skip_group_check=True),
                                      reads=["hbiasb", "onesrb"], writes=[sck])
                        em.op("dve", lambda e, nmx=nmx, ps_sc=ps_sc: e.reduce_max(
                            out=nmx, in_=ps_sc[:, 0:1152], axis=AX.X,
                            negate=True),
                            reads=[sck], writes=[f"nmax{psl}"])
                        em.op("act", lambda e, pbx=pbx, nmx=nmx, ssx=ssx,
                              ps_sc=ps_sc: e.activation(
                                  out=pbx, in_=ps_sc[:, 0:1152], func=AF.Exp,
                                  bias=nmx, accum_out=ssx),
                              reads=[sck, f"nmax{psl}"],
                              writes=[f"pb{psl}", f"ssum{psl}"])
                        em.op("dve", lambda e, rsx=rsx, ssx=ssx: e.reciprocal(
                            out=rsx, in_=ssx),
                            reads=[f"ssum{psl}"], writes=[f"rsum{psl}"])

                        def tail(h=h, qs=qs, psl=psl, pbx=pbx, vwb=vwb, vwk=vwk,
                                 ps_sc=ps_sc, sck=sck, rsx=rsx):
                            ptx = ptb[:, psl * 1152:(psl + 1) * 1152]
                            for c in range(9):
                                em.op("pe", lambda e, c=c, pbx=pbx, ps_sc=ps_sc:
                                      e.matmul(
                                          ps_sc[:, c * 128:(c + 1) * 128],
                                          lhsT=pbx[:, c * 128:(c + 1) * 128],
                                          rhs=identb[:, :], start=True, stop=True),
                                      reads=[f"pb{psl}", "identb"], writes=[sck])
                            em.op("act", lambda e, ptx=ptx, ps_sc=ps_sc: e.copy(
                                out=ptx, in_=ps_sc[:, 0:1152]),
                                reads=[sck], writes=[f"ptb{psl}"])
                            for c in range(9):
                                em.op("pe", lambda e, c=c, vwb=vwb, qs=qs,
                                      ptx=ptx: e.matmul(
                                          ps_pv[:, qs * 128:(qs + 1) * 128],
                                          lhsT=ptx[:, c * 128:(c + 1) * 128],
                                          rhs=vwb[:, (qs + c) * 128:
                                                   (qs + c + 1) * 128],
                                          start=(c == 0), stop=(c == 8),
                                          skip_group_check=True),
                                      reads=[vwk, f"ptb{psl}"], writes=["ps_pv"])
                            em.op("dve", lambda e, h=h, qs=qs, rsx=rsx:
                                  e.tensor_scalar_mul(
                                      out=attn_tm[:, (h * 4 + qs) * 128:
                                                  (h * 4 + qs + 1) * 128],
                                      in0=ps_pv[:, qs * 128:(qs + 1) * 128],
                                      scalar1=rsx),
                                  reads=["ps_pv", f"rsum{psl}"],
                                  writes=[f"attm{h}_{qs}"])

                        if prev_tail is not None:
                            prev_tail()
                        if wo_pieces:
                            wo_pieces.pop(0)()
                        prev_tail = tail
                if prev_tail is not None:
                    prev_tail()
                while wo_pieces:
                    wo_pieces.pop(0)()
                for h in range(NH):
                    reg = ps_scs[h % 2]
                    rk = f"ps_sc{h % 2}"
                    for qs in range(4):
                        em.op("pe", lambda e, h=h, qs=qs, reg=reg: e.matmul(
                            reg[:, qs * 128:(qs + 1) * 128],
                            lhsT=attn_tm[:, (h * 4 + qs) * 128:
                                         (h * 4 + qs + 1) * 128],
                            rhs=identb[:, :], start=True, stop=True),
                            reads=[f"attm{h}_{qs}", "identb"], writes=[rk])
                    em.op("act", lambda e, h=h, reg=reg: e.copy(
                        out=attn[:, h * TT:(h + 1) * TT], in_=reg[:, 0:512]),
                        reads=[rk], writes=[f"attn{h}"])
                def wo_piece(m, qt=qt):
                    sl = m % 2
                    if m % 4 == 0:
                        em.dma("sp",
                               x1c.ap().rearrange("p (blk c) -> p blk c", blk=4),
                               x1_scr[m * 128:(m + 4) * 128,
                                      qt * TT:(qt + 1) * TT]
                               .rearrange("(blk p) c -> p blk c", p=128),
                               reads=["x1_scr"],
                               writes=[f"x1c{j}" for j in range(4)])
                    psw = ps_wo[0]
                    pswk = "ps_wo0"
                    for k in range(KD):
                        em.op("pe", lambda e, psw=psw, m=m, k=k: e.matmul(
                            psw[:, :],
                            lhsT=wosb[:, (m * KD + k) * 128:(m * KD + k + 1) * 128],
                            rhs=attn[:, k * TT:(k + 1) * TT],
                            start=(k == 0), stop=(k == KD - 1)),
                            reads=[f"wosb{m}", f"attn{k}"], writes=[pswk])
                    xc1 = x1c[:, (m % 4) * TT:(m % 4 + 1) * TT]
                    xc2 = x2c[:, sl * TT:(sl + 1) * TT]
                    em.op("dve", lambda e, psw=psw, xc1=xc1, xc2=xc2: e.tensor_add(
                        out=xc2, in0=psw[:, :], in1=xc1),
                        reads=[pswk, f"x1c{m % 4}"], writes=[f"x2c{sl}"])
                    em.dma("pool", x2_scr[m * 128:(m + 1) * 128,
                                          qt * TT:(qt + 1) * TT], xc2,
                           reads=[f"x2c{sl}"], writes=["x2_scr"])

                wo_pieces = [lambda m=m: wo_piece(m) for m in range(KD)]
            while wo_pieces:
                wo_pieces.pop(0)()
            em.barrier()

        ffn_pass("c", x2_scr, "x2_scr", o_out, "o_out", n_own_tiles)

        em.barrier()
        with nc.Block() as block:
            em.finalize(block)

    return nc


def _sylvester(n):
    h = np.array([[1.0]], dtype=np.float32)
    while h.shape[0] < n:
        h = np.block([[h, h], [h, -h]])
    return h.astype(np.float32)


def _rope_tables(pos, scale):
    inv = 1.0 / (ROPE_THETA ** (np.arange(0, HD, 2, dtype=np.float32) / HD))
    f = pos[None, :].astype(np.float32) * inv[:, None]
    c = np.cos(f) * scale
    s = np.sin(f) * scale
    return (np.concatenate([c, c], axis=0).astype(np.float32),
            np.concatenate([s, s], axis=0).astype(np.float32))


def make_core_inputs(x_full, weights, t_own=2048, cores=None, ways=None):
    B, S, _ = x_full.shape
    n_chunks = S // t_own
    t_all = t_own + WINDOW

    w_qkv = np.asarray(weights["w_qkv"], np.float32)
    q_norm_w = np.asarray(weights["q_norm_w"], np.float32)
    k_norm_w = np.asarray(weights["k_norm_w"], np.float32)
    w_o = np.asarray(weights["w_o"], np.float32)
    SU_ug = np.asarray(weights["SU_ug"], np.float32)
    SU_dp = np.asarray(weights["SU_dp"], np.float32)
    Wsc_ug = np.asarray(weights["Wscale_ug"], np.float32)
    Wsc_dp = np.asarray(weights["Wscale_dp"], np.float32)
    W_up = np.asarray(weights["W_up"], np.float32)
    W_gate = np.asarray(weights["W_gate"], np.float32)
    W_down = np.asarray(weights["W_down"], np.float32)

    com = {}
    wup_full = W_up.T.reshape(KD, 128, INTER)
    wgt_full = W_gate.T.reshape(KD, 128, INTER)
    com["wqkv_in"] = np.ascontiguousarray(
        w_qkv.reshape(3 * KD, 128, KD, 128).transpose(0, 3, 2, 1)
        .reshape(3 * KD, 128, KD * 128))
    com["wo_in"] = np.ascontiguousarray(
        w_o.reshape(KD, 128, KD, 128).transpose(0, 3, 2, 1).reshape(KD, 128, KD * 128))
    wdn_full = W_down.T.reshape(KI, 128, D)
    com["su_ug"] = np.ascontiguousarray(SU_ug.reshape(KD, 128).T)
    com["su_dp"] = np.ascontiguousarray(SU_dp.reshape(KI, 128).T)
    com["wsc_u"] = np.ascontiguousarray(Wsc_ug[:INTER].reshape(KI, 128).T)
    com["wsc_g"] = np.ascontiguousarray(Wsc_ug[INTER:].reshape(KI, 128).T)
    com["wsc_dp"] = np.ascontiguousarray(Wsc_dp.reshape(KD, 128).T)
    com["qnw"] = np.ascontiguousarray(q_norm_w.reshape(KD, 128).T)
    com["knw"] = np.ascontiguousarray(k_norm_w.reshape(KD, 128).T)
    com["h128"] = _sylvester(128)
    com["ident"] = np.eye(128, dtype=np.float32)
    psw = np.zeros((128, 128), np.float32)
    for mcol in range(64):
        psw[mcol + 64, mcol] = -1.0
        psw[mcol, mcol + 64] = 1.0
    com["pswap"] = psw
    p_i = np.arange(128)[:, None]
    k_i = np.arange(128)[None, :]
    com["mask_c"] = np.where(p_i >= k_i, 0.0, NEG).astype(np.float32)
    com["mask_a"] = np.where(p_i < k_i, 0.0, NEG).astype(np.float32)

    if cores is None:
        cores = [(b, ch) for b in range(B) for ch in range(n_chunks)]
    if ways is None:
        ways = len(cores)
    in_maps = []
    for ci, (b, ch) in enumerate(cores):
        d = dict(com)
        mu = INTER // ways
        md = D // ways
        d["wup_in"] = np.ascontiguousarray(wup_full[:, :, ci * mu:(ci + 1) * mu])
        d["wgt_in"] = np.ascontiguousarray(wgt_full[:, :, ci * mu:(ci + 1) * mu])
        d["wdn_in"] = np.ascontiguousarray(wdn_full[:, :, ci * md:(ci + 1) * md])
        own = x_full[b, ch * t_own:(ch + 1) * t_own, :]
        d["x_in"] = np.ascontiguousarray(own.T)
        g0 = ch * t_own
        own_pos = np.arange(g0, g0 + t_own)
        cq, sq = _rope_tables(own_pos, 1.0 / np.sqrt(HD))
        ck, sk = _rope_tables(own_pos, 1.0)
        d["cosq"], d["sinq"] = cq, sq
        d["cosk"], d["sink"] = ck, sk
        hb = np.zeros((1, t_all), np.float32)
        n_inv = max(0, min(WINDOW, WINDOW - g0))
        if n_inv > 0:
            hb[0, :n_inv] = NEG
        d["hbias"] = hb
        d["prevrank"] = np.array([[(ci - 1) % max(len(cores), 1)]], np.int32)
        in_maps.append(d)
    return in_maps


def kernel(L0_attn_hidden_states, L0_attn_w_qkv, L0_attn_q_norm_w,
           L0_attn_k_norm_w, L0_attn_w_o, L0_ffn_SU_ug, L0_ffn_SU_dp,
           L0_ffn_Wscale_ug, L0_ffn_Wscale_dp, L0_ffn_W_up, L0_ffn_W_gate,
           L0_ffn_W_down):
    x = np.asarray(L0_attn_hidden_states, np.float32)
    B, S, _ = x.shape
    t_own = S // 4
    weights = {
        "w_qkv": L0_attn_w_qkv, "q_norm_w": L0_attn_q_norm_w,
        "k_norm_w": L0_attn_k_norm_w, "w_o": L0_attn_w_o,
        "SU_ug": L0_ffn_SU_ug, "SU_dp": L0_ffn_SU_dp,
        "Wscale_ug": L0_ffn_Wscale_ug, "Wscale_dp": L0_ffn_Wscale_dp,
        "W_up": L0_ffn_W_up, "W_gate": L0_ffn_W_gate, "W_down": L0_ffn_W_down,
    }
    in_maps = make_core_inputs(x, weights, t_own=t_own, ways=N_CORES)
    nc = build_program(t_own, ways=N_CORES)
    res = run_bass_kernel_spmd(nc, in_maps, core_ids=list(range(N_CORES)))
    out = np.empty((B, S, D), np.float32)
    ci = 0
    for b in range(B):
        for ch in range(4):
            out[b, ch * t_own:(ch + 1) * t_own, :] = \
                np.asarray(res.results[ci]["o_out"]).T
            ci += 1
    return out


# version 22 (final)
# speedup vs baseline: 1.0226x; 1.0001x over previous
import numpy as np
from contextlib import ExitStack

import concourse.bass as bass
import concourse.mybir as mybir
from concourse.bass_utils import run_bass_kernel_spmd

F32 = mybir.dt.float32
BF16 = mybir.dt.bfloat16
AF = mybir.ActivationFunctionType
ALU = mybir.AluOpType
AX = mybir.AxisListType

D = 2048
NH = 16
HD = 128
INTER = 8192
WINDOW = 1024
KD = D // 128
KI = INTER // 128
TT = 512
NEG = -1.0e30
LN_EPS = 1e-5
RMS_EPS = 1e-6
ROPE_THETA = 10000.0
N_CORES = 8


class Emitter:
    ENGS = ("pe", "act", "dve", "pool", "sp")
    NDMA = 8

    def __init__(self, nc, ctx):
        self.nc = nc
        self.ops = {e: [] for e in self.ENGS}
        self.sems = {}
        names = ["s_pe", "s_act", "s_dve", "s_pool", "s_sp"]
        names += [f"dh{i}" for i in range(self.NDMA)]
        names += [f"dg{i}" for i in range(self.NDMA)]
        names += ["ccs"]
        for name in names:
            self.sems[name] = ctx.enter_context(nc.semaphore(name))
        self.count = {name: 0 for name in self.sems}
        self.clock = {e: {name: 0 for name in self.sems} for e in self.ENGS}
        self.res = {}
        self.eng_sem = {"pe": "s_pe", "act": "s_act", "dve": "s_dve",
                        "pool": "s_pool", "sp": "s_sp"}
        self.dma_idx = {"sp": 0, "pool": 0}

    def _entry(self, key):
        e = self.res.get(key)
        if e is None:
            e = {"w": None, "r": []}
            self.res[key] = e
        return e

    def _emit(self, eng, builder, reads, writes, done_dep, extra_waits=()):
        waits = {}

        def need(dep):
            if dep is not None and waits.get(dep[0], 0) < dep[1]:
                waits[dep[0]] = dep[1]

        for dep in extra_waits:
            need(dep)
        for k in reads:
            need(self._entry(k)["w"])
        for k in writes:
            e = self._entry(k)
            need(e["w"])
            for rd in e["r"]:
                need(rd)
        ck = self.clock[eng]
        emitted = []
        for sem, val in waits.items():
            if eng == "pe" and sem == "s_pe":
                continue
            if ck[sem] < val:
                emitted.append((self.sems[sem], val))
                ck[sem] = val
        self.ops[eng].append((emitted, builder))
        for k in reads:
            self._entry(k)["r"].append(done_dep)
        for k in writes:
            e = self._entry(k)
            e["w"] = done_dep
            e["r"] = []

    def op(self, eng, builder, reads=(), writes=()):
        sem = self.eng_sem[eng]
        self.count[sem] += 1
        s, v = self.sems[sem], self.count[sem]

        def b(e, _b=builder, _s=s):
            _b(e).then_inc(_s, 1)

        self._emit(eng, b, reads, writes, (sem, self.count[sem]))

    def dma(self, queue, out, in_, reads=(), writes=()):
        pref = "dh" if queue == "sp" else "dg"
        dsem = f"{pref}{self.dma_idx[queue] % self.NDMA}"
        self.dma_idx[queue] += 1
        prev = self.count[dsem]
        self.count[dsem] = prev + 16
        s = self.sems[dsem]

        def b(e, _o=out, _i=in_, _s=s):
            e.dma_start(out=_o, in_=_i).then_inc(_s, 16)

        self._emit(queue, b, reads, writes, (dsem, prev + 16),
                   extra_waits=((dsem, prev),) if prev else ())

    def dma_custom(self, queue, builder_fn, reads=(), writes=()):
        pref = "dh" if queue == "sp" else "dg"
        dsem = f"{pref}{self.dma_idx[queue] % self.NDMA}"
        self.dma_idx[queue] += 1
        prev = self.count[dsem]
        self.count[dsem] = prev + 16
        s = self.sems[dsem]

        def b(e, _s=s):
            builder_fn(e).then_inc(_s, 16)

        self._emit(queue, b, reads, writes, (dsem, prev + 16),
                   extra_waits=((dsem, prev),) if prev else ())

    def collective(self, nc_fn, reads=(), writes=()):
        dsem = "ccs"
        if dsem not in self.sems:
            raise RuntimeError("ccs semaphore not allocated")
        self.count[dsem] += 1
        s, v = self.sems[dsem], self.count[dsem]

        def b(e, _s=s):
            nc_fn(e).then_inc(_s)

        self._emit("pool", b, reads, writes, (dsem, v))

    def barrier(self):
        snap = dict(self.count)
        for eng in self.ENGS:
            ck = self.clock[eng]
            emitted = []
            for sem, val in snap.items():
                if sem != self.eng_sem[eng] and ck[sem] < val:
                    emitted.append((self.sems[sem], val))
                    ck[sem] = val
            if emitted:
                self.ops[eng].append((emitted, None))
        self.res = {}

    def finalize(self, block):
        def mk(eng_name):
            ops = self.ops[eng_name]

            def body(e):
                for waits, builder in ops:
                    for s, v in waits:
                        e.wait_ge(s, v)
                    if builder is not None:
                        builder(e)
            return body
        block.tensor(mk("pe"))
        block.scalar(mk("act"))
        block.vector(mk("dve"))
        block.gpsimd(mk("pool"))
        block.sync(mk("sp"))


def build_program(t_own, ways=N_CORES):
    t_all = t_own + WINDOW
    n_tiles = t_all // TT
    n_own_tiles = t_own // TT
    own0 = WINDOW

    nc = bass.Bass("TRN2", target_bir_lowering=False)

    def din(name, shape, dt=F32):
        return nc.dram_tensor(name, shape, dt, kind="ExternalInput").ap()

    x_in = din("x_in", [D, t_own])
    wup_in = din("wup_in", [KD, 128, INTER // ways])
    wgt_in = din("wgt_in", [KD, 128, INTER // ways])
    wqkv_in = din("wqkv_in", [3 * KD, 128, KD * 128])
    wo_in = din("wo_in", [KD, 128, KD * 128])
    wdn_in = din("wdn_in", [KI, 128, D // ways])
    su_ug = din("su_ug", [128, KD])
    su_dp = din("su_dp", [128, KI])
    wsc_u = din("wsc_u", [128, KI])
    wsc_g = din("wsc_g", [128, KI])
    wsc_dp = din("wsc_dp", [128, KD])
    qnw = din("qnw", [128, KD])
    knw = din("knw", [128, KD])
    cosq = din("cosq", [128, t_own])
    sinq = din("sinq", [128, t_own])
    cosk = din("cosk", [128, t_own])
    sink = din("sink", [128, t_own])
    prevrank = din("prevrank", [1, 1], mybir.dt.int32)
    hbias = din("hbias", [1, t_all])
    h128 = din("h128", [128, 128])
    ident = din("ident", [128, 128])
    pswap = din("pswap", [128, 128])
    mask_c = din("mask_c", [128, 128])
    mask_a = din("mask_a", [128, 128])

    o_out = nc.dram_tensor("o_out", [D, t_own], F32, kind="ExternalOutput").ap()

    def dscr(name, shape, dt):
        return nc.dram_tensor(name, shape, dt, kind="Internal").ap()

    scr_up = dscr("scr_up", [KI, 128, KD * 128], BF16)
    scr_gt = dscr("scr_gt", [KI, 128, KD * 128], BF16)
    if ways > 1:
        sup_my = dscr("sup_my", [KI // ways, 128, KD * 128], BF16)
        sgt_my = dscr("sgt_my", [KI // ways, 128, KD * 128], BF16)
        sdn_my = dscr("sdn_my", [KD // ways, 128, KI * 128], BF16)
    scr_qkv = dscr("scr_qkv", [3 * KD, 128, KD * 128], BF16)
    scr_dn = dscr("scr_dn", [KD, 128, KI * 128], BF16)
    x1_scr = dscr("x1_scr", [D, t_own], F32)
    ktail = dscr("ktail", [NH, 128, WINDOW], BF16)
    vtail = dscr("vtail", [NH, 128, WINDOW // 128, 128], BF16)
    kgat_h = nc.dram_tensor("kgat", [ways, NH, 128, WINDOW], BF16,
                            kind="Internal")
    vgat_h = nc.dram_tensor("vgat", [ways, NH, 128, WINDOW // 128, 128],
                            BF16, kind="Internal")
    kgat = kgat_h.ap()
    vgat = vgat_h.ap()
    x2_scr = dscr("x2_scr", [D, t_own], F32)
    q_scr = dscr("q_scr", [128, NH, t_own], BF16)
    k_scr = dscr("k_scr", [NH, 128, t_all], BF16)
    v_scr = dscr("v_scr", [NH, 128, t_all // 128, 128], BF16)

    with ExitStack() as top:
        em = Emitter(nc, top)
        tsb = lambda n, s, d: top.enter_context(nc.sbuf_tensor(n, s, d))

        h128b = tsb("h128b", [128, 128], BF16)
        identb = tsb("identb", [128, 128], BF16)
        pswapb = tsb("pswapb", [128, 128], BF16)
        maskcb = tsb("maskcb", [128, 128], BF16)
        maskab = tsb("maskab", [128, 128], BF16)
        hbiasb = tsb("hbiasb", [1, t_all], BF16)
        onesb = tsb("onesb", [128, 1], BF16)
        onesrow = tsb("onesrow", [1, 128], F32)
        onesrb = tsb("onesrb", [1, 128], BF16)
        su_ug_s = tsb("su_ug_s", [128, KD], F32)
        su_dp_s = tsb("su_dp_s", [128, KI], F32)
        wsc_u_s = tsb("wsc_u_s", [128, KI], F32)
        wsc_g_s = tsb("wsc_g_s", [128, KI], F32)
        wsc_dp_s = tsb("wsc_dp_s", [128, KD], F32)
        qnw_s = tsb("qnw_s", [128, KD], F32)
        knw_s = tsb("knw_s", [128, KD], F32)

        for buf, src, key in ((h128b, h128, "h128b"), (identb, ident, "identb"),
                              (pswapb, pswap, "pswapb"), (maskcb, mask_c, "maskcb"),
                              (maskab, mask_a, "maskab"), (hbiasb, hbias, "hbiasb")):
            em.dma("pool", buf[:, :], src[:, :], writes=[key])
        for buf, src, key in ((su_ug_s, su_ug, "su_ug_s"), (su_dp_s, su_dp, "su_dp_s"),
                              (wsc_u_s, wsc_u, "wsc_u_s"), (wsc_g_s, wsc_g, "wsc_g_s"),
                              (wsc_dp_s, wsc_dp, "wsc_dp_s"), (qnw_s, qnw, "qnw_s"),
                              (knw_s, knw, "knw_s")):
            em.dma("sp", buf[:, :], src[:, :], writes=[key])
        em.op("dve", lambda e: e.memset(onesb[:, :], 1.0), writes=["onesb"])
        zerot = tsb("zerot", [128, WINDOW], BF16)
        em.op("dve", lambda e: e.memset(zerot[:, :], 0.0), writes=["zerot"])
        for h_ in range(NH):
            em.dma("sp", k_scr[h_, :, 0:WINDOW], zerot[:, :],
                   reads=["zerot"], writes=["k_scr"])
            em.dma("sp",
                   v_scr[h_, :, 0:WINDOW // 128, :].rearrange(
                       "b c d -> b (c d)"),
                   zerot[:, :], reads=["zerot"], writes=["v_scr"])
        em.op("dve", lambda e: e.memset(onesrow[:, :], 1.0), writes=["onesrow"])
        em.op("dve", lambda e: e.memset(onesrb[:, :], 1.0), writes=["onesrb"])
        em.op("dve", lambda e: e.tensor_scalar_mul(
            out=su_ug_s[:, :], in0=su_ug_s[:, :], scalar1=float(1.0 / np.sqrt(D))),
            reads=["su_ug_s"], writes=["su_ug_s"])
        em.op("dve", lambda e: e.tensor_scalar_mul(
            out=su_dp_s[:, :], in0=su_dp_s[:, :], scalar1=float(1.0 / np.sqrt(INTER))),
            reads=["su_dp_s"], writes=["su_dp_s"])
        em.op("dve", lambda e: e.tensor_scalar_mul(
            out=wsc_dp_s[:, :], in0=wsc_dp_s[:, :], scalar1=0.5),
            reads=["wsc_dp_s"], writes=["wsc_dp_s"])
        comb_us = tsb("comb_us", [128, KI], F32)
        em.op("dve", lambda e: e.tensor_mul(
            out=comb_us[:, :], in0=wsc_u_s[:, :], in1=su_dp_s[:, :]),
            reads=["wsc_u_s", "su_dp_s"], writes=["comb_us"])

        with ExitStack() as ph:
            psb = lambda n, s, d: ph.enter_context(nc.sbuf_tensor(n, s, d))
            cb = psb("p0cb", [128, 64 * 256], BF16)
            pbuf = psb("p0pb", [128, 64 * 256], BF16)
            wst = [psb(f"p0wst{i}", [128, KI * 128], BF16) for i in range(4)]
            ps0 = [ph.enter_context(nc.psum_tensor(f"p0ps{i}", [128, 512], F32))
                   for i in range(2)]

            def transform(src_ap, nt, span, scale_ap, scale_key, scr, scr_pref,
                          m_total):
                nmc = span // 128
                for s in range(m_total // span):
                    for i in range(nt):
                        em.dma("pool", cb[:, i * span:(i + 1) * span],
                               src_ap[i, :, s * span:(s + 1) * span],
                               writes=[f"p0cb{i}"])
                    bufs = {"p0cb": cb, "p0pb": pbuf}
                    srck, dstk = "p0cb", "p0pb"
                    h = 1
                    while h < nt:
                        run = h * span
                        for g in range(0, nt, 2 * h):
                            a0, b0 = g * span, (g + h) * span
                            sa = bufs[srck][:, a0:a0 + run]
                            sbb = bufs[srck][:, b0:b0 + run]
                            oa = bufs[dstk][:, a0:a0 + run]
                            ob = bufs[dstk][:, b0:b0 + run]
                            rks = [f"{srck}{i}" for i in range(g, g + 2 * h)]
                            em.op("dve", lambda e, oa=oa, sa=sa, sbb=sbb:
                                  e.tensor_add(out=oa, in0=sa, in1=sbb),
                                  reads=rks,
                                  writes=[f"{dstk}{i}"
                                          for i in range(g, g + h)])
                            em.op("dve", lambda e, ob=ob, sa=sa, sbb=sbb:
                                  e.tensor_sub(out=ob, in0=sa, in1=sbb),
                                  reads=rks,
                                  writes=[f"{dstk}{i}"
                                          for i in range(g + h, g + 2 * h)])
                        srck, dstk = dstk, srck
                        h *= 2
                    for i in range(nt):
                        ps = ps0[i % 2]
                        pk = f"p0ps{i % 2}"
                        em.op("pe", lambda e, ps=ps, span=span,
                              r=bufs[srck][:, i * span:(i + 1) * span]:
                              e.matmul(ps[:, 0:span], lhsT=h128b[:, :], rhs=r,
                                       start=True, stop=True),
                              reads=[f"{srck}{i}", "h128b"], writes=[pk])
                        for c in range(nmc):
                            dst = wst[c][:, i * 128:(i + 1) * 128]
                            if scale_ap is None:
                                em.op("act", lambda e, ps=ps, dst=dst, c=c:
                                      e.copy(out=dst, in_=ps[:, c * 128:
                                                            (c + 1) * 128]),
                                      reads=[pk], writes=[f"p0w{c}_{i}"])
                            else:
                                em.op("dve", lambda e, ps=ps, dst=dst, c=c, i=i:
                                      e.tensor_scalar_mul(
                                          out=dst, in0=ps[:, c * 128:(c + 1) * 128],
                                          scalar1=scale_ap[:, i:i + 1]),
                                      reads=[pk, scale_key],
                                      writes=[f"p0w{c}_{i}"])
                    for c in range(nmc):
                        mc = s * nmc + c
                        em.dma("pool", scr[mc, :, 0:nt * 128],
                               wst[c][:, 0:nt * 128],
                               reads=[f"p0w{c}_{i}" for i in range(nt)],
                               writes=[f"{scr_pref}{mc}"])

            if ways == 1:
                transform(wup_in, KD, 512, su_ug_s, "su_ug_s", scr_up, "scrup",
                          INTER)
                transform(wgt_in, KD, 512, su_ug_s, "su_ug_s", scr_gt, "scrgt",
                          INTER)
                transform(wdn_in, KI, 256, None, None, scr_dn, "scrdn", D)
            else:
                rg = [list(range(ways))]
                nmu = KI // ways
                nmd = KD // ways
                transform(wup_in, KD, 512, su_ug_s, "su_ug_s", sup_my, "supmy",
                          INTER // ways)
                em.collective(
                    lambda e: e.collective_compute(
                        "AllGather", ALU.bypass, replica_groups=rg,
                        ins=[sup_my.rearrange("a b c -> (a b) c").opt()],
                        outs=[scr_up.rearrange("a b c -> (a b) c").opt()]),
                    reads=[f"supmy{i}" for i in range(nmu)],
                    writes=[f"scrup{m}" for m in range(KI)])
                transform(wgt_in, KD, 512, su_ug_s, "su_ug_s", sgt_my, "sgtmy",
                          INTER // ways)
                em.collective(
                    lambda e: e.collective_compute(
                        "AllGather", ALU.bypass, replica_groups=rg,
                        ins=[sgt_my.rearrange("a b c -> (a b) c").opt()],
                        outs=[scr_gt.rearrange("a b c -> (a b) c").opt()]),
                    reads=[f"sgtmy{i}" for i in range(nmu)],
                    writes=[f"scrgt{m}" for m in range(KI)])
                transform(wdn_in, KI, 256, None, None, sdn_my, "sdnmy", D // ways)
                em.collective(
                    lambda e: e.collective_compute(
                        "AllGather", ALU.bypass, replica_groups=rg,
                        ins=[sdn_my.rearrange("a b c -> (a b) c").opt()],
                        outs=[scr_dn.rearrange("a b c -> (a b) c").opt()]),
                    reads=[f"sdnmy{i}" for i in range(nmd)],
                    writes=[f"scrdn{m}" for m in range(KD)])
            em.barrier()

        def ffn_pass(tag, src_ap, src_key, dst_ap, dst_key, tiles):
            with ExitStack() as ph:
                psb = lambda n, s, d: ph.enter_context(nc.sbuf_tensor(tag + n, s, d))
                xbf = psb("xbf", [128, KD * TT], BF16)
                sqs = psb("sqs", [128, 2 * TT], BF16)
                xcs = psb("xcs", [128, 2 * TT], F32)
                rows = psb("rows", [1, 3 * TT], F32)
                m_b = psb("m_b", [128, TT], F32)
                r_b = psb("r_b", [128, TT], F32)
                xf = psb("xf", [128, KD * TT], BF16)
                yb = psb("yb", [128, KI * TT], BF16)
                gst = psb("gst", [128, 2 * TT], F32)
                wub = psb("wub", [128, 4 * KD * 128], BF16)
                wgb = psb("wgb", [128, 4 * KD * 128], BF16)
                wdb = psb("wdb", [128, 2 * KI * 128], BF16)
                xrs = psb("xrs", [128, 2 * TT], F32)
                x1s = psb("x1s", [128, 2 * TT], F32)
                ps_s1 = ph.enter_context(nc.psum_tensor(tag + "ps_s1", [1, TT], F32))
                ps_s2 = ph.enter_context(nc.psum_tensor(tag + "ps_s2", [1, TT], F32))
                ps_bc = ph.enter_context(nc.psum_tensor(tag + "ps_bc", [128, TT], F32))
                ps_u = [ph.enter_context(
                    nc.psum_tensor(tag + f"ps_u{i}", [128, TT], F32)) for i in range(2)]
                ps_g = [ph.enter_context(
                    nc.psum_tensor(tag + f"ps_g{i}", [128, TT], F32)) for i in range(2)]

                def prologue_pieces(t):
                    c0 = t * TT
                    loads, pieces = [], []
                    for i0 in range(0, KD, 8):
                        def pld(i0=i0, c0=c0):
                            em.dma("pool",
                                   xbf[:, i0 * TT:(i0 + 8) * TT].rearrange(
                                       "p (blk c) -> p blk c", blk=8),
                                   src_ap[i0 * 128:(i0 + 8) * 128, c0:c0 + TT]
                                   .rearrange("(blk p) c -> p blk c", p=128),
                                   reads=[src_key],
                                   writes=[f"xbf{i}" for i in range(i0, i0 + 8)])
                        loads.append(pld)
                    for i in range(KD):
                        def pst(i=i):
                            sl = i % 2
                            sq = sqs[:, sl * TT:(sl + 1) * TT]
                            em.op("act", lambda e, sq=sq, i=i: e.activation(
                                out=sq, in_=xbf[:, i * TT:(i + 1) * TT],
                                func=AF.Square),
                                reads=[f"xbf{i}"], writes=[f"sqs{sl}"])
                            em.op("pe", lambda e, i=i: e.matmul(
                                ps_s1[:, :], lhsT=onesb[:, :],
                                rhs=xbf[:, i * TT:(i + 1) * TT],
                                start=(i == 0), stop=(i == KD - 1),
                                skip_group_check=True),
                                reads=[f"xbf{i}", "onesb"], writes=["ps_s1"])
                            em.op("pe", lambda e, sq=sq, i=i: e.matmul(
                                ps_s2[:, :], lhsT=onesb[:, :], rhs=sq,
                                start=(i == 0), stop=(i == KD - 1),
                                skip_group_check=True),
                                reads=[f"sqs{sl}", "onesb"], writes=["ps_s2"])
                        pieces.append(pst)

                    def prows():
                        mr, vr = rows[:, 0:TT], rows[:, TT:2 * TT]
                        tr = rows[:, 2 * TT:3 * TT]
                        rr = mr
                        em.op("dve", lambda e: e.tensor_scalar_mul(
                            out=mr, in0=ps_s1[:, :], scalar1=1.0 / D),
                            reads=["ps_s1"], writes=["row_m"])
                        em.op("dve", lambda e: e.tensor_scalar_mul(
                            out=vr, in0=ps_s2[:, :], scalar1=1.0 / D),
                            reads=["ps_s2"], writes=["row_v"])
                        em.op("dve", lambda e: e.tensor_mul(out=tr, in0=mr, in1=mr),
                              reads=["row_m"], writes=["row_t"])
                        em.op("dve", lambda e: e.tensor_sub(out=vr, in0=vr, in1=tr),
                              reads=["row_v", "row_t"], writes=["row_v"])
                        em.op("dve", lambda e: e.tensor_scalar_add(
                            out=vr, in0=vr, scalar1=LN_EPS),
                            reads=["row_v"], writes=["row_v"])
                        em.op("dve", lambda e: e.reciprocal(out=tr, in_=vr),
                              reads=["row_v"], writes=["row_t"])
                        em.op("pe", lambda e: e.matmul(
                            ps_bc[:, :], lhsT=onesrow[:, :], rhs=mr,
                            start=True, stop=True),
                            reads=["row_m", "onesrow"], writes=["ps_bc"])
                        em.op("dve", lambda e: e.tensor_copy(
                            out=m_b[:, :], in_=ps_bc[:, :]),
                            reads=["ps_bc"], writes=["m_b"])
                        em.op("act", lambda e: e.activation(
                            out=rr, in_=tr, func=AF.Sqrt),
                            reads=["row_t"], writes=["row_m"])
                        em.op("pe", lambda e: e.matmul(
                            ps_bc[:, :], lhsT=onesrow[:, :], rhs=rr,
                            start=True, stop=True),
                            reads=["row_m", "onesrow"], writes=["ps_bc"])
                        em.op("dve", lambda e: e.tensor_copy(
                            out=r_b[:, :], in_=ps_bc[:, :]),
                            reads=["ps_bc"], writes=["r_b"])
                    pieces.append(prows)
                    for i in range(KD):
                        def papp(i=i):
                            sl = i % 2
                            xc = xcs[:, sl * TT:(sl + 1) * TT]
                            em.op("dve", lambda e, xc=xc, i=i: e.tensor_sub(
                                out=xc, in0=xbf[:, i * TT:(i + 1) * TT],
                                in1=m_b[:, :]),
                                reads=[f"xbf{i}", "m_b"], writes=[f"xcs{sl}"])
                            em.op("dve", lambda e, xc=xc, i=i: e.tensor_mul(
                                out=xf[:, i * TT:(i + 1) * TT], in0=xc,
                                in1=r_b[:, :]),
                                reads=[f"xcs{sl}", "r_b"], writes=[f"xf{i}"])
                        pieces.append(papp)
                    return loads, pieces

                def emit_upgate(t):
                    for m in range(KI):
                        sl = m % 4
                        wu = wub[:, sl * KD * 128:(sl + 1) * KD * 128]
                        wg = wgb[:, sl * KD * 128:(sl + 1) * KD * 128]
                        wuk, wgk = f"wub{sl}", f"wgb{sl}"
                        em.dma("sp", wu, scr_up[m, :, :], reads=[f"scrup{m}"],
                               writes=[wuk])
                        em.dma("sp", wg, scr_gt[m, :, :], reads=[f"scrgt{m}"],
                               writes=[wgk])
                        psu, psg = ps_u[m % 2], ps_g[m % 2]
                        psuk, psgk = f"ps_u{m % 2}", f"ps_g{m % 2}"
                        for k in range(KD):
                            em.op("pe", lambda e, psu=psu, wu=wu, k=k: e.matmul(
                                psu[:, :], lhsT=wu[:, k * 128:(k + 1) * 128],
                                rhs=xf[:, k * TT:(k + 1) * TT],
                                start=(k == 0), stop=(k == KD - 1)),
                                reads=[wuk, f"xf{k}"], writes=[psuk])
                        for k in range(KD):
                            em.op("pe", lambda e, psg=psg, wg=wg, k=k: e.matmul(
                                psg[:, :], lhsT=wg[:, k * 128:(k + 1) * 128],
                                rhs=xf[:, k * TT:(k + 1) * TT],
                                start=(k == 0), stop=(k == KD - 1)),
                                reads=[wgk, f"xf{k}"], writes=[psgk])
                        gs = gst[:, (m % 2) * TT:(m % 2 + 1) * TT]
                        em.op("act", lambda e, psg=psg, gs=gs, m=m: e.activation(
                            out=gs, in_=psg[:, :], func=AF.Silu,
                            scale=wsc_g_s[:, m:m + 1]),
                            reads=[psgk, "wsc_g_s"], writes=[f"gst{m % 2}"])
                        em.op("dve", lambda e, psu=psu, gs=gs, m=m:
                              e.scalar_tensor_tensor(
                                  out=yb[:, m * TT:(m + 1) * TT], in0=psu[:, :],
                                  scalar=comb_us[:, m:m + 1], in1=gs,
                                  op0=ALU.mult, op1=ALU.mult),
                              reads=[psuk, f"gst{m % 2}", "comb_us"],
                              writes=[f"yb{m}"])

                def down_pieces(t):
                    c0 = t * TT
                    pieces = []
                    for m in range(KD):
                        def pdn(m=m, c0=c0):
                            sl = m % 2
                            wd = wdb[:, sl * KI * 128:(sl + 1) * KI * 128]
                            wdk = f"wdb{sl}"
                            em.dma("sp", wd, scr_dn[m, :, :], reads=[f"scrdn{m}"],
                                   writes=[wdk])
                            xr = xrs[:, sl * TT:(sl + 1) * TT]
                            em.dma("sp", xr,
                                   src_ap[m * 128:(m + 1) * 128, c0:c0 + TT],
                                   reads=[src_key], writes=[f"xrs{sl}"])
                            psd = ps_u[sl]
                            psdk = f"ps_u{sl}"
                            for k in range(KI):
                                em.op("pe", lambda e, psd=psd, wd=wd, k=k: e.matmul(
                                    psd[:, :], lhsT=wd[:, k * 128:(k + 1) * 128],
                                    rhs=yb[:, k * TT:(k + 1) * TT],
                                    start=(k == 0), stop=(k == KI - 1)),
                                    reads=[wdk, f"yb{k}"], writes=[psdk])
                            x1c = x1s[:, sl * TT:(sl + 1) * TT]
                            em.op("dve", lambda e, psd=psd, x1c=x1c, xr=xr, m=m:
                                  e.scalar_tensor_tensor(
                                      out=x1c, in0=psd[:, :],
                                      scalar=wsc_dp_s[:, m:m + 1], in1=xr,
                                      op0=ALU.mult, op1=ALU.add),
                                  reads=[psdk, f"xrs{sl}", "wsc_dp_s"],
                                  writes=[f"x1s{sl}"])
                            em.dma("pool",
                                   dst_ap[m * 128:(m + 1) * 128, c0:c0 + TT], x1c,
                                   reads=[f"x1s{sl}"], writes=[dst_key])
                        pieces.append(pdn)
                    return pieces

                prev_down = None
                for t in range(tiles):
                    loads, pro = prologue_pieces(t)
                    for p in loads:
                        p()
                    if prev_down is None:
                        for p in pro:
                            p()
                    else:
                        nd = len(prev_down)
                        npr = len(pro)
                        j = 0
                        for i, pd in enumerate(prev_down):
                            pd()
                            jn = (i + 1) * npr // nd
                            while j < jn:
                                pro[j]()
                                j += 1
                    emit_upgate(t)
                    prev_down = down_pieces(t)
                for p in prev_down:
                    p()
                em.barrier()

        ffn_pass("a1", x_in, "x_in", x1_scr, "x1_scr", n_own_tiles)

        with ExitStack() as ph:
            psb = lambda n, s, d: ph.enter_context(nc.sbuf_tensor("a2" + n, s, d))
            x1bf = psb("x1bf", [128, 2 * KD * TT], BF16)
            sqs = psb("sqs", [128, 2 * TT], BF16)
            xn2 = psb("xn2", [128, 2 * KD * TT], BF16)
            qf = psb("qf", [128, KD * TT], BF16)
            kf = psb("kf", [128, KD * TT], BF16)
            cq2 = psb("cq2", [128, TT], F32)
            sq2 = psb("sq2", [128, TT], F32)
            ck2 = psb("ck2", [128, TT], F32)
            sk2 = psb("sk2", [128, TT], F32)
            rows = psb("rows", [1, 6 * TT], F32)
            m_b = psb("m_b", [128, TT], F32)
            r_b = psb("r_b", [128, TT], F32)
            qr_b = psb("qr_b", [128, TT], F32)
            kr_b = psb("kr_b", [128, TT], F32)
            wqb = psb("wqb", [128, 4 * KD * 128], BF16)
            cqt = psb("cqt", [128, 2 * TT], F32)
            sqt = psb("sqt", [128, 2 * TT], F32)
            ckt = psb("ckt", [128, 2 * TT], F32)
            skt = psb("skt", [128, 2 * TT], F32)
            uu = psb("uu", [128, TT], F32)
            ub = psb("ub", [128, TT], BF16)
            swf = psb("swf", [128, TT], F32)
            aa = psb("aa", [128, TT], F32)
            bb2 = psb("bb2", [128, TT], F32)
            qkb = psb("qkb", [128, 2 * TT], BF16)
            vbs = psb("vbs", [128, TT], BF16)
            vts = psb("vts", [128, 2 * 4 * 128], BF16)
            ps_qk = [ph.enter_context(nc.psum_tensor(f"a2ps{i}", [128, TT], F32))
                     for i in range(2)]
            ps_sq = ph.enter_context(nc.psum_tensor("a2psq", [1, TT], F32))
            ps_sk = ph.enter_context(nc.psum_tensor("a2psk", [1, TT], F32))
            ps_bc = ph.enter_context(nc.psum_tensor("a2psbc", [128, TT], F32))
            ps_sw = ph.enter_context(nc.psum_tensor("a2pssw", [128, TT], F32))
            ps_tr = [ph.enter_context(nc.psum_tensor(f"a2pstr{i}", [128, 128], F32))
                     for i in range(2)]

            def a2_loads(t):
                c0 = t * TT
                xsl = (t % 2) * KD * TT
                tsl = (t % 2) * TT
                for i0 in range(0, KD, 8):
                    em.dma("pool",
                           x1bf[:, xsl + i0 * TT:xsl + (i0 + 8) * TT].rearrange(
                               "p (blk c) -> p blk c", blk=8),
                           x1_scr[i0 * 128:(i0 + 8) * 128, c0:c0 + TT]
                           .rearrange("(blk p) c -> p blk c", p=128),
                           reads=["x1_scr"],
                           writes=[f"x1bf{t % 2}_{i}" for i in range(i0, i0 + 8)])
                em.dma("sp", ckt[:, tsl:tsl + TT], cosk[:, c0:c0 + TT],
                       writes=[f"ckt{t % 2}"])
                em.dma("sp", skt[:, tsl:tsl + TT], sink[:, c0:c0 + TT],
                       writes=[f"skt{t % 2}"])
                em.dma("sp", cqt[:, tsl:tsl + TT], cosq[:, c0:c0 + TT],
                       writes=[f"cqt{t % 2}"])
                em.dma("sp", sqt[:, tsl:tsl + TT], sinq[:, c0:c0 + TT],
                       writes=[f"sqt{t % 2}"])

            def a2_prologue_pieces(t):
                xsl = (t % 2) * KD * TT
                xk = lambda i: f"x1bf{t % 2}_{i}"
                pieces = []
                for i in range(KD):
                    def pst(i=i):
                        sl = i % 2
                        sq = sqs[:, sl * TT:(sl + 1) * TT]
                        xb_ = x1bf[:, xsl + i * TT:xsl + (i + 1) * TT]
                        em.op("act", lambda e, xb_=xb_, sq=sq: e.activation(
                            out=sq, in_=xb_, func=AF.Square),
                            reads=[xk(i)], writes=[f"sqs{sl}"])
                        em.op("pe", lambda e, xb_=xb_, i=i: e.matmul(
                            ps_sq[:, :], lhsT=onesb[:, :], rhs=xb_,
                            start=(i == 0), stop=(i == KD - 1),
                            skip_group_check=True),
                            reads=[xk(i), "onesb"], writes=["ps_sq"])
                        em.op("pe", lambda e, sq=sq, i=i: e.matmul(
                            ps_sk[:, :], lhsT=onesb[:, :], rhs=sq,
                            start=(i == 0), stop=(i == KD - 1),
                            skip_group_check=True),
                            reads=[f"sqs{sl}", "onesb"], writes=["ps_sk"])
                    pieces.append(pst)

                def prows():
                    mr, vr = rows[:, 0:TT], rows[:, TT:2 * TT]
                    tr = rows[:, 2 * TT:3 * TT]
                    rr = mr
                    em.op("dve", lambda e: e.tensor_scalar_mul(
                        out=mr, in0=ps_sq[:, :], scalar1=1.0 / D),
                        reads=["ps_sq"], writes=["row_m"])
                    em.op("dve", lambda e: e.tensor_scalar_mul(
                        out=vr, in0=ps_sk[:, :], scalar1=1.0 / D),
                        reads=["ps_sk"], writes=["row_v"])
                    em.op("dve", lambda e: e.tensor_mul(out=tr, in0=mr, in1=mr),
                          reads=["row_m"], writes=["row_t"])
                    em.op("dve", lambda e: e.tensor_sub(out=vr, in0=vr, in1=tr),
                          reads=["row_v", "row_t"], writes=["row_v"])
                    em.op("dve", lambda e: e.tensor_scalar_add(
                        out=vr, in0=vr, scalar1=LN_EPS),
                        reads=["row_v"], writes=["row_v"])
                    em.op("dve", lambda e: e.reciprocal(out=tr, in_=vr),
                          reads=["row_v"], writes=["row_t"])
                    em.op("pe", lambda e: e.matmul(
                        ps_bc[:, :], lhsT=onesrow[:, :], rhs=mr,
                        start=True, stop=True),
                        reads=["row_m", "onesrow"], writes=["ps_bc"])
                    em.op("dve", lambda e: e.tensor_copy(
                        out=m_b[:, :], in_=ps_bc[:, :]),
                        reads=["ps_bc"], writes=["m_b"])
                    em.op("act", lambda e: e.activation(
                        out=rr, in_=tr, func=AF.Sqrt),
                        reads=["row_t"], writes=["row_m"])
                    em.op("pe", lambda e: e.matmul(
                        ps_bc[:, :], lhsT=onesrow[:, :], rhs=rr,
                        start=True, stop=True),
                        reads=["row_m", "onesrow"], writes=["ps_bc"])
                    em.op("dve", lambda e: e.tensor_copy(
                        out=r_b[:, :], in_=ps_bc[:, :]),
                        reads=["ps_bc"], writes=["r_b"])
                pieces.append(prows)
                for i in range(KD):
                    def papp(i=i):
                        xb_ = x1bf[:, xsl + i * TT:xsl + (i + 1) * TT]
                        em.op("dve", lambda e, xb_=xb_: e.tensor_sub(
                            out=uu[:, :], in0=xb_, in1=m_b[:, :]),
                            reads=[xk(i), "m_b"], writes=["uu"])
                        em.op("dve", lambda e, i=i: e.tensor_mul(
                            out=xn2[:, xsl + i * TT:xsl + (i + 1) * TT],
                            in0=uu[:, :], in1=r_b[:, :]),
                            reads=["uu", "r_b"], writes=[f"xn2{t % 2}_{i}"])
                    pieces.append(papp)
                return pieces

            def qkv_mm(t, m):
                xsl = (t % 2) * KD * TT
                sl = m % 4
                wq = wqb[:, sl * KD * 128:(sl + 1) * KD * 128]
                wqk = f"wqb{sl}"
                if t == 0:
                    em.dma("pool", wq, wqkv_in[m, :, :], writes=[wqk])
                    em.dma("pool", scr_qkv[m, :, :], wq, reads=[wqk],
                           writes=[f"scrqkv{m}"])
                else:
                    em.dma("sp", wq, scr_qkv[m, :, :], reads=[f"scrqkv{m}"],
                           writes=[wqk])
                ps = ps_qk[m % 2]
                psk = f"ps_qk{m % 2}"
                for k in range(KD):
                    em.op("pe", lambda e, ps=ps, wq=wq, k=k, xsl=xsl: e.matmul(
                        ps[:, :], lhsT=wq[:, k * 128:(k + 1) * 128],
                        rhs=xn2[:, xsl + k * TT:xsl + (k + 1) * TT],
                        start=(k == 0), stop=(k == KD - 1)),
                        reads=[wqk, f"xn2{t % 2}_{k}"], writes=[psk])
                return ps, psk, m % 2

            def qk_section(t):
                has_q = True
                for m in range(2 * KD):
                    ps, psk, sl = qkv_mm(t, m)
                    if m < KD:
                        if not has_q:
                            continue
                        em.op("act", lambda e, ps=ps, m=m: e.copy(
                            out=qf[:, m * TT:(m + 1) * TT], in_=ps[:, :]),
                            reads=[psk], writes=[f"qf{m}"])
                        sq = sqs[:, sl * TT:(sl + 1) * TT]
                        em.op("act", lambda e, ps=ps, sq=sq: e.activation(
                            out=sq, in_=ps[:, :], func=AF.Square),
                            reads=[psk], writes=[f"sqs{sl}"])
                        em.op("pe", lambda e, sq=sq, m=m: e.matmul(
                            ps_sq[:, :], lhsT=onesb[:, :], rhs=sq,
                            start=(m == 0), stop=(m == KD - 1),
                            skip_group_check=True),
                            reads=[f"sqs{sl}", "onesb"], writes=["ps_sq"])
                    else:
                        mk = m - KD
                        em.op("act", lambda e, ps=ps, mk=mk: e.copy(
                            out=kf[:, mk * TT:(mk + 1) * TT], in_=ps[:, :]),
                            reads=[psk], writes=[f"kf{mk}"])
                        sq = sqs[:, sl * TT:(sl + 1) * TT]
                        em.op("act", lambda e, ps=ps, sq=sq: e.activation(
                            out=sq, in_=ps[:, :], func=AF.Square),
                            reads=[psk], writes=[f"sqs{sl}"])
                        em.op("pe", lambda e, sq=sq, mk=mk: e.matmul(
                            ps_sk[:, :], lhsT=onesb[:, :], rhs=sq,
                            start=(mk == 0), stop=(mk == KD - 1),
                            skip_group_check=True),
                            reads=[f"sqs{sl}", "onesb"], writes=["ps_sk"])
                qrr, krr = rows[:, 3 * TT:4 * TT], rows[:, 4 * TT:5 * TT]
                t2r = rows[:, 5 * TT:6 * TT]

                def rms_row(out_r, ps_src, ps_key, okey):
                    em.op("dve", lambda e, out_r=out_r, ps_src=ps_src:
                          e.tensor_scalar_mul(out=t2r, in0=ps_src[:, :],
                                              scalar1=1.0 / D),
                          reads=[ps_key], writes=["row_t2"])
                    em.op("dve", lambda e: e.tensor_scalar_add(
                        out=t2r, in0=t2r, scalar1=RMS_EPS),
                        reads=["row_t2"], writes=["row_t2"])
                    em.op("dve", lambda e, out_r=out_r: e.reciprocal(
                        out=out_r, in_=t2r),
                        reads=["row_t2"], writes=[okey])
                    em.op("act", lambda e, out_r=out_r: e.activation(
                        out=out_r, in_=out_r, func=AF.Sqrt),
                        reads=[okey], writes=[okey])

                tsl = (t % 2) * TT
                if has_q:
                    rms_row(qrr, ps_sq, "ps_sq", "row_q")
                    em.op("pe", lambda e: e.matmul(
                        ps_bc[:, :], lhsT=onesrow[:, :], rhs=qrr,
                        start=True, stop=True),
                        reads=["row_q", "onesrow"], writes=["ps_bc"])
                    em.op("dve", lambda e: e.tensor_copy(
                        out=qr_b[:, :], in_=ps_bc[:, :]),
                        reads=["ps_bc"], writes=["qr_b"])
                    em.op("dve", lambda e, tsl=tsl: e.tensor_mul(
                        out=cq2[:, :], in0=cqt[:, tsl:tsl + TT], in1=qr_b[:, :]),
                        reads=[f"cqt{t % 2}", "qr_b"], writes=["cq2"])
                    em.op("dve", lambda e, tsl=tsl: e.tensor_mul(
                        out=sq2[:, :], in0=sqt[:, tsl:tsl + TT], in1=qr_b[:, :]),
                        reads=[f"sqt{t % 2}", "qr_b"], writes=["sq2"])
                rms_row(krr, ps_sk, "ps_sk", "row_k")
                em.op("pe", lambda e: e.matmul(
                    ps_bc[:, :], lhsT=onesrow[:, :], rhs=krr,
                    start=True, stop=True),
                    reads=["row_k", "onesrow"], writes=["ps_bc"])
                em.op("dve", lambda e: e.tensor_copy(
                    out=kr_b[:, :], in_=ps_bc[:, :]),
                    reads=["ps_bc"], writes=["kr_b"])
                em.op("dve", lambda e, tsl=tsl: e.tensor_mul(
                    out=ck2[:, :], in0=ckt[:, tsl:tsl + TT], in1=kr_b[:, :]),
                    reads=[f"ckt{t % 2}", "kr_b"], writes=["ck2"])
                em.op("dve", lambda e, tsl=tsl: e.tensor_mul(
                    out=sk2[:, :], in0=skt[:, tsl:tsl + TT], in1=kr_b[:, :]),
                    reads=[f"skt{t % 2}", "kr_b"], writes=["sk2"])

            def rope_out(t, src_f, src_key, w_ap, w_key,
                         ct, ct_key, st_, st_key, dst, dst_key, sl):
                em.op("dve", lambda e: e.tensor_scalar_mul(
                    out=ub[:, :], in0=src_f, scalar1=w_ap),
                    reads=[src_key, w_key], writes=["ub"])
                em.op("pe", lambda e: e.matmul(
                    ps_sw[:, :], lhsT=pswapb[:, :], rhs=ub[:, :],
                    start=True, stop=True),
                    reads=["ub", "pswapb"], writes=["ps_sw"])
                em.op("act", lambda e: e.copy(out=swf[:, :], in_=ps_sw[:, :]),
                      reads=["ps_sw"], writes=["swf"])
                em.op("dve", lambda e, ct=ct: e.tensor_mul(
                    out=aa[:, :], in0=ub[:, :], in1=ct),
                    reads=["ub", ct_key], writes=["aa"])
                em.op("dve", lambda e, st_=st_: e.tensor_mul(
                    out=bb2[:, :], in0=swf[:, :], in1=st_),
                    reads=["swf", st_key], writes=["bb2"])
                qk = qkb[:, sl * TT:(sl + 1) * TT]
                em.op("dve", lambda e, qk=qk: e.tensor_add(
                    out=qk, in0=aa[:, :], in1=bb2[:, :]),
                    reads=["aa", "bb2"], writes=[f"qkb{sl}"])
                em.dma("pool", dst, qk, reads=[f"qkb{sl}"], writes=[dst_key])

            def a2_tail_pieces(t):
                c0 = t * TT
                has_q = True
                tsl = (t % 2) * TT
                rope_pieces = []
                if has_q:
                    for h in range(NH):
                        def pq(h=h, c0=c0):
                            rope_out(t, qf[:, h * TT:(h + 1) * TT], f"qf{h}",
                                     qnw_s[:, h:h + 1], "qnw_s",
                                     cq2[:, :], "cq2", sq2[:, :], "sq2",
                                     q_scr[:, h, c0:c0 + TT],
                                     "q_scr", h % 2)
                        rope_pieces.append(pq)
                for h in range(NH):
                    def pk_(h=h, c0=c0):
                        rope_out(t, kf[:, h * TT:(h + 1) * TT], f"kf{h}",
                                 knw_s[:, h:h + 1], "knw_s",
                                 ck2[:, :], "ck2", sk2[:, :], "sk2",
                                 k_scr[h, :, WINDOW + c0:WINDOW + c0 + TT],
                                 "k_scr", h % 2)
                    rope_pieces.append(pk_)
                pieces = []
                nrp = len(rope_pieces)
                for mv in range(NH):
                    def pv_(mv=mv, c0=c0, j0=mv * nrp // NH,
                            j1=(mv + 1) * nrp // NH):
                        ps, psk, sl = qkv_mm(t, 2 * KD + mv)
                        em.op("act", lambda e, ps=ps: e.copy(
                            out=vbs[:, :], in_=ps[:, :]),
                            reads=[psk], writes=["vbs"])
                        vsl = (mv % 2) * 4 * 128
                        for s4 in range(TT // 128):
                            ptr = ps_tr[s4 % 2]
                            ptrk = f"ps_tr{s4 % 2}"
                            vt = vts[:, vsl + s4 * 128:vsl + (s4 + 1) * 128]
                            em.op("pe", lambda e, ptr=ptr, s4=s4: e.matmul(
                                ptr[:, :], lhsT=vbs[:, s4 * 128:(s4 + 1) * 128],
                                rhs=identb[:, :], start=True, stop=True),
                                reads=["vbs", "identb"], writes=[ptrk])
                            em.op("act", lambda e, ptr=ptr, vt=vt: e.copy(
                                out=vt, in_=ptr[:, :]),
                                reads=[ptrk], writes=[f"vts{mv % 2}"])
                        em.dma("sp",
                               v_scr[mv, :, WINDOW // 128 + c0 // 128:
                                     WINDOW // 128 + c0 // 128 + 4, :],
                               vts[:, vsl:vsl + 4 * 128].rearrange(
                                   "p (kb c) -> p kb c", kb=4),
                               reads=[f"vts{mv % 2}"], writes=["v_scr"])
                        for j in range(j0, j1):
                            rope_pieces[j]()
                    pieces.append(pv_)
                return pieces

            prev_tail = None
            for t in range(n_own_tiles):
                a2_loads(t)
                pro = a2_prologue_pieces(t)
                if prev_tail is None:
                    for p in pro:
                        p()
                else:
                    nd = len(prev_tail)
                    npr = len(pro)
                    j = 0
                    for i, pd in enumerate(prev_tail):
                        pd()
                        jn = min(npr, (i + 1) * 2 * npr // nd)
                        while j < jn:
                            pro[j]()
                            j += 1
                qk_section(t)
                prev_tail = a2_tail_pieces(t)
            for p in prev_tail:
                p()

            em.barrier()

        with ExitStack() as ph:
            psb = lambda n, s, d: ph.enter_context(nc.sbuf_tensor("b" + n, s, d))
            KW = WINDOW + TT
            NKC = KW // 128
            qsb = psb("qsb", [128, NH * TT], BF16)
            ksb = psb("ksb", [128, 4 * KW], BF16)
            vsb = psb("vsb", [128, 4 * NKC * 128], BF16)
            pf32 = psb("pf32", [128, 2 * 1152], F32)
            pb = psb("pb", [128, 2 * 1152], BF16)
            ptb = psb("ptb", [128, 2 * 1152], BF16)
            nmax = psb("nmax", [128, 2], F32)
            rows_nm = psb("rows_nm", [128, 6], F32)
            ssum = psb("ssum", [128, 2], F32)
            rsum = psb("rsum", [128, 2], F32)
            attn = psb("attn", [128, NH * TT], BF16)
            attn_tm = psb("attn_tm", [128, NH * 4 * 128], BF16)
            wosb = psb("wosb", [128, KD * KD * 128], BF16)
            x1c = psb("x1c", [128, 4 * TT], F32)
            x2c = psb("x2c", [128, 2 * TT], F32)
            ps_scs = [ph.enter_context(nc.psum_tensor(f"bps_sc{i}", [128, 1152],
                                                       F32)) for i in range(2)]
            ps_pv = ph.enter_context(nc.psum_tensor("bps_pv", [128, TT], F32))
            ps_wo = [ph.enter_context(nc.psum_tensor("bps_wo", [128, TT], F32))]

            idxt = ph.enter_context(nc.sbuf_tensor("bidxt", [1, 1],
                                                   mybir.dt.int32))
            em.dma("sp", ktail[:, :, :],
                   k_scr[:, :, t_all - WINDOW:t_all], writes=["ktail"])
            em.dma("sp", vtail[:, :, :, :],
                   v_scr[:, :, (t_all - WINDOW) // 128:t_all // 128, :],
                   writes=["vtail"])
            if ways > 1:
                rg = [list(range(ways))]
                em.collective(
                    lambda e: e.collective_compute(
                        "AllGather", ALU.bypass, replica_groups=rg,
                        ins=[ktail.rearrange("a b c -> (a b) c").opt()],
                        outs=[kgat.rearrange("r a b c -> (r a b) c").opt()]),
                    reads=["ktail"], writes=["kgat"])
                em.collective(
                    lambda e: e.collective_compute(
                        "AllGather", ALU.bypass, replica_groups=rg,
                        ins=[vtail.rearrange("a b c d -> (a b c) d").opt()],
                        outs=[vgat.rearrange("r a b c d -> (r a b c) d").opt()]),
                    reads=["vtail"], writes=["vgat"])
            else:
                em.dma("sp", kgat[0], ktail[:, :, :],
                       reads=["ktail"], writes=["kgat"])
                em.dma("sp", vgat[0], vtail[:, :, :, :],
                       reads=["vtail"], writes=["vgat"])
            em.dma("sp", idxt[:, :], prevrank[:, :], writes=["idxt"])

            kstride = NH * 128 * WINDOW
            vstride = NH * 128 * (WINDOW // 128) * 128

            def dyn_fetch(e, gat_h, stride, dst_ap, ndims_ap):
                with e.register() as reg:
                    e.reg_load(reg, idxt[:1, :1])
                    with e.register() as ofs:
                        e.reg_mul(ofs, reg, stride)
                        src_dyn = bass.AP(gat_h, ofs, ndims_ap)
                        return e.dma_start(out=dst_ap, in_=src_dyn)

            k_ap = [[128 * WINDOW, NH], [WINDOW, 128], [1, WINDOW]]
            v_ap = [[128 * WINDOW, NH], [WINDOW, 128], [1, WINDOW]]

            def bk(e):
                return dyn_fetch(e, kgat_h, kstride,
                                 k_scr[:, :, 0:WINDOW], k_ap)

            def bv(e):
                return dyn_fetch(e, vgat_h, vstride,
                                 v_scr[:, :, 0:WINDOW // 128, :].rearrange(
                                     "a b c d -> a b (c d)"), v_ap)

            em.dma_custom("pool", bk, reads=["kgat", "idxt"], writes=["k_scr_h"])
            em.dma_custom("pool", bv, reads=["vgat", "idxt"], writes=["v_scr_h"])

            for m in range(KD):
                em.dma("pool", wosb[:, m * KD * 128:(m + 1) * KD * 128],
                       wo_in[m, :, :], writes=[f"wosb{m}"])

            qt_order = [qt for qt in range(n_own_tiles) if qt * TT >= WINDOW]
            qt_order += [qt for qt in range(n_own_tiles) if qt * TT < WINDOW]
            wo_pieces = []
            for qt in qt_order:
                kw0 = qt * TT
                em.dma("sp", qsb.ap().rearrange("p (h c) -> p h c", h=NH),
                       q_scr[:, :, qt * TT:(qt + 1) * TT],
                       reads=[],
                       writes=[f"qsb{h}" for h in range(NH)])
                prev_tail = None
                for h in range(NH):
                    sl = h % 2
                    if h % 2 == 0:
                        psl2 = (h // 2) % 2
                        em.dma("sp",
                               ksb[:, psl2 * 2 * KW:(psl2 * 2 + 2) * KW]
                               .rearrange("p (hh c) -> p hh c", hh=2),
                               k_scr[h:h + 2, :, kw0:kw0 + KW].rearrange(
                                   "hh p c -> p hh c"),
                               reads=(["k_scr_h"] if kw0 < WINDOW else []),
                               writes=[f"ksb{psl2 * 2}", f"ksb{psl2 * 2 + 1}"])
                        em.dma("sp",
                               vsb[:, psl2 * 2 * NKC * 128:
                                   (psl2 * 2 + 2) * NKC * 128].rearrange(
                                   "p (hh kb c) -> p hh kb c", hh=2, kb=NKC),
                               v_scr[h:h + 2, :, kw0 // 128:kw0 // 128 + NKC,
                                     :].rearrange("hh p kb c -> p hh kb c"),
                               reads=(["v_scr_h"] if kw0 < WINDOW else []),
                               writes=[f"vsb{psl2 * 2}", f"vsb{psl2 * 2 + 1}"])
                    slot = (h // 2) % 2 * 2 + h % 2
                    kwb = ksb[:, slot * KW:(slot + 1) * KW]
                    vwb = vsb[:, slot * NKC * 128:(slot + 1) * NKC * 128]
                    kwk, vwk = f"ksb{slot}", f"vsb{slot}"
                    for qs in range(TT // 128):
                        psl = qs % 2
                        ps_sc = ps_scs[psl]
                        sck = f"ps_sc{psl}"
                        pbx = pb[:, psl * 1152:(psl + 1) * 1152]
                        pfx = pf32[:, psl * 1152:(psl + 1) * 1152]
                        nmx = nmax[:, psl:psl + 1]
                        ssx = ssum[:, psl:psl + 1]
                        rsx = rsum[:, psl:psl + 1]
                        qlhs = qsb[:, h * TT + qs * 128: h * TT + qs * 128 + 128]
                        kof = qs * 128
                        for o, n in ((0, 512), (512, 512), (1024, 128)):
                            em.op("pe", lambda e, o=o, n=n, kwb=kwb, qlhs=qlhs,
                                  kof=kof, ps_sc=ps_sc: e.matmul(
                                      ps_sc[:, o:o + n], lhsT=qlhs,
                                      rhs=kwb[:, kof + o:kof + o + n],
                                      start=True, stop=True),
                                  reads=[f"qsb{h}", kwk], writes=[sck])
                        em.op("pe", lambda e, ps_sc=ps_sc: e.matmul(
                            ps_sc[:, 0:128], lhsT=identb[:, :], rhs=maskab[:, :],
                            start=False, stop=True, skip_group_check=True),
                            reads=["maskab", "identb"], writes=[sck])
                        em.op("pe", lambda e, ps_sc=ps_sc: e.matmul(
                            ps_sc[:, 1024:1152], lhsT=identb[:, :], rhs=maskcb[:, :],
                            start=False, stop=True, skip_group_check=True),
                            reads=["maskcb", "identb"], writes=[sck])
                        if kw0 < WINDOW:
                            hb0 = kw0 + kof
                            for o, n in ((0, 512), (512, 512), (1024, 128)):
                                em.op("pe", lambda e, o=o, n=n, hb0=hb0,
                                      ps_sc=ps_sc: e.matmul(
                                          ps_sc[:, o:o + n], lhsT=onesrb[:, :],
                                          rhs=hbiasb[:, hb0 + o:hb0 + o + n],
                                          start=False, stop=True,
                                          skip_group_check=True),
                                      reads=["hbiasb", "onesrb"], writes=[sck])
                        em.op("dve", lambda e, nmx=nmx, ps_sc=ps_sc: e.reduce_max(
                            out=nmx, in_=ps_sc[:, 0:1152], axis=AX.X,
                            negate=True),
                            reads=[sck], writes=[f"nmax{psl}"])
                        em.op("act", lambda e, pbx=pbx, nmx=nmx, ssx=ssx,
                              ps_sc=ps_sc: e.activation(
                                  out=pbx, in_=ps_sc[:, 0:1152], func=AF.Exp,
                                  bias=nmx, accum_out=ssx),
                              reads=[sck, f"nmax{psl}"],
                              writes=[f"pb{psl}", f"ssum{psl}"])
                        em.op("dve", lambda e, rsx=rsx, ssx=ssx: e.reciprocal(
                            out=rsx, in_=ssx),
                            reads=[f"ssum{psl}"], writes=[f"rsum{psl}"])

                        def tail(h=h, qs=qs, psl=psl, pbx=pbx, vwb=vwb, vwk=vwk,
                                 ps_sc=ps_sc, sck=sck, rsx=rsx):
                            ptx = ptb[:, psl * 1152:(psl + 1) * 1152]
                            for c in range(9):
                                em.op("pe", lambda e, c=c, pbx=pbx, ps_sc=ps_sc:
                                      e.matmul(
                                          ps_sc[:, c * 128:(c + 1) * 128],
                                          lhsT=pbx[:, c * 128:(c + 1) * 128],
                                          rhs=identb[:, :], start=True, stop=True),
                                      reads=[f"pb{psl}", "identb"], writes=[sck])
                            em.op("act", lambda e, ptx=ptx, ps_sc=ps_sc: e.copy(
                                out=ptx, in_=ps_sc[:, 0:1152]),
                                reads=[sck], writes=[f"ptb{psl}"])
                            for c in range(9):
                                em.op("pe", lambda e, c=c, vwb=vwb, qs=qs,
                                      ptx=ptx: e.matmul(
                                          ps_pv[:, qs * 128:(qs + 1) * 128],
                                          lhsT=ptx[:, c * 128:(c + 1) * 128],
                                          rhs=vwb[:, (qs + c) * 128:
                                                   (qs + c + 1) * 128],
                                          start=(c == 0), stop=(c == 8),
                                          skip_group_check=True),
                                      reads=[vwk, f"ptb{psl}"], writes=["ps_pv"])
                            em.op("dve", lambda e, h=h, qs=qs, rsx=rsx:
                                  e.tensor_scalar_mul(
                                      out=attn_tm[:, (h * 4 + qs) * 128:
                                                  (h * 4 + qs + 1) * 128],
                                      in0=ps_pv[:, qs * 128:(qs + 1) * 128],
                                      scalar1=rsx),
                                  reads=["ps_pv", f"rsum{psl}"],
                                  writes=[f"attm{h}_{qs}"])

                        if prev_tail is not None:
                            prev_tail()
                        if wo_pieces:
                            wo_pieces.pop(0)()
                        prev_tail = tail
                if prev_tail is not None:
                    prev_tail()
                while wo_pieces:
                    wo_pieces.pop(0)()
                for h in range(NH):
                    reg = ps_scs[h % 2]
                    rk = f"ps_sc{h % 2}"
                    for qs in range(4):
                        em.op("pe", lambda e, h=h, qs=qs, reg=reg: e.matmul(
                            reg[:, qs * 128:(qs + 1) * 128],
                            lhsT=attn_tm[:, (h * 4 + qs) * 128:
                                         (h * 4 + qs + 1) * 128],
                            rhs=identb[:, :], start=True, stop=True),
                            reads=[f"attm{h}_{qs}", "identb"], writes=[rk])
                    em.op("act", lambda e, h=h, reg=reg: e.copy(
                        out=attn[:, h * TT:(h + 1) * TT], in_=reg[:, 0:512]),
                        reads=[rk], writes=[f"attn{h}"])
                def wo_piece(m, qt=qt):
                    sl = m % 2
                    if m % 4 == 0:
                        em.dma("sp",
                               x1c.ap().rearrange("p (blk c) -> p blk c", blk=4),
                               x1_scr[m * 128:(m + 4) * 128,
                                      qt * TT:(qt + 1) * TT]
                               .rearrange("(blk p) c -> p blk c", p=128),
                               reads=["x1_scr"],
                               writes=[f"x1c{j}" for j in range(4)])
                    psw = ps_wo[0]
                    pswk = "ps_wo0"
                    for k in range(KD):
                        em.op("pe", lambda e, psw=psw, m=m, k=k: e.matmul(
                            psw[:, :],
                            lhsT=wosb[:, (m * KD + k) * 128:(m * KD + k + 1) * 128],
                            rhs=attn[:, k * TT:(k + 1) * TT],
                            start=(k == 0), stop=(k == KD - 1)),
                            reads=[f"wosb{m}", f"attn{k}"], writes=[pswk])
                    xc1 = x1c[:, (m % 4) * TT:(m % 4 + 1) * TT]
                    xc2 = x2c[:, sl * TT:(sl + 1) * TT]
                    em.op("dve", lambda e, psw=psw, xc1=xc1, xc2=xc2: e.tensor_add(
                        out=xc2, in0=psw[:, :], in1=xc1),
                        reads=[pswk, f"x1c{m % 4}"], writes=[f"x2c{sl}"])
                    em.dma("pool", x2_scr[m * 128:(m + 1) * 128,
                                          qt * TT:(qt + 1) * TT], xc2,
                           reads=[f"x2c{sl}"], writes=["x2_scr"])

                wo_pieces = [lambda m=m: wo_piece(m) for m in range(KD)]
            while wo_pieces:
                wo_pieces.pop(0)()
            em.barrier()

        ffn_pass("c", x2_scr, "x2_scr", o_out, "o_out", n_own_tiles)

        em.barrier()
        with nc.Block() as block:
            em.finalize(block)

    return nc


def _sylvester(n):
    h = np.array([[1.0]], dtype=np.float32)
    while h.shape[0] < n:
        h = np.block([[h, h], [h, -h]])
    return h.astype(np.float32)


def _rope_tables(pos, scale):
    inv = 1.0 / (ROPE_THETA ** (np.arange(0, HD, 2, dtype=np.float32) / HD))
    f = pos[None, :].astype(np.float32) * inv[:, None]
    c = np.cos(f) * scale
    s = np.sin(f) * scale
    return (np.concatenate([c, c], axis=0).astype(np.float32),
            np.concatenate([s, s], axis=0).astype(np.float32))


def make_core_inputs(x_full, weights, t_own=2048, cores=None, ways=None):
    B, S, _ = x_full.shape
    n_chunks = S // t_own
    t_all = t_own + WINDOW

    w_qkv = np.asarray(weights["w_qkv"], np.float32)
    q_norm_w = np.asarray(weights["q_norm_w"], np.float32)
    k_norm_w = np.asarray(weights["k_norm_w"], np.float32)
    w_o = np.asarray(weights["w_o"], np.float32)
    SU_ug = np.asarray(weights["SU_ug"], np.float32)
    SU_dp = np.asarray(weights["SU_dp"], np.float32)
    Wsc_ug = np.asarray(weights["Wscale_ug"], np.float32)
    Wsc_dp = np.asarray(weights["Wscale_dp"], np.float32)
    W_up = np.asarray(weights["W_up"], np.float32)
    W_gate = np.asarray(weights["W_gate"], np.float32)
    W_down = np.asarray(weights["W_down"], np.float32)

    com = {}
    wup_full = W_up.T.reshape(KD, 128, INTER)
    wgt_full = W_gate.T.reshape(KD, 128, INTER)
    com["wqkv_in"] = np.ascontiguousarray(
        w_qkv.reshape(3 * KD, 128, KD, 128).transpose(0, 3, 2, 1)
        .reshape(3 * KD, 128, KD * 128))
    com["wo_in"] = np.ascontiguousarray(
        w_o.reshape(KD, 128, KD, 128).transpose(0, 3, 2, 1).reshape(KD, 128, KD * 128))
    wdn_full = W_down.T.reshape(KI, 128, D)
    com["su_ug"] = np.ascontiguousarray(SU_ug.reshape(KD, 128).T)
    com["su_dp"] = np.ascontiguousarray(SU_dp.reshape(KI, 128).T)
    com["wsc_u"] = np.ascontiguousarray(Wsc_ug[:INTER].reshape(KI, 128).T)
    com["wsc_g"] = np.ascontiguousarray(Wsc_ug[INTER:].reshape(KI, 128).T)
    com["wsc_dp"] = np.ascontiguousarray(Wsc_dp.reshape(KD, 128).T)
    com["qnw"] = np.ascontiguousarray(q_norm_w.reshape(KD, 128).T)
    com["knw"] = np.ascontiguousarray(k_norm_w.reshape(KD, 128).T)
    com["h128"] = _sylvester(128)
    com["ident"] = np.eye(128, dtype=np.float32)
    psw = np.zeros((128, 128), np.float32)
    for mcol in range(64):
        psw[mcol + 64, mcol] = -1.0
        psw[mcol, mcol + 64] = 1.0
    com["pswap"] = psw
    p_i = np.arange(128)[:, None]
    k_i = np.arange(128)[None, :]
    com["mask_c"] = np.where(p_i >= k_i, 0.0, NEG).astype(np.float32)
    com["mask_a"] = np.where(p_i < k_i, 0.0, NEG).astype(np.float32)

    if cores is None:
        cores = [(b, ch) for b in range(B) for ch in range(n_chunks)]
    if ways is None:
        ways = len(cores)
    in_maps = []
    for ci, (b, ch) in enumerate(cores):
        d = dict(com)
        mu = INTER // ways
        md = D // ways
        d["wup_in"] = np.ascontiguousarray(wup_full[:, :, ci * mu:(ci + 1) * mu])
        d["wgt_in"] = np.ascontiguousarray(wgt_full[:, :, ci * mu:(ci + 1) * mu])
        d["wdn_in"] = np.ascontiguousarray(wdn_full[:, :, ci * md:(ci + 1) * md])
        own = x_full[b, ch * t_own:(ch + 1) * t_own, :]
        d["x_in"] = np.ascontiguousarray(own.T)
        g0 = ch * t_own
        own_pos = np.arange(g0, g0 + t_own)
        cq, sq = _rope_tables(own_pos, 1.0 / np.sqrt(HD))
        ck, sk = _rope_tables(own_pos, 1.0)
        d["cosq"], d["sinq"] = cq, sq
        d["cosk"], d["sink"] = ck, sk
        hb = np.zeros((1, t_all), np.float32)
        n_inv = max(0, min(WINDOW, WINDOW - g0))
        if n_inv > 0:
            hb[0, :n_inv] = NEG
        d["hbias"] = hb
        d["prevrank"] = np.array([[(ci - 1) % max(len(cores), 1)]], np.int32)
        in_maps.append(d)
    return in_maps


def kernel(L0_attn_hidden_states, L0_attn_w_qkv, L0_attn_q_norm_w,
           L0_attn_k_norm_w, L0_attn_w_o, L0_ffn_SU_ug, L0_ffn_SU_dp,
           L0_ffn_Wscale_ug, L0_ffn_Wscale_dp, L0_ffn_W_up, L0_ffn_W_gate,
           L0_ffn_W_down):
    x = np.asarray(L0_attn_hidden_states, np.float32)
    B, S, _ = x.shape
    t_own = S // 4
    weights = {
        "w_qkv": L0_attn_w_qkv, "q_norm_w": L0_attn_q_norm_w,
        "k_norm_w": L0_attn_k_norm_w, "w_o": L0_attn_w_o,
        "SU_ug": L0_ffn_SU_ug, "SU_dp": L0_ffn_SU_dp,
        "Wscale_ug": L0_ffn_Wscale_ug, "Wscale_dp": L0_ffn_Wscale_dp,
        "W_up": L0_ffn_W_up, "W_gate": L0_ffn_W_gate, "W_down": L0_ffn_W_down,
    }
    in_maps = make_core_inputs(x, weights, t_own=t_own, ways=N_CORES)
    nc = build_program(t_own, ways=N_CORES)
    res = run_bass_kernel_spmd(nc, in_maps, core_ids=list(range(N_CORES)))
    out = np.empty((B, S, D), np.float32)
    ci = 0
    for b in range(B):
        for ch in range(4):
            out[b, ch * t_own:(ch + 1) * t_own, :] = \
                np.asarray(res.results[ci]["o_out"]).T
            ci += 1
    return out
